# version 05
# speedup vs baseline: 1.1624x; 1.0003x over previous
import math
import numpy as np
import concourse.bass as bass
import concourse.mybir as mybir
from concourse.bass_utils import run_bass_kernel_spmd

F32 = mybir.dt.float32
BF16 = mybir.dt.bfloat16
I32 = mybir.dt.int32
AF = mybir.ActivationFunctionType
ALU = mybir.AluOpType
AX = mybir.AxisListType

N_CORES = 8
B, S, D = 16, 2048, 1024
B_LOC = B // N_CORES
T_LOC = B_LOC * S
LN_EPS = 1e-5


ENGINES = ("tensor", "vector", "scalar", "gpsimd", "sync")
DMA_QUEUES = ("sync", "gpsimd", "scalar")


class Reg:
    __slots__ = ("space", "name", "p0", "p1", "b0", "b1")

    def __init__(self, space, name, p0, p1, b0, b1):
        self.space, self.name = space, name
        self.p0, self.p1, self.b0, self.b1 = p0, p1, b0, b1

    def overlaps(self, o):
        return (self.name == o.name and self.p0 < o.p1 and o.p0 < self.p1
                and self.b0 < o.b1 and o.b0 < self.b1)

    def covers(self, o):
        return (self.name == o.name and self.p0 <= o.p0 and self.p1 >= o.p1
                and self.b0 <= o.b0 and self.b1 >= o.b1)

    def __repr__(self):
        return f"R({self.name},p{self.p0}:{self.p1},b{self.b0}:{self.b1})"


class TV:
    __slots__ = ("h", "psl", "csl", "dt_size", "space", "vdt", "base_el")

    def __init__(self, h, psl=None, csl=None, space="SB", vdt=None, base_el=0):
        self.h = h
        self.psl = psl if psl is not None else (0, h.shape[0])
        self.csl = csl if csl is not None else (0, int(np.prod(h.shape[1:])))
        self.vdt = vdt
        self.dt_size = mybir.dt.size(vdt if vdt is not None else h.dtype)
        self.space = space
        self.base_el = base_el

    def __getitem__(self, key):
        ps, cs = key
        assert isinstance(ps, slice) and isinstance(cs, slice)
        p0 = self.psl[0] + (ps.start or 0)
        p1 = self.psl[0] + (ps.stop if ps.stop is not None else (self.psl[1] - self.psl[0]))
        c0 = self.csl[0] + (cs.start or 0)
        c1 = self.csl[0] + (cs.stop if cs.stop is not None else (self.csl[1] - self.csl[0]))
        return TV(self.h, (p0, p1), (c0, c1), self.space, self.vdt, self.base_el)

    def ap(self):
        a = self.h.ap()
        if len(self.h.shape) > 2:
            a = a.rearrange(
                "p " + " ".join(f"d{i}" for i in range(len(self.h.shape) - 1)) +
                " -> p (" + " ".join(f"d{i}" for i in range(len(self.h.shape) - 1)) + ")")
        if self.vdt is not None:
            a = a.bitcast(self.vdt)
        b = self.base_el
        return a[self.psl[0]:self.psl[1], b + self.csl[0]:b + self.csl[1]]

    @property
    def reg(self):
        return Reg(self.space, self.h.name, self.psl[0], self.psl[1],
                   (self.base_el + self.csl[0]) * self.dt_size,
                   (self.base_el + self.csl[1]) * self.dt_size)

    @property
    def shape(self):
        return (self.psl[1] - self.psl[0], self.csl[1] - self.csl[0])


def arena_tv(arena_h, off_bytes, shape, dtype):
    esz = mybir.dt.size(dtype)
    assert off_bytes % esz == 0
    p, cols = shape
    return TV(arena_h, (0, p), (0, cols), "SB", vdt=dtype, base_el=off_bytes // esz)


def dram_reg(h, row0=None, row1=None):
    shape = h.shape
    rowbytes = int(np.prod(shape[1:])) * mybir.dt.size(h.dtype) if len(shape) > 1 else mybir.dt.size(h.dtype)
    r0 = 0 if row0 is None else row0
    r1 = shape[0] if row1 is None else row1
    return Reg("DRAM", h.name, 0, 1, r0 * rowbytes, r1 * rowbytes)


class Op:
    __slots__ = ("idx", "engine", "emit", "reads", "writes", "is_dma", "slot",
                 "use", "inc", "inc_no", "waits", "tag", "deps")

    def __init__(self, idx, engine, emit, reads, writes, is_dma, tag):
        self.idx, self.engine, self.emit = idx, engine, emit
        self.reads, self.writes = reads, writes
        self.is_dma = is_dma
        self.slot = None
        self.use = 0
        self.inc = False
        self.waits = []
        self.tag = tag


class Prog:
    def __init__(self, nc, dma_pool=12):
        self.nc = nc
        self.ops = []
        self.per_engine = {e: [] for e in ENGINES}
        self.records = {}
        self.dma_pool = dma_pool
        self.dma_uses = {q: 0 for q in DMA_QUEUES}
        self._finalized = False

    def _dep_events(self, op):
        deps = set()
        for r in op.reads:
            for (reg, oi, is_w) in self.records.get(r.name, ()):
                if is_w and reg.overlaps(r):
                    deps.add(oi)
        for w in op.writes:
            for (reg, oi, is_w) in self.records.get(w.name, ()):
                if reg.overlaps(w):
                    deps.add(oi)
        return deps

    def _record_access(self, op):
        for w in op.writes:
            lst = self.records.setdefault(w.name, [])
            lst[:] = [rec for rec in lst if not w.covers(rec[0])]
            lst.append((w, op.idx, True))
        for r in op.reads:
            lst = self.records.setdefault(r.name, [])
            lst.append((r, op.idx, False))
        for rs in (op.writes, op.reads):
            for x in rs:
                lst = self.records.get(x.name)
                if lst is not None and len(lst) > 512:
                    k = len(lst) - 384
                    old = lst[:k]
                    u = Reg(old[0][0].space, x.name,
                            min(r.p0 for (r, _, _) in old),
                            max(r.p1 for (r, _, _) in old),
                            min(r.b0 for (r, _, _) in old),
                            max(r.b1 for (r, _, _) in old))
                    mi = max(oi for (_, oi, _) in old)
                    lst[:k] = [(u, mi, True)]

    @staticmethod
    def _to_reg(x):
        r = x.reg if isinstance(x, TV) else x
        if r.space == "PS":
            return Reg("PS", r.name, 0, 128, 0, 1 << 30)
        return r

    def add(self, engine, emit, reads=(), writes=(), is_dma=False, tag=""):
        assert not self._finalized
        regs_r = [self._to_reg(x) for x in reads]
        regs_w = [self._to_reg(x) for x in writes]
        regs_w = regs_w + [r for r in regs_r if r.space == "PS"]
        op = Op(len(self.ops), engine, emit, regs_r, regs_w, is_dma, tag)
        op.deps = self._dep_events(op)
        self.ops.append(op)
        self.per_engine[engine].append(op)
        self._record_access(op)
        if is_dma:
            q = engine
            op.use = self.dma_uses[q]
            self.dma_uses[q] += 1
        return op

    def dma(self, queue, out, in_, out_regs=None, in_regs=None, tag="dma", **kw):
        out_ap = out.ap() if isinstance(out, TV) else out
        in_ap = in_.ap() if isinstance(in_, TV) else in_
        reads = in_regs if in_regs is not None else ([in_] if isinstance(in_, TV) else [])
        writes = out_regs if out_regs is not None else ([out] if isinstance(out, TV) else [])
        return self.add(queue, lambda e: e.dma_start(out=out_ap, in_=in_ap, **kw),
                        reads=reads, writes=writes, is_dma=True, tag=tag)

    def mm(self, out, lhsT, rhs, start, stop, tag="mm", **kw):
        o, l, r = out.ap(), lhsT.ap(), rhs.ap()
        return self.add("tensor", lambda e: e.matmul(o, l, r, start=start, stop=stop, **kw),
                        reads=[lhsT, rhs], writes=[out], tag=tag)

    def transp(self, out, in_, ident, tag="tr"):
        o, i, idn = out.ap(), in_.ap(), ident.ap()
        return self.add("tensor", lambda e: e.transpose(o, i, idn),
                        reads=[in_, ident], writes=[out], tag=tag)

    def act(self, out, in_, func, bias=0.0, scale=1.0, accum_out=None, tag="act"):
        o, i = out.ap(), in_.ap()
        b = bias.ap() if isinstance(bias, TV) else np.float32(bias)
        s = scale.ap() if isinstance(scale, TV) else np.float32(scale)
        ao = accum_out.ap() if isinstance(accum_out, TV) else None
        reads = [in_] + [x for x in (bias, scale) if isinstance(x, TV)]
        writes = [out] + ([accum_out] if accum_out is not None else [])
        return self.add("scalar",
                        lambda e: e.activation(o, i, func, bias=b, scale=s, accum_out=ao),
                        reads=reads, writes=writes, tag=tag)

    def v(self, meth, out, ins, tag="v", engine="vector", **kw):
        o = out.ap() if isinstance(out, TV) else out
        aps = [x.ap() if isinstance(x, TV) else x for x in ins]
        reads = [x for x in ins if isinstance(x, TV)]
        kw2 = {k: (v.ap() if isinstance(v, TV) else v) for k, v in kw.items()}
        reads += [v for v in kw.values() if isinstance(v, TV)]
        return self.add(engine, lambda e: getattr(e, meth)(o, *aps, **kw2),
                        reads=reads, writes=[out] if isinstance(out, TV) else [], tag=tag)

    def raw(self, engine, emit, reads=(), writes=(), tag="raw"):
        return self.add(engine, emit, reads=reads, writes=writes, tag=tag)

    def barrier(self, tag="bar"):
        last = {}
        for e in ENGINES:
            if self.per_engine[e]:
                last[e] = self.per_engine[e][-1].idx
        dma_last = {}
        for op in self.ops:
            if op.is_dma:
                dma_last[(op.engine, op.use % self.dma_pool)] = op.idx
        dep_idxs = set(last.values()) | set(dma_last.values())
        for e in ENGINES:
            op = self.add(e, lambda eng: eng.nop(), reads=(), writes=(),
                          tag=tag + "_" + e)
            op.deps |= {d for d in dep_idxs if d != op.idx}

    def finalize(self, sems):
        assert not self._finalized
        for op in self.ops:
            for d in op.deps:
                p = self.ops[d]
                if p.is_dma:
                    continue
                if p.engine == "tensor" and op.engine == "tensor":
                    continue
                p.inc = True
        inc_no = {e: 0 for e in ENGINES}
        for op in self.ops:
            if op.is_dma:
                q = op.engine
                op.slot = (q, op.use % self.dma_pool, op.use // self.dma_pool)
            if op.inc and not op.is_dma:
                inc_no[op.engine] += 1
                op.inc_no = inc_no[op.engine]
        clocks = {e: {} for e in ENGINES}
        snap = {}

        def key_of(p):
            if p.is_dma:
                return ("dma", p.slot[0], p.slot[1])
            return ("eng", p.engine)

        def val_of(p):
            if p.is_dma:
                return 16 * (p.slot[2] + 1)
            return getattr(p, "inc_no", None)

        for op in self.ops:
            clk = clocks[op.engine]
            if op.is_dma:
                q, s, u = op.slot
                if u > 0:
                    k = ("dma", q, s)
                    if clk.get(k, 0) < 16 * u:
                        op.waits.append((k, 16 * u))
                        clk[k] = 16 * u
            for d in sorted(op.deps):
                p = self.ops[d]
                if p.engine == op.engine and not p.is_dma:
                    if p.engine == "tensor":
                        continue
                k = key_of(p)
                v = val_of(p)
                if v is None:
                    continue
                if clk.get(k, 0) < v:
                    op.waits.append((k, v))
                    clk[k] = v
                    ps = snap.get(d)
                    if ps:
                        for kk, vv in ps.items():
                            if clk.get(kk, 0) < vv:
                                clk[kk] = vv
            if op.waits or op.is_dma or True:
                snap[op.idx] = dict(clk)
                if op.is_dma:
                    snap[op.idx][("dma", op.slot[0], op.slot[1])] = 16 * (op.slot[2] + 1)
                elif getattr(op, "inc_no", None):
                    snap[op.idx][("eng", op.engine)] = op.inc_no
        self._finalized = True
        self.sems = sems

    def emit_all(self, block):
        sems = self.sems

        def emit_engine(engname):
            def body(eng):
                for op in self.per_engine[engname]:
                    for (k, v) in op.waits:
                        eng.wait_ge(sems[k], v)
                    inst = op.emit(eng)
                    if op.is_dma:
                        q, s, u = op.slot
                        inst.then_inc(sems[("dma", q, s)], 16)
                    elif op.inc:
                        inst.then_inc(sems[("eng", op.engine)], 1)
            return body

        for e in ENGINES:
            getattr(block, e)(emit_engine(e))

    def final_waits(self):
        out = []
        for q in DMA_QUEUES:
            n = self.dma_uses[q]
            for s in range(self.dma_pool):
                uses = (n - s + self.dma_pool - 1) // self.dma_pool
                if uses > 0:
                    out.append((("dma", q, s), 16 * uses))
        return out


def make_sems(nc, ctx, dma_pool=12):
    sems = {}
    for e in ENGINES:
        sems[("eng", e)] = ctx.enter_context(nc.semaphore(f"se_{e}"))
    for q in DMA_QUEUES:
        for i in range(dma_pool):
            sems[("dma", q, i)] = ctx.enter_context(nc.semaphore(f"sd_{q}{i}"))
    return sems



def sb(nc, name, p, cols, dtype):
    return TV(nc.alloc_sbuf_tensor(name, [p, cols], dtype), space="SB")


def ps(nc, name, p, cols, dtype=F32):
    return TV(nc.alloc_psum_tensor(name, [p, cols], dtype), space="PS")


def rsqrt_eps(P, out, var, eps=LN_EPS, tag="rsqrt"):
    P.act(out, var, AF.Ln, bias=eps, tag=tag + "_ln")
    P.act(out, out, AF.Exp, scale=-0.5, tag=tag + "_exp")


def ln_stats(P, x, mu, rstd, scr, n_inv, eps=LN_EPS, tag="ln"):
    P.v("reduce_sum", mu, [x], axis=AX.X, tag=tag + "_sum")
    P.v("tensor_scalar", mu, [mu], scalar1=n_inv, scalar2=None,
        op0=ALU.mult, tag=tag + "_mu")
    P.v("tensor_scalar", scr, [x], scalar1=mu, scalar2=None,
        op0=ALU.subtract, tag=tag + "_ctr")
    P.v("tensor_tensor", scr, [scr, scr], op=ALU.mult, tag=tag + "_sq")
    P.v("reduce_sum", rstd, [scr], axis=AX.X, tag=tag + "_vsum")
    P.v("tensor_scalar", rstd, [rstd], scalar1=n_inv, scalar2=None,
        op0=ALU.mult, tag=tag + "_var")
    rsqrt_eps(P, rstd, rstd, eps=eps, tag=tag + "_rs")


class Common:

    def __init__(self, nc, P):
        self.ones = sb(nc, "c_ones", 128, 128, F32)
        self.identb = sb(nc, "c_identb", 128, 128, BF16)
        P.v("memset", self.ones, [1.0], tag="ones", engine="gpsimd")
        P.v("memset", self.identb, [1.0], tag="identb", engine="gpsimd")
        idap = self.identb.ap()
        P.add("gpsimd", lambda e: e.affine_select(
            out=idap, in_=idap, pattern=[[-1, 128]], compare_op=ALU.is_equal,
            fill=0.0, base=0, channel_multiplier=1),
            reads=[self.identb], writes=[self.identb], tag="ident_sel")


def transpose_128(P, com, out_ps, xin, tag="tr"):
    P.transp(out_ps, xin, com.identb, tag=tag)




def emit_l1ffn(P, nc, com, resid_in, out_hbm, w1t_h, b1_h, w2t_h, b2_h, pfx="f"):
    EPSQ = 1e-5
    CH = 256
    NCH = T_LOC // CH
    NSUB = CH // 128
    wq1 = TV(nc.alloc_sbuf_tensor(pfx + "wq1", [128, 8, 4096], BF16))
    wq2 = TV(nc.alloc_sbuf_tensor(pfx + "wq2", [128, 32, 1024], BF16))
    arena = nc.alloc_sbuf_tensor(pfx + "arena", [128, 16384], mybir.dt.uint8)
    wld = arena_tv(arena, 0, [128, 8 * 512], F32)
    wld2 = arena_tv(arena, 0, [128, 32 * 128], F32)
    xch = arena_tv(arena, 0, [128, 2 * 1024], F32)
    ych = arena_tv(arena, 8192, [128, 2 * 1024], F32)
    wldh = arena_tv(arena, 0, [128, 8 * 256], F32)
    wld2h = arena_tv(arena, 0, [128, 32 * 64], F32)
    qm2 = arena_tv(arena, 8192, [128, 8 * 256], F32)
    qm2b = arena_tv(arena, 8192, [128, 32 * 64], F32)
    b1t = TV(nc.alloc_sbuf_tensor(pfx + "b1t", [128, 32], F32))
    b2bc = TV(nc.alloc_sbuf_tensor(pfx + "b2bc", [128, 1024], F32))
    stats = TV(nc.alloc_sbuf_tensor(pfx + "stats", [128, 64], F32))
    gsc = TV(nc.alloc_sbuf_tensor(pfx + "gsc", [128, 32], F32))
    xln = TV(nc.alloc_sbuf_tensor(pfx + "xln", [128, 2, 1024], BF16))
    xlnT = TV(nc.alloc_sbuf_tensor(pfx + "xlnT", [128, 8, 256], BF16))
    hT = TV(nc.alloc_sbuf_tensor(pfx + "hT", [128, 32, 256], BF16))
    scr = TV(nc.alloc_sbuf_tensor(pfx + "scr", [128, 1024], F32))

    ps_tr = ps(nc, pfx + "ps_tr", 128, 512, BF16)
    ps_h = [ps(nc, pfx + f"ps_h{i}", 128, 512, F32) for i in range(2)]
    ps_y = [ps(nc, pfx + f"ps_y{i}", 128, 512, F32) for i in range(2)]
    ps_g = ps(nc, pfx + "ps_g", 128, 128, F32)

    w1t_ap = w1t_h.ap().rearrange("(a p) f -> p a f", p=128)
    w2t_ap = w2t_h.ap().rearrange("(a p) f -> p a f", p=128)

    for ci in range(8):
        P.dma("sync", wld, w1t_ap[:, :, ci * 512:(ci + 1) * 512],
              in_regs=[dram_reg(w1t_h)], tag=f"w1g{ci}")
        P.v("reduce_sum", gsc[:, ci:ci + 1], [wld], axis=AX.X,
            apply_absolute_value=True, tag=f"w1abs{ci}")
    for ci in range(8):
        P.dma("sync", wld2, w2t_ap[:, :, ci * 128:(ci + 1) * 128],
              in_regs=[dram_reg(w2t_h)], tag=f"w2g{ci}")
        P.v("reduce_sum", gsc[:, 8 + ci:9 + ci], [wld2], axis=AX.X,
            apply_absolute_value=True, tag=f"w2abs{ci}")
    P.mm(TV(ps_g.h, (0, 1), (0, 16), "PS"), com.ones[:, 0:1], gsc[:, 0:16],
         start=True, stop=True, tag="gsum")
    P.v("tensor_copy", stats[0:1, 0:16], [TV(ps_g.h, (0, 1), (0, 16), "PS")], tag="gevac")
    P.v("reduce_sum", stats[0:1, 16:17], [stats[0:1, 0:8]], axis=AX.X, tag="g1")
    P.v("reduce_sum", stats[0:1, 17:18], [stats[0:1, 8:16]], axis=AX.X, tag="g2")
    P.v("tensor_scalar", stats[0:1, 16:17], [stats[0:1, 16:17]],
        scalar1=1.0 / (4096 * 1024), scalar2=EPSQ, op0=ALU.mult, op1=ALU.add, tag="g1m")
    P.v("tensor_scalar", stats[0:1, 17:18], [stats[0:1, 17:18]],
        scalar1=1.0 / (4096 * 1024), scalar2=EPSQ, op0=ALU.mult, op1=ALU.add, tag="g2m")
    P.v("tensor_scalar", stats[0:1, 18:19], [stats[0:1, 16:17]],
        scalar1=0.5, scalar2=None, op0=ALU.mult, tag="g1t")
    P.v("tensor_scalar", stats[0:1, 19:20], [stats[0:1, 17:18]],
        scalar1=0.5, scalar2=None, op0=ALU.mult, tag="g2t")
    P.v("tensor_scalar", stats[0:1, 20:22], [stats[0:1, 18:20]],
        scalar1=-1.0, scalar2=None, op0=ALU.mult, tag="gneg")
    P.mm(TV(ps_g.h, (0, 128), (0, 4), "PS"), com.ones[0:1, 0:128], stats[0:1, 18:22],
         start=True, stop=True, tag="gbc")
    sbc = stats[:, 24:28]
    P.v("tensor_copy", sbc, [TV(ps_g.h, (0, 128), (0, 4), "PS")], tag="gbce")

    for ci in range(16):
        P.dma("sync", wldh, w1t_ap[:, :, ci * 256:(ci + 1) * 256],
              in_regs=[dram_reg(w1t_h)], tag=f"w1q{ci}")
        P.v("tensor_scalar", qm2, [wldh], scalar1=sbc[:, 2:3], scalar2=None,
            op0=ALU.is_le, tag=f"q1a{ci}")
        dap = wq1.h.ap()[:, :, ci * 256:(ci + 1) * 256]
        sap = wldh.ap()
        m2ap = qm2.ap()
        th = sbc[:, 0:1].ap()
        P.add("vector", lambda e, dap=dap, sap=sap, m2ap=m2ap, th=th: e.scalar_tensor_tensor(
            out=dap, in0=sap, scalar=th, in1=m2ap, op0=ALU.is_ge, op1=ALU.subtract),
            reads=[wldh, qm2, sbc], writes=[wq1], tag=f"q1c{ci}")
    for ci in range(16):
        P.dma("sync", wld2h, w2t_ap[:, :, ci * 64:(ci + 1) * 64],
              in_regs=[dram_reg(w2t_h)], tag=f"w2q{ci}")
        P.v("tensor_scalar", qm2b, [wld2h], scalar1=sbc[:, 3:4], scalar2=None,
            op0=ALU.is_le, tag=f"q2a{ci}")
        dap = wq2.h.ap()[:, :, ci * 64:(ci + 1) * 64]
        sap = wld2h.ap()
        m2ap = qm2b.ap()
        th = sbc[:, 1:2].ap()
        P.add("vector", lambda e, dap=dap, sap=sap, m2ap=m2ap, th=th: e.scalar_tensor_tensor(
            out=dap, in0=sap, scalar=th, in1=m2ap, op0=ALU.is_ge, op1=ALU.subtract),
            reads=[wld2h, qm2b, sbc], writes=[wq2], tag=f"q2c{ci}")

    P.dma("sync", b1t, b1_h.ap(), in_regs=[dram_reg(b1_h)], tag="b1")
    P.dma("sync", b2bc, b2_h.ap().unsqueeze(0).partition_broadcast(128),
          in_regs=[dram_reg(b2_h)], tag="b2")

    rin_ap = resid_in.ap().rearrange("(c p) d -> p c d", p=128)
    rout_ap = out_hbm.ap().rearrange("(c p) d -> p c d", p=128)

    for ch in range(NCH):
        P.dma("sync", xch, rin_ap[:, ch * NSUB:(ch + 1) * NSUB, :],
              in_regs=[dram_reg(resid_in, ch * CH, (ch + 1) * CH)], tag=f"x{ch}")
        for c in range(NSUB):
            x_t = xch[:, c * 1024:(c + 1) * 1024]
            mu = stats[:, 32 + c * 2:33 + c * 2]
            rs_ = stats[:, 33 + c * 2:34 + c * 2]
            ln_stats(P, x_t, mu, rs_, scr, 1.0 / 1024, tag=f"ln{ch}_{c}")
            P.v("tensor_scalar", xln[:, c * 1024:(c + 1) * 1024], [x_t],
                scalar1=mu, scalar2=rs_, op0=ALU.subtract, op1=ALU.mult,
                tag=f"lnn{ch}_{c}")
        for c in range(NSUB):
            for dt_ in range(8):
                transpose_128(P, com,
                              TV(ps_tr.h, (0, 128), ((dt_ % 4) * 128, (dt_ % 4 + 1) * 128), "PS"),
                              xln[:, c * 1024 + dt_ * 128: c * 1024 + (dt_ + 1) * 128],
                              tag=f"tr{ch}_{c}_{dt_}")
                if dt_ % 4 == 3:
                    grp = dt_ // 4
                    dap = xlnT.h.ap()[:, grp * 4:(grp + 1) * 4, c * 128:(c + 1) * 128]
                    pap = ps_tr.ap()
                    P.add("vector", lambda e, dap=dap, pap=pap: e.tensor_copy(out=dap, in_=pap),
                          reads=[ps_tr],
                          writes=[TV(xlnT.h, (0, 128), (grp * 4 * 256, (grp * 4 + 4) * 256))],
                          tag=f"tre{ch}_{c}_{grp}")
        for ft in range(32):
            bank = TV(ps_h[ft % 2].h, (0, 128), (0, CH), "PS")
            for dt_ in range(8):
                P.mm(bank, TV(wq1.h, (0, 128), (dt_ * 4096 + ft * 128, dt_ * 4096 + (ft + 1) * 128)),
                     TV(xlnT.h, (0, 128), (dt_ * 256, (dt_ + 1) * 256)),
                     start=(dt_ == 0), stop=(dt_ == 7), tag=f"h{ch}_{ft}")
            P.act(hT[:, ft * 256:(ft + 1) * 256], bank, AF.Relu,
                  bias=b1t[:, ft:ft + 1], tag=f"he{ch}_{ft}")
        for c in range(NSUB):
            for half in range(2):
                bank = ps_y[half]
                for ft in range(32):
                    P.mm(bank, TV(hT.h, (0, 128), (ft * 256 + c * 128, ft * 256 + (c + 1) * 128)),
                         TV(wq2.h, (0, 128), (ft * 1024 + half * 512, ft * 1024 + (half + 1) * 512)),
                         start=(ft == 0), stop=(ft == 31), tag=f"y{ch}_{c}_{half}")
                P.v("scalar_tensor_tensor",
                    ych[:, c * 1024 + half * 512:c * 1024 + (half + 1) * 512],
                    [bank], scalar=0.0, in1=b2bc[:, half * 512:(half + 1) * 512],
                    op0=ALU.add, op1=ALU.add, tag=f"ye{ch}_{c}_{half}")
        for c in range(NSUB):
            y_t = ych[:, c * 1024:(c + 1) * 1024]
            mu = stats[:, 40 + c * 2:41 + c * 2]
            rs_ = stats[:, 41 + c * 2:42 + c * 2]
            ln_stats(P, y_t, mu, rs_, scr, 1.0 / 1024, tag=f"lo{ch}_{c}")
            P.v("tensor_scalar", y_t, [y_t], scalar1=mu, scalar2=rs_,
                op0=ALU.subtract, op1=ALU.mult, tag=f"lon{ch}_{c}")
            P.v("tensor_tensor", xch[:, c * 1024:(c + 1) * 1024],
                [y_t, xch[:, c * 1024:(c + 1) * 1024]], op=ALU.add, tag=f"oadd{ch}_{c}")
        P.dma("gpsimd", rout_ap[:, ch * NSUB:(ch + 1) * NSUB, :], xch,
              out_regs=[dram_reg(out_hbm, ch * CH, (ch + 1) * CH)],
              in_regs=[xch], tag=f"o{ch}")



def emit_moe(P, nc, com, resid_in, out_hbm, gate_h, w1_h, w3_h, w2_h, pfx="m"):
    TH = 2048
    NCH = TH // 512
    xmT = TV(nc.alloc_sbuf_tensor(pfx + "xmT", [128, 8, TH], BF16))
    arena = nc.alloc_sbuf_tensor(pfx + "arena", [128, 16384], mybir.dt.uint8)
    xch = arena_tv(arena, 0, [128, 4 * 1024], F32)
    xlnt = TV(nc.alloc_sbuf_tensor(pfx + "xlnt", [128, 4, 1024], BF16))
    wbuf = nc.alloc_sbuf_tensor(pfx + "wbuf", [128, 49152], mybir.dt.uint8)
    def wtv(par, which):
        off = par * 24576 + which * 8192
        return arena_tv(wbuf, off, [128, 4096], BF16)
    s1 = TV(nc.alloc_sbuf_tensor(pfx + "s1", [128, 2, 4, 512], BF16))
    s3 = TV(nc.alloc_sbuf_tensor(pfx + "s3", [128, 4, 512], BF16))
    acc = TV(nc.alloc_sbuf_tensor(pfx + "acc", [128, 16, 1024], F32))
    comb = TV(nc.alloc_sbuf_tensor(pfx + "comb", [128, 16, 16], F32))
    scs = TV(nc.alloc_sbuf_tensor(pfx + "scs", [128, 16, 16], F32))
    gsc = TV(nc.alloc_sbuf_tensor(pfx + "gscr", [128, 32], F32))
    gatesb = TV(nc.alloc_sbuf_tensor(pfx + "gate", [128, 8, 16], BF16))
    stats = TV(nc.alloc_sbuf_tensor(pfx + "stats", [128, 32], F32))
    scr = TV(nc.alloc_sbuf_tensor(pfx + "scr", [128, 1024], F32))
    ob = TV(nc.alloc_sbuf_tensor(pfx + "ob", [128, 4, 1024], F32))
    maski = TV(nc.alloc_sbuf_tensor(pfx + "maski", [128, 16], I32))
    negt = TV(nc.alloc_sbuf_tensor(pfx + "negt", [128, 16], F32))
    P.v("memset", negt, [-1e9], tag=pfx + "neg", engine="gpsimd")
    pb = [ps(nc, pfx + f"pb{i}", 128, 512, F32) for i in range(8)]

    rin_ap = resid_in.ap().rearrange("(c p) d -> p c d", p=128)
    rout_ap = out_hbm.ap().rearrange("(c p) d -> p c d", p=128)

    P.dma("gpsimd", gatesb, gate_h.ap().rearrange("(a p) e -> p a e", p=128),
          in_regs=[dram_reg(gate_h)], tag="gate")

    for half in range(2):
        tb = half * TH
        cb = half * TH // 128
        for ch in range(NCH):
            P.dma("sync", xch, rin_ap[:, cb + ch * 4:cb + (ch + 1) * 4, :],
                  in_regs=[dram_reg(resid_in, tb + ch * 512, tb + (ch + 1) * 512)],
                  tag=f"h{half}x{ch}")
            for c in range(4):
                x_t = xch[:, c * 1024:(c + 1) * 1024]
                mu = stats[:, c * 2:c * 2 + 1]
                rs_ = stats[:, c * 2 + 1:c * 2 + 2]
                ln_stats(P, x_t, mu, rs_, scr, 1.0 / 1024, tag=f"h{half}ln{ch}_{c}")
                P.v("tensor_scalar", xlnt[:, c * 1024:(c + 1) * 1024], [x_t],
                    scalar1=mu, scalar2=rs_, op0=ALU.subtract, op1=ALU.mult,
                    tag=f"h{half}lnn{ch}_{c}")
            for c in range(4):
                for dt_ in range(8):
                    pv = TV(pb[7].h, (0, 128), (0, 1024), "PS", vdt=BF16)
                    transpose_128(P, com, pv[:, (dt_ % 4) * 128:(dt_ % 4 + 1) * 128],
                                  xlnt[:, c * 1024 + dt_ * 128:c * 1024 + (dt_ + 1) * 128],
                                  tag=f"h{half}tr{ch}_{c}_{dt_}")
                    if dt_ % 4 == 3:
                        grp = dt_ // 4
                        dap = xmT.h.ap()[:, grp * 4:(grp + 1) * 4,
                                         ch * 512 + c * 128:ch * 512 + (c + 1) * 128]
                        pap = pv[:, 0:512].ap()
                        P.add("vector", lambda e, dap=dap, pap=pap: e.tensor_copy(out=dap, in_=pap),
                              reads=[pv], writes=[TV(xmT.h, (0, 128),
                                                     (grp * 4 * TH, (grp * 4 + 4) * TH))],
                              tag=f"h{half}tre{ch}_{c}_{grp}")
        for tt_ in range(16):
            pg = TV(pb[6].h, (0, 128), (0, 16), "PS")
            for dt_ in range(8):
                P.mm(pg, TV(xmT.h, (0, 128), (dt_ * TH + tt_ * 128, dt_ * TH + (tt_ + 1) * 128)),
                     gatesb[:, dt_ * 16:(dt_ + 1) * 16],
                     start=(dt_ == 0), stop=(dt_ == 7), tag=f"h{half}g{tt_}")
            sc_t = scs[:, tt_ * 16:(tt_ + 1) * 16]
            P.act(sc_t, pg, AF.Tanh, scale=0.5, tag=f"h{half}gs{tt_}")
            P.v("tensor_scalar", sc_t, [sc_t], scalar1=0.5, scalar2=0.5,
                op0=ALU.mult, op1=ALU.add, tag=f"h{half}gss{tt_}")
            emit_gate_combine(P, nc, com, scs, comb, gsc, maski, negt, tt_, tag=f"h{half}gc{tt_}")
        for e in range(18):
            par = e % 2
            w1b, w3b, w2b = wtv(par, 0), wtv(par, 1), wtv(par, 2)
            for (wb, wh, wtag) in ((w1b, w1_h, "w1"), (w3b, w3_h, "w3"), (w2b, w2_h, "w2")):
                src = wh.ap()[e].rearrange("(a p) f -> p a f", p=128)
                P.dma("gpsimd", wb, src, in_regs=[dram_reg(wh, e, e + 1)],
                      tag=f"h{half}{wtag}e{e}")
            def emit_y(ch):
                par2 = ch % 2
                for tsub in range(4):
                    ttile = ch * 4 + tsub
                    for h2 in range(2):
                        bank = pb[h2]
                        for ft in range(4):
                            P.mm(bank, TV(s1.h, (0, 128),
                                          (par2 * 2048 + ft * 512 + tsub * 128,
                                           par2 * 2048 + ft * 512 + (tsub + 1) * 128)),
                                 w2b[:, ft * 1024 + h2 * 512:ft * 1024 + (h2 + 1) * 512],
                                 start=(ft == 0), stop=(ft == 3), tag=f"h{half}e{e}c{ch}y")
                        dst = acc[:, ttile * 1024 + h2 * 512:ttile * 1024 + (h2 + 1) * 512]
                        if e == 0:
                            P.v("tensor_scalar", dst, [bank],
                                scalar1=comb[:, ttile * 16 + e:ttile * 16 + e + 1],
                                scalar2=None, op0=ALU.mult,
                                tag=f"h{half}e{e}c{ch}a")
                        elif e < 16:
                            P.v("scalar_tensor_tensor", dst, [bank],
                                scalar=comb[:, ttile * 16 + e:ttile * 16 + e + 1],
                                in1=dst, op0=ALU.mult, op1=ALU.add,
                                tag=f"h{half}e{e}c{ch}a")
                        else:
                            P.v("scalar_tensor_tensor", dst, [bank],
                                scalar=1.0, in1=dst, op0=ALU.mult, op1=ALU.add,
                                tag=f"h{half}e{e}c{ch}a")
            for ch in range(NCH):
                par2 = ch % 2
                for ft in range(4):
                    for dt_ in range(8):
                        P.mm(pb[2 + ft], w1b[:, dt_ * 512 + ft * 128:dt_ * 512 + (ft + 1) * 128],
                             TV(xmT.h, (0, 128), (dt_ * TH + ch * 512, dt_ * TH + (ch + 1) * 512)),
                             start=(dt_ == 0), stop=(dt_ == 7), tag=f"h{half}e{e}c{ch}h1")
                    P.act(TV(s1.h, (0, 128), (par2 * 2048 + ft * 512, par2 * 2048 + (ft + 1) * 512)),
                          pb[2 + ft], AF.Silu, tag=f"h{half}e{e}c{ch}s1_{ft}")
                for ft in range(4):
                    for dt_ in range(8):
                        P.mm(pb[(6 + ft) % 8 if ft < 2 else 6 + ft - 2] if False else pb[6 if ft % 2 == 0 else 7],
                             w3b[:, dt_ * 512 + ft * 128:dt_ * 512 + (ft + 1) * 128],
                             TV(xmT.h, (0, 128), (dt_ * TH + ch * 512, dt_ * TH + (ch + 1) * 512)),
                             start=(dt_ == 0), stop=(dt_ == 7), tag=f"h{half}e{e}c{ch}h3")
                    P.v("tensor_copy", s3[:, ft * 512:(ft + 1) * 512],
                        [pb[6 if ft % 2 == 0 else 7]],
                        tag=f"h{half}e{e}c{ch}s3_{ft}")
                for ft in range(4):
                    P.add("vector", lambda e_, d=TV(s1.h, (0, 128),
                                                    (par2 * 2048 + ft * 512,
                                                     par2 * 2048 + (ft + 1) * 512)).ap(),
                          b2=s3[:, ft * 512:(ft + 1) * 512].ap():
                          e_.tensor_tensor(out=d, in0=d, in1=b2, op=ALU.mult),
                          reads=[TV(s1.h, (0, 128), (par2 * 2048 + ft * 512,
                                                     par2 * 2048 + (ft + 1) * 512)),
                                 s3],
                          writes=[TV(s1.h, (0, 128), (par2 * 2048 + ft * 512,
                                                      par2 * 2048 + (ft + 1) * 512))],
                          op=None, tag=f"h{half}e{e}c{ch}gu{ft}") if False else                     P.v("tensor_tensor",
                        TV(s1.h, (0, 128), (par2 * 2048 + ft * 512, par2 * 2048 + (ft + 1) * 512)),
                        [TV(s1.h, (0, 128), (par2 * 2048 + ft * 512, par2 * 2048 + (ft + 1) * 512)),
                         s3[:, ft * 512:(ft + 1) * 512]],
                        op=ALU.mult, tag=f"h{half}e{e}c{ch}gu{ft}")
                if ch > 0:
                    emit_y(ch - 1)
            emit_y(NCH - 1)
        for ch in range(NCH):
            P.dma("sync", xch, rin_ap[:, cb + ch * 4:cb + (ch + 1) * 4, :],
                  in_regs=[dram_reg(resid_in, tb + ch * 512, tb + (ch + 1) * 512)],
                  tag=f"h{half}xr{ch}")
            for c in range(4):
                ttile = ch * 4 + c
                a_t = acc[:, ttile * 1024:(ttile + 1) * 1024]
                mu = stats[:, 16 + c * 2:17 + c * 2]
                rs_ = stats[:, 17 + c * 2:18 + c * 2]
                ln_stats(P, a_t, mu, rs_, scr, 1.0 / 1024, tag=f"h{half}lo{ch}_{c}")
                P.v("tensor_scalar", a_t, [a_t], scalar1=mu, scalar2=rs_,
                    op0=ALU.subtract, op1=ALU.mult, tag=f"h{half}lon{ch}_{c}")
                P.v("tensor_tensor", ob[:, c * 1024:(c + 1) * 1024],
                    [a_t, xch[:, c * 1024:(c + 1) * 1024]], op=ALU.add,
                    tag=f"h{half}oadd{ch}_{c}")
            P.dma("gpsimd", rout_ap[:, cb + ch * 4:cb + (ch + 1) * 4, :], ob,
                  out_regs=[dram_reg(out_hbm, tb + ch * 512, tb + (ch + 1) * 512)],
                  in_regs=[ob], tag=f"h{half}o{ch}")


def emit_gate_combine(P, nc, com, scs, comb, gsc, maski, negt, tt_, tag="gc"):
    base = tt_ * 16
    scall = TV(scs.h, (0, 128), (base, base + 16))
    msk = TV(comb.h, (0, 128), (base, base + 16))
    g = gsc

    def gcol(i):
        return scs.h.ap().rearrange("p a b -> p (a b)")[:, base:base + 16] \
            .rearrange("p (gg i) -> p gg i", gg=4)[:, :, i:i + 1].squeeze(2)

    rd = [scall, g]
    wr = [g]

    def vv(dst, a, b, op, t):
        P.add("vector", lambda e, d=dst, a=a, b=b, op=op:
              e.tensor_tensor(out=d, in0=a, in1=b, op=op), reads=rd, writes=wr,
              tag=tag + t)

    c0, c1, c2, c3 = gcol(0), gcol(1), gcol(2), gcol(3)
    m2 = g[:, 0:4].ap(); tmp = g[:, 4:8].ap()
    vv(m2, c0, c1, ALU.min, "p01")
    for (a, b, t) in ((c0, c2, "02"), (c0, c3, "03"), (c1, c2, "12"),
                      (c1, c3, "13"), (c2, c3, "23")):
        vv(tmp, a, b, ALU.min, "p" + t)
        vv(m2, m2, tmp, ALU.max, "x" + t)
    m1 = g[:, 4:8].ap()
    vv(m1, c0, c1, ALU.max, "m1a")
    vv(m1, m1, c2, ALU.max, "m1b")
    vv(m1, m1, c3, ALU.max, "m1c")
    gs = g[:, 12:16].ap()
    vv(gs, m1, m2, ALU.add, "gs")
    gg = lambda i: g[:, 12 + i:13 + i].ap()
    t2 = g[:, 16:17].ap(); tm = g[:, 17:18].ap()
    vv(t2, gg(0), gg(1), ALU.min, "q01")
    for (a, b, t) in ((gg(0), gg(2), "02"), (gg(0), gg(3), "03"),
                      (gg(1), gg(2), "12"), (gg(1), gg(3), "13"),
                      (gg(2), gg(3), "23")):
        vv(tm, a, b, ALU.min, "q" + t)
        vv(t2, t2, tm, ALU.max, "y" + t)
    pen = g[:, 18:22].ap()
    P.add("vector", lambda e, d=pen, a=gs, s=t2: e.tensor_scalar(
        out=d, in0=a, scalar1=s, scalar2=None, op0=ALU.is_ge),
        reads=rd, writes=wr, tag=tag + "pen0")
    P.add("vector", lambda e, d=pen: e.tensor_scalar(
        out=d, in0=d, scalar1=1e9, scalar2=-1e9, op0=ALU.mult, op1=ALU.add),
        reads=rd, writes=wr, tag=tag + "pen1")
    for gi in range(4):
        P.add("vector", lambda e, d=msk[:, gi * 4:(gi + 1) * 4].ap(),
              a=scall[:, gi * 4:(gi + 1) * 4].ap(), s=g[:, 18 + gi:19 + gi].ap():
              e.tensor_scalar(out=d, in0=a, scalar1=s, scalar2=None, op0=ALU.add),
              reads=rd, writes=[TV(comb.h, (0, 128), (base, base + 16)), g],
              tag=tag + f"msk{gi}")
    r = g[:, 16:32].ap()
    P.add("vector", lambda e, d=r, a=msk.ap(): e.tensor_copy(out=d, in_=a),
          reads=[msk, g], writes=[g], tag=tag + "rcp")
    mx = g[:, 8:9].ap()
    for it in range(3):
        P.add("vector", lambda e, d=mx, a=r: e.reduce_max(out=d, in_=a, axis=AX.X),
              reads=[g], writes=[g], tag=tag + f"t4m{it}")
        P.add("vector", lambda e, d=maski.ap(), a=r, s=mx: e.tensor_scalar(
            out=d, in0=a, scalar1=s, scalar2=None, op0=ALU.is_ge),
            reads=[g], writes=[maski], tag=tag + f"t4e{it}")
        P.add("vector", lambda e, d=r, m=maski.ap(), nt=negt.ap():
              e.copy_predicated(out=d, mask=m, data=nt),
              reads=[g, maski, negt], writes=[g], tag=tag + f"t4s{it}")
    th4 = g[:, 9:10].ap()
    P.add("vector", lambda e, d=th4, a=r: e.reduce_max(out=d, in_=a, axis=AX.X),
          reads=[g], writes=[g], tag=tag + "th4")
    P.add("vector", lambda e, d=msk.ap(), a=msk.ap(), s=th4: e.tensor_scalar(
        out=d, in0=a, scalar1=s, scalar2=None, op0=ALU.is_ge),
        reads=[msk, g], writes=[msk], tag=tag + "sel")
    P.add("vector", lambda e, d=msk.ap(), a=msk.ap(), b=scall.ap(): e.tensor_tensor(
        out=d, in0=a, in1=b, op=ALU.mult),
        reads=[msk, scall], writes=[msk], tag=tag + "wr")
    dn = g[:, 10:11].ap()
    P.add("vector", lambda e, d=dn, a=msk.ap(): e.reduce_sum(out=d, in_=a, axis=AX.X),
          reads=[msk, g], writes=[g], tag=tag + "dn")
    P.add("vector", lambda e, d=dn: e.tensor_scalar(
        out=d, in0=d, scalar1=1e-6, scalar2=None, op0=ALU.add),
        reads=[g], writes=[g], tag=tag + "dne")
    rc = g[:, 11:12].ap()
    P.add("vector", lambda e, d=rc, a=dn: e.reciprocal(out=d, in_=a),
          reads=[g], writes=[g], tag=tag + "rc")
    P.add("vector", lambda e, d=msk.ap(), a=msk.ap(), s=rc: e.tensor_scalar(
        out=d, in0=a, scalar1=s, scalar2=None, op0=ALU.mult),
        reads=[msk, g], writes=[msk], tag=tag + "cmb")



def emit_attn(P, nc, com, resid_in, out_hbm, wdq_h, wuq_h, wdkv_h, wuk_h,
              wuv_h, wo_h, pfx="a"):
    NB = B_LOC
    wdq = TV(nc.alloc_sbuf_tensor(pfx + "wdq", [128, 8, 256], BF16))
    wdkv = TV(nc.alloc_sbuf_tensor(pfx + "wdkv", [128, 8, 256], BF16))
    wuq = TV(nc.alloc_sbuf_tensor(pfx + "wuq", [128, 2, 1024], BF16))
    wuk = TV(nc.alloc_sbuf_tensor(pfx + "wuk", [128, 2, 1024], BF16))
    wuv = TV(nc.alloc_sbuf_tensor(pfx + "wuv", [128, 2, 1024], BF16))
    wo = TV(nc.alloc_sbuf_tensor(pfx + "wo", [128, 8, 1024], BF16))
    cqT = TV(nc.alloc_sbuf_tensor(pfx + "cqT", [128, 2, T_LOC], BF16))
    ckvT = TV(nc.alloc_sbuf_tensor(pfx + "ckvT", [128, 2, T_LOC], BF16))
    vt = TV(nc.alloc_sbuf_tensor(pfx + "vt", [128, 16, 16, 65], BF16))
    qt = TV(nc.alloc_sbuf_tensor(pfx + "qt", [64, 2048], BF16))
    kt = TV(nc.alloc_sbuf_tensor(pfx + "kt", [64, 2048], BF16))
    et = TV(nc.alloc_sbuf_tensor(pfx + "et", [128, 16, 512], BF16))
    oT = TV(nc.alloc_sbuf_tensor(pfx + "oT", [128, 8, 2048], BF16))
    masks = TV(nc.alloc_sbuf_tensor(pfx + "masks", [128, 4, 512], BF16))
    obuf = TV(nc.alloc_sbuf_tensor(pfx + "obuf", [64, 512], BF16))
    recb = TV(nc.alloc_sbuf_tensor(pfx + "recb", [128, 512], F32))
    arena = nc.alloc_sbuf_tensor(pfx + "arena", [128, 32768], mybir.dt.uint8)
    xch = arena_tv(arena, 0, [128, 4 * 1024], F32)
    ych = arena_tv(arena, 16384, [128, 2 * 1024], F32)
    xlnt = arena_tv(arena, 24576, [128, 2048], BF16)
    scr = arena_tv(arena, 28672, [128, 1024], F32)
    xlnT = TV(nc.alloc_sbuf_tensor(pfx + "xlnT", [128, 8, 512], BF16))
    stats = TV(nc.alloc_sbuf_tensor(pfx + "stats", [128, 32], F32))
    pb = [ps(nc, pfx + f"pb{i}", 128, 512, F32) for i in range(8)]

    rin_ap = resid_in.ap().rearrange("(c p) d -> p c d", p=128)
    rout_ap = out_hbm.ap().rearrange("(c p) d -> p c d", p=128)

    for (wb, wh, n) in ((wdq, wdq_h, "wdq"), (wdkv, wdkv_h, "wdkv")):
        P.dma("gpsimd", wb, wh.ap().rearrange("(a p) f -> p a f", p=128),
              in_regs=[dram_reg(wh)], tag=n)
    for (wb, wh, n) in ((wuq, wuq_h, "wuq"), (wuk, wuk_h, "wuk"), (wuv, wuv_h, "wuv")):
        P.dma("gpsimd", wb, wh.ap().rearrange("(a p) f -> p a f", p=128),
              in_regs=[dram_reg(wh)], tag=n)
    P.dma("gpsimd", wo, wo_h.ap().rearrange("(a p) f -> p a f", p=128),
          in_regs=[dram_reg(wo_h)], tag="wo")
    P.v("memset", masks, [1.0], tag="mkáµ¢", engine="gpsimd")
    for j in range(4):
        dap = masks.h.ap()[:, j, :]
        P.add("gpsimd", lambda e, dap=dap, j=j: e.affine_select(
            out=dap, in_=dap, pattern=[[1, 512]], compare_op=ALU.is_ge,
            fill=0.0, base=-128 * j, channel_multiplier=-1),
            reads=[masks], writes=[masks], tag=f"mask{j}")

    def stage_a(ch):
        P.dma("sync", xch, rin_ap[:, ch * 4:(ch + 1) * 4, :],
              in_regs=[dram_reg(resid_in, ch * 512, (ch + 1) * 512)], tag=f"x{ch}")
        for c in range(4):
            x_t = xch[:, c * 1024:(c + 1) * 1024]
            mu = stats[:, c * 2:c * 2 + 1]
            rs_ = stats[:, c * 2 + 1:c * 2 + 2]
            ln_stats(P, x_t, mu, rs_, scr, 1.0 / 1024, tag=f"ln{ch}_{c}")
            P.v("tensor_scalar", xlnt[:, (c % 2) * 1024:(c % 2) * 1024 + 1024], [x_t],
                scalar1=mu, scalar2=rs_, op0=ALU.subtract, op1=ALU.mult,
                tag=f"lnn{ch}_{c}")
            for dt_ in range(8):
                pv = TV(pb[7].h, (0, 128), (0, 1024), "PS", vdt=BF16)
                transpose_128(P, com, pv[:, (dt_ % 4) * 128:(dt_ % 4 + 1) * 128],
                              xlnt[:, (c % 2) * 1024 + dt_ * 128:(c % 2) * 1024 + (dt_ + 1) * 128],
                              tag=f"tr{ch}_{c}_{dt_}")
                if dt_ % 4 == 3:
                    grp = dt_ // 4
                    dap = xlnT.h.ap()[:, grp * 4:(grp + 1) * 4, c * 128:(c + 1) * 128]
                    pap = pv[:, 0:512].ap()
                    P.add("vector", lambda e, dap=dap, pap=pap: e.tensor_copy(out=dap, in_=pap),
                          reads=[pv], writes=[TV(xlnT.h, (0, 128),
                                                 (grp * 4 * 512, (grp * 4 + 4) * 512))],
                          tag=f"tre{ch}_{c}_{grp}")
        for (dst, wsb, bank) in ((cqT, wdq, 0), (ckvT, wdkv, 2)):
            for ct in range(2):
                pp = pb[bank + ct]
                for dt_ in range(8):
                    P.mm(pp, wsb[:, dt_ * 256 + ct * 128:dt_ * 256 + (ct + 1) * 128],
                         xlnT[:, dt_ * 512:(dt_ + 1) * 512],
                         start=(dt_ == 0), stop=(dt_ == 7), tag=f"c{ch}_{ct}")
                dap = dst.h.ap()[:, ct, ch * 512:(ch + 1) * 512]
                P.add("scalar", lambda e, dap=dap, pap=pp.ap(): e.copy(out=dap, in_=pap),
                      reads=[pp], writes=[TV(dst.h, (0, 128),
                                             (ct * T_LOC + ch * 512, ct * T_LOC + (ch + 1) * 512))],
                      tag=f"ce{ch}_{ct}")

    for b in range(NB):
        for ch in range(4 * b, 4 * b + 4):
            stage_a(ch)
        tb = b * 2048
        onesap = vt.h.ap()[:, :, :, 64:65]
        P.add("vector", lambda e, d=onesap: e.memset(d, 1.0),
              reads=[], writes=[vt], tag=f"b{b}vones")
        for tt_ in range(16):
            for half in range(2):
                pp = pb[6]
                for ct in range(2):
                    P.mm(pp, TV(ckvT.h, (0, 128),
                                (ct * T_LOC + tb + tt_ * 128, ct * T_LOC + tb + (tt_ + 1) * 128)),
                         wuv[:, ct * 1024 + half * 512:ct * 1024 + (half + 1) * 512],
                         start=(ct == 0), stop=(ct == 1), tag=f"b{b}v{tt_}_{half}")
                dap = vt.h.ap()[:, tt_, half * 8:(half + 1) * 8, 0:64]
                pap = pp.ap().rearrange("p (h v) -> p h v", h=8)
                P.add("vector", lambda e, dap=dap, pap=pap: e.tensor_copy(out=dap, in_=pap),
                      reads=[pp], writes=[TV(vt.h, (0, 128),
                                             (tt_ * 16 * 65, (tt_ + 1) * 16 * 65))],
                      tag=f"b{b}ve{tt_}_{half}")
        for h in range(16):
            for (dst, wsb) in ((qt, wuq), (kt, wuk)):
                for qc in range(4):
                    pp = TV(pb[6].h, (0, 64), (0, 512), "PS")
                    for ct in range(2):
                        P.mm(pp, wsb[:, ct * 1024 + h * 64:ct * 1024 + (h + 1) * 64],
                             TV(cqT.h if dst is qt else ckvT.h, (0, 128),
                                (ct * T_LOC + tb + qc * 512, ct * T_LOC + tb + (qc + 1) * 512)),
                             start=(ct == 0), stop=(ct == 1), tag=f"b{b}h{h}qk{qc}")
                    P.add("scalar", lambda e, dap=dst[:, qc * 512:(qc + 1) * 512].ap(),
                          pap=pp.ap(): e.copy(out=dap, in_=pap),
                          reads=[pp], writes=[dst[:, qc * 512:(qc + 1) * 512]],
                          tag=f"b{b}h{h}qke{qc}")
            for qc in range(4):
                nkt = 4 * (qc + 1)
                for kt_ in range(nkt):
                    pp = pb[kt_ % 4]
                    P.mm(pp, kt[:, kt_ * 128:(kt_ + 1) * 128], qt[:, qc * 512:(qc + 1) * 512],
                         start=True, stop=True, tag=f"b{b}h{h}s{qc}_{kt_}")
                    P.act(et[:, kt_ * 512:(kt_ + 1) * 512], pp, AF.Exp,
                          scale=0.125, bias=-20.0, tag=f"b{b}h{h}e{qc}_{kt_}")
                    j = kt_ - 4 * qc
                    if j >= 0:
                        P.v("tensor_tensor", et[:, kt_ * 512:(kt_ + 1) * 512],
                            [et[:, kt_ * 512:(kt_ + 1) * 512], masks[:, j * 512:(j + 1) * 512]],
                            op=ALU.mult, tag=f"b{b}h{h}m{qc}_{kt_}")
                po = TV(pb[4].h, (0, 65), (0, 512), "PS")
                for kt_ in range(nkt):
                    P.mm(po, TV(vt.h, (0, 128), ((kt_ * 16 + h) * 65, (kt_ * 16 + h) * 65 + 65)),
                         et[:, kt_ * 512:(kt_ + 1) * 512],
                         start=(kt_ == 0), stop=(kt_ == nkt - 1), tag=f"b{b}h{h}pv{qc}")
                P.v("reciprocal", TV(recb.h, (64, 65), (0, 512)),
                    [TV(pb[4].h, (64, 65), (0, 512), "PS")], tag=f"b{b}h{h}rc{qc}")
                prd = TV(pb[5].h, (0, 64), (0, 512), "PS")
                P.mm(prd, TV(com.ones.h, (64, 65), (0, 64)),
                     TV(recb.h, (64, 65), (0, 512)), start=True, stop=True,
                     tag=f"b{b}h{h}rb{qc}")
                P.add("scalar", lambda e, dap=obuf.ap(),
                      pap=TV(pb[4].h, (0, 64), (0, 512), "PS").ap(): e.copy(out=dap, in_=pap),
                      reads=[TV(pb[4].h, (0, 65), (0, 512), "PS")], writes=[obuf],
                      tag=f"b{b}h{h}oe{qc}")
                dst = TV(oT.h, ((h % 2) * 64, (h % 2) * 64 + 64),
                         ((h // 2) * 2048 + qc * 512, (h // 2) * 2048 + (qc + 1) * 512))
                P.v("tensor_tensor", dst, [prd, obuf], op=ALU.mult,
                    tag=f"b{b}h{h}od{qc}")
        for ch in range(4):
            P.dma("sync", xch, rin_ap[:, b * 16 + ch * 4:b * 16 + (ch + 1) * 4, :],
                  in_regs=[dram_reg(resid_in, tb + ch * 512, tb + (ch + 1) * 512)],
                  tag=f"b{b}xr{ch}")
            for ts4 in range(4):
                yv = ych[:, (ts4 % 2) * 1024:(ts4 % 2 + 1) * 1024]
                for half in range(2):
                    pp = pb[2 * (ts4 % 2) + half]
                    for dt_ in range(8):
                        P.mm(pp, TV(oT.h, (0, 128),
                                    (dt_ * 2048 + ch * 512 + ts4 * 128,
                                     dt_ * 2048 + ch * 512 + (ts4 + 1) * 128)),
                             wo[:, dt_ * 1024 + half * 512:dt_ * 1024 + (half + 1) * 512],
                             start=(dt_ == 0), stop=(dt_ == 7), tag=f"b{b}wo{ch}_{ts4}")
                    P.v("tensor_copy", yv[:, half * 512:(half + 1) * 512], [pp],
                        tag=f"b{b}woe{ch}_{ts4}_{half}")
                mu = stats[:, 16 + 2 * (ts4 % 2):17 + 2 * (ts4 % 2)]
                rs_ = stats[:, 17 + 2 * (ts4 % 2) + 2:18 + 2 * (ts4 % 2) + 2]
                ln_stats(P, yv, mu, rs_, scr[:, (ts4 % 2) * 512:(ts4 % 2) * 512 + 512] if False else scr,
                         1.0 / 1024, tag=f"b{b}lo{ch}_{ts4}")
                P.v("tensor_scalar", yv, [yv], scalar1=mu,
                    scalar2=rs_, op0=ALU.subtract, op1=ALU.mult, tag=f"b{b}lon{ch}_{ts4}")
                P.v("tensor_tensor", xch[:, ts4 * 1024:(ts4 + 1) * 1024],
                    [yv, xch[:, ts4 * 1024:(ts4 + 1) * 1024]],
                    op=ALU.add, tag=f"b{b}oadd{ch}_{ts4}")
            P.dma("gpsimd", rout_ap[:, b * 16 + ch * 4:b * 16 + (ch + 1) * 4, :], xch,
                  out_regs=[dram_reg(out_hbm, tb + ch * 512, tb + (ch + 1) * 512)],
                  in_regs=[xch], tag=f"b{b}o{ch}")



def emit_ln_stream(P, nc, com, src_h, dst_h, pfx, add_h=None, wvec_h=None):
    xch = TV(nc.alloc_sbuf_tensor(pfx + "xch", [128, 4, 1024], F32))
    ach = TV(nc.alloc_sbuf_tensor(pfx + "ach", [128, 4, 1024], F32)) if add_h is not None else None
    scr = TV(nc.alloc_sbuf_tensor(pfx + "scr", [128, 1024], F32))
    stats = TV(nc.alloc_sbuf_tensor(pfx + "stats", [128, 8], F32))
    wb = None
    if wvec_h is not None:
        wb = TV(nc.alloc_sbuf_tensor(pfx + "wb", [128, 1024], F32))
        P.dma("sync", wb, wvec_h.ap().unsqueeze(0).partition_broadcast(128),
              in_regs=[dram_reg(wvec_h)], tag=pfx + "w")
    src_ap = src_h.ap().rearrange("(c p) d -> p c d", p=128)
    dst_ap = dst_h.ap().rearrange("(c p) d -> p c d", p=128)
    add_ap = add_h.ap().rearrange("(c p) d -> p c d", p=128) if add_h is not None else None
    for ch in range(T_LOC // 512):
        P.dma("sync", xch, src_ap[:, ch * 4:(ch + 1) * 4, :],
              in_regs=[dram_reg(src_h, ch * 512, (ch + 1) * 512)], tag=f"{pfx}x{ch}")
        if add_h is not None:
            P.dma("sync", ach, add_ap[:, ch * 4:(ch + 1) * 4, :],
                  in_regs=[dram_reg(add_h, ch * 512, (ch + 1) * 512)], tag=f"{pfx}a{ch}")
        for c in range(4):
            x_t = xch[:, c * 1024:(c + 1) * 1024]
            mu = stats[:, 0:1]
            rs_ = stats[:, 1:2]
            ln_stats(P, x_t, mu, rs_, scr, 1.0 / 1024, tag=f"{pfx}ln{ch}_{c}")
            P.v("tensor_scalar", x_t, [x_t], scalar1=mu, scalar2=rs_,
                op0=ALU.subtract, op1=ALU.mult, tag=f"{pfx}n{ch}_{c}")
            if wb is not None:
                P.v("tensor_tensor", x_t, [x_t, wb], op=ALU.mult, tag=f"{pfx}w{ch}_{c}")
            if add_h is not None:
                P.v("tensor_tensor", x_t, [x_t, ach[:, c * 1024:(c + 1) * 1024]],
                    op=ALU.add, tag=f"{pfx}ad{ch}_{c}")
        P.dma("gpsimd", dst_ap[:, ch * 4:(ch + 1) * 4, :], xch,
              out_regs=[dram_reg(dst_h, ch * 512, (ch + 1) * 512)],
              in_regs=[xch], tag=f"{pfx}o{ch}")


def emit_rwkv_post(P, nc, com, xi_h, residb_h, out_h, lnoutw_h, pfx="rp"):
    xch = TV(nc.alloc_sbuf_tensor(pfx + "xch", [128, 4, 1024], F32))
    ach = TV(nc.alloc_sbuf_tensor(pfx + "ach", [128, 4, 1024], F32))
    scr = TV(nc.alloc_sbuf_tensor(pfx + "scr", [128, 1024], F32))
    stats = TV(nc.alloc_sbuf_tensor(pfx + "stats", [128, 8], F32))
    wb = TV(nc.alloc_sbuf_tensor(pfx + "wb", [128, 1024], F32))
    P.dma("sync", wb, lnoutw_h.ap().unsqueeze(0).partition_broadcast(128),
          in_regs=[dram_reg(lnoutw_h)], tag=pfx + "w")
    src_ap = xi_h.ap().rearrange("(c p) d -> p c d", p=128)
    add_ap = residb_h.ap().rearrange("(c p) d -> p c d", p=128)
    dst_ap = out_h.ap().rearrange("(c p) d -> p c d", p=128)
    for ch in range(T_LOC // 512):
        P.dma("sync", xch, src_ap[:, ch * 4:(ch + 1) * 4, :],
              in_regs=[dram_reg(xi_h, ch * 512, (ch + 1) * 512)], tag=f"{pfx}x{ch}")
        P.dma("sync", ach, add_ap[:, ch * 4:(ch + 1) * 4, :],
              in_regs=[dram_reg(residb_h, ch * 512, (ch + 1) * 512)], tag=f"{pfx}a{ch}")
        for c in range(4):
            x_t = xch[:, c * 1024:(c + 1) * 1024]
            mu = stats[:, 0:1]
            rs_ = stats[:, 1:2]
            ln_stats(P, x_t, mu, rs_, scr, 1.0 / 1024, tag=f"{pfx}l1{ch}_{c}")
            P.v("tensor_scalar", x_t, [x_t], scalar1=mu, scalar2=rs_,
                op0=ALU.subtract, op1=ALU.mult, tag=f"{pfx}n1{ch}_{c}")
            P.v("tensor_tensor", x_t, [x_t, wb], op=ALU.mult, tag=f"{pfx}w{ch}_{c}")
            ln_stats(P, x_t, mu, rs_, scr, 1.0 / 1024, tag=f"{pfx}l2{ch}_{c}")
            P.v("tensor_scalar", x_t, [x_t], scalar1=mu, scalar2=rs_,
                op0=ALU.subtract, op1=ALU.mult, tag=f"{pfx}n2{ch}_{c}")
            P.v("tensor_tensor", x_t, [x_t, ach[:, c * 1024:(c + 1) * 1024]],
                op=ALU.add, tag=f"{pfx}ad{ch}_{c}")
        P.dma("gpsimd", dst_ap[:, ch * 4:(ch + 1) * 4, :], xch,
              out_regs=[dram_reg(out_h, ch * 512, (ch + 1) * 512)],
              in_regs=[xch], tag=f"{pfx}o{ch}")


TM_STAGE = 99

def emit_timemix(P, nc, com, xi_h, pvec_h, tm1_h, tm2_h, td1_h, td2_h,
                 wr_h, wk_h, wv_h, wg_h, wo_h, gnw_h, gnb_h, masku_h, pfx="t",
                 xo_h=None):
    CH = 256
    TBANK2 = 3
    NSC = CH // 64
    NTS = CH // 128
    wsb = {}
    for (n, h_) in (("wr", wr_h), ("wk", wk_h), ("wv", wv_h), ("wg", wg_h), ("wo", wo_h)):
        wsb[n] = TV(nc.alloc_sbuf_tensor(pfx + n, [128, 8, 1024], BF16))
        P.dma("gpsimd", wsb[n], h_.ap().rearrange("(a p) f -> p a f", p=128),
              in_regs=[dram_reg(h_)], tag=pfx + n)
    tm1 = TV(nc.alloc_sbuf_tensor(pfx + "tm1", [128, 8, 160], BF16))
    P.dma("gpsimd", tm1, tm1_h.ap().rearrange("(a p) f -> p a f", p=128),
          in_regs=[dram_reg(tm1_h)], tag=pfx + "tm1")
    tm2 = TV(nc.alloc_sbuf_tensor(pfx + "tm2", [32, 5, 1024], BF16))
    P.dma("gpsimd", tm2, tm2_h.ap(), in_regs=[dram_reg(tm2_h)], tag=pfx + "tm2")
    td1 = TV(nc.alloc_sbuf_tensor(pfx + "td1", [128, 8, 64], BF16))
    P.dma("gpsimd", td1, td1_h.ap().rearrange("(a p) f -> p a f", p=128),
          in_regs=[dram_reg(td1_h)], tag=pfx + "td1")
    td2 = TV(nc.alloc_sbuf_tensor(pfx + "td2", [64, 1024], BF16))
    P.dma("gpsimd", td2, td2_h.ap(), in_regs=[dram_reg(td2_h)], tag=pfx + "td2")
    pv = TV(nc.alloc_sbuf_tensor(pfx + "pv", [128, 80], F32))
    P.dma("sync", pv, pvec_h.ap(), in_regs=[dram_reg(pvec_h)], tag=pfx + "pv")
    gnw = TV(nc.alloc_sbuf_tensor(pfx + "gnw", [128, 1024], F32))
    gnb = TV(nc.alloc_sbuf_tensor(pfx + "gnb", [128, 1024], F32))
    P.dma("sync", gnw, gnw_h.ap().unsqueeze(0).partition_broadcast(128),
          in_regs=[dram_reg(gnw_h)], tag=pfx + "gnw")
    P.dma("sync", gnb, gnb_h.ap().unsqueeze(0).partition_broadcast(128),
          in_regs=[dram_reg(gnb_h)], tag=pfx + "gnb")
    masku = TV(nc.alloc_sbuf_tensor(pfx + "masku", [128, 512], BF16))
    P.dma("gpsimd", masku, masku_h.ap(), in_regs=[dram_reg(masku_h)], tag=pfx + "mu")
    sel01 = TV(nc.alloc_sbuf_tensor(pfx + "sel", [128, 2], BF16))
    P.v("memset", sel01, [0.0], tag=pfx + "s0", engine="gpsimd")
    P.v("memset", sel01[0:64, 0:1], [1.0], tag=pfx + "s1", engine="gpsimd")
    P.v("memset", TV(sel01.h, (64, 128), (1, 2)), [1.0], tag=pfx + "s2", engine="gpsimd")
    ext = TV(nc.alloc_sbuf_tensor(pfx + "ext", [128, 8, CH + 1], BF16))
    sxT = TV(nc.alloc_sbuf_tensor(pfx + "sxT", [128, 8, CH], BF16))
    mixf = TV(nc.alloc_sbuf_tensor(pfx + "mixf", [32, 5, CH], BF16))
    xf = [TV(nc.alloc_sbuf_tensor(pfx + f"xf{i}", [128, 8, CH], BF16)) for i in range(2)]
    t1b4 = TV(nc.alloc_sbuf_tensor(pfx + "t1b", [128, 4, CH], BF16))
    wT = TV(nc.alloc_sbuf_tensor(pfx + "wT", [128, 8, CH], F32))
    dT = TV(nc.alloc_sbuf_tensor(pfx + "dT", [128, 8, CH], F32))
    dprevd = TV(nc.alloc_sbuf_tensor(pfx + "dprev", [128, 2, CH], F32))
    zz = TV(nc.alloc_sbuf_tensor(pfx + "zz", [128, 64], F32))
    rt = TV(nc.alloc_sbuf_tensor(pfx + "rt", [128, 8, CH], BF16))
    ktl = TV(nc.alloc_sbuf_tensor(pfx + "ktl", [128, 8, CH], BF16))
    khat = TV(nc.alloc_sbuf_tensor(pfx + "khat", [128, 8, 256], BF16))
    sts = TV(nc.alloc_sbuf_tensor(pfx + "sts", [128, 4, 512], BF16))
    vtk = TV(nc.alloc_sbuf_tensor(pfx + "vtk", [128, NTS, 1024], BF16))
    gtk = TV(nc.alloc_sbuf_tensor(pfx + "gtk", [128, NTS, 1024], BF16))
    wkv = TV(nc.alloc_sbuf_tensor(pfx + "wkv", [128, NTS, 1024], F32))
    ddg = TV(nc.alloc_sbuf_tensor(pfx + "ddg", [128, NTS, 16], F32))

    state = TV(nc.alloc_sbuf_tensor(pfx + "state", [128, 8, 64], F32))
    stateb = TV(nc.alloc_sbuf_tensor(pfx + "stateb", [128, 8, 64], BF16))
    xch = TV(nc.alloc_sbuf_tensor(pfx + "xch", [128, NTS, 1024], F32))
    xlt = TV(nc.alloc_sbuf_tensor(pfx + "xlt", [128, 1024], BF16))
    t1g = TV(nc.alloc_sbuf_tensor(pfx + "t1g", [128, 512], BF16))
    hd = TV(nc.alloc_sbuf_tensor(pfx + "hd", [64, 256], BF16))
    t1fd = TV(nc.alloc_sbuf_tensor(pfx + "t1f", [128, 2, 256], F32))
    kscd = TV(nc.alloc_sbuf_tensor(pfx + "ksc", [128, 2, 128], BF16))
    dend = TV(nc.alloc_sbuf_tensor(pfx + "dend", [128, 32], F32))
    scr = TV(nc.alloc_sbuf_tensor(pfx + "scr", [128, 1024], F32))
    stats = TV(nc.alloc_sbuf_tensor(pfx + "stats", [128, 64], F32))
    gno = TV(nc.alloc_sbuf_tensor(pfx + "gno", [128, NTS, 1024], BF16))
    goT = TV(nc.alloc_sbuf_tensor(pfx + "goT", [128, 8, CH], BF16))
    ruk = goT
    pb = [ps(nc, pfx + f"pb{i}", 128, 512, F32) for i in range(8)]

    xi_ap = xi_h.ap().rearrange("(c p) d -> p c d", p=128)
    xo_h = xi_h if xo_h is None else xo_h
    xo_ap = xo_h.ap().rearrange("(c p) d -> p c d", p=128)
    maacol = {"x": 8, "w": 16, "k": 24, "v": 32, "r": 40, "g": 48}

    for b in range(B_LOC):
        P.v("memset", state, [0.0], tag=f"{pfx}b{b}st0", engine="gpsimd")
        P.v("memset", stateb, [0.0], tag=f"{pfx}b{b}st0b", engine="gpsimd")
        for ch in range(2048 // CH):
            tb = b * 2048 + ch * CH
            cb = tb // 128
            P.dma("sync", xch, xi_ap[:, cb:cb + NTS, :],
                  in_regs=[dram_reg(xi_h, tb, tb + CH)], tag=f"{pfx}b{b}c{ch}x")
            if ch == 0:
                P.v("memset", TV(ext.h, (0, 128), (0, 8 * (CH + 1))), [0.0],
                    tag=f"{pfx}b{b}c{ch}z", engine="gpsimd")
            else:
                eap = ext.h.ap()
                P.add("vector", lambda e, d=eap[:, :, 0:1], s=eap[:, :, CH:CH + 1]:
                      e.tensor_copy(out=d, in_=s),
                      reads=[ext], writes=[TV(ext.h, (0, 128), (0, 8 * (CH + 1)))],
                      tag=f"{pfx}b{b}c{ch}bd")
            for c in range(NTS):
                x_t = xch[:, c * 1024:(c + 1) * 1024]
                mu = stats[:, 0:1]; rs_ = stats[:, 1:2]
                ln_stats(P, x_t, mu, rs_, scr, 1.0 / 1024, tag=f"{pfx}b{b}c{ch}l{c}")
                P.v("tensor_scalar", xlt, [x_t], scalar1=mu, scalar2=rs_,
                    op0=ALU.subtract, op1=ALU.mult, tag=f"{pfx}b{b}c{ch}n{c}")
                for dt_ in range(8):
                    grp = dt_ // 4
                    pvw = TV(pb[7 if (2 * c + grp) % 2 == 0 else TBANK2].h,
                             (0, 128), (0, 1024), "PS", vdt=BF16)
                    transpose_128(P, com, pvw[:, (dt_ % 4) * 128:(dt_ % 4 + 1) * 128],
                                  TV(xlt.h, (0, 128), (dt_ * 128, (dt_ + 1) * 128)),
                                  tag=f"{pfx}b{b}c{ch}t{c}_{dt_}")
                    if dt_ % 4 == 3:
                        dap = ext.h.ap()[:, grp * 4:(grp + 1) * 4,
                                         1 + c * 128:1 + (c + 1) * 128]
                        P.add("vector", lambda e, dap=dap, pap=pvw[:, 0:512].ap():
                              e.tensor_copy(out=dap, in_=pap),
                              reads=[pvw], writes=[TV(ext.h, (0, 128), (0, 8 * (CH + 1)))],
                              tag=f"{pfx}b{b}c{ch}te{c}_{grp}")
            for dt_ in range(8):
                P.add("vector", lambda e,
                      d=ext.h.ap()[:, dt_, 1:CH + 1], s=pv[:, dt_:dt_ + 1].ap():
                      e.tensor_scalar(out=d, in0=d, scalar1=s, scalar2=None, op0=ALU.mult),
                      reads=[ext, pv], writes=[TV(ext.h, (0, 128), (0, 8 * (CH + 1)))],
                      tag=f"{pfx}b{b}c{ch}lw{dt_}")
            P.add("vector", lambda e, d=sxT.ap(),
                  a=ext.h.ap()[:, :, 0:CH], b2=ext.h.ap()[:, :, 1:CH + 1]:
                  e.tensor_tensor(out=d, in0=a, in1=b2, op=ALU.subtract),
                  reads=[ext], writes=[sxT], tag=f"{pfx}b{b}c{ch}sx")
            if TM_STAGE < 2:
                P.dma("gpsimd", xo_ap[:, cb:cb + NTS, :], xch,
                      out_regs=[dram_reg(xo_h, tb, tb + CH)], in_regs=[xch],
                      tag=f"{pfx}b{b}c{ch}oS1")
                continue
            xxx = xf[0]
            for dt_ in range(8):
                P.add("vector", lambda e, d=xxx.h.ap()[:, dt_, :],
                      a=sxT.h.ap()[:, dt_, :], s=pv[:, maacol["x"] + dt_:maacol["x"] + dt_ + 1].ap(),
                      x2=ext.h.ap()[:, dt_, 1:CH + 1]:
                      e.scalar_tensor_tensor(out=d, in0=a, scalar=s, in1=x2,
                                             op0=ALU.mult, op1=ALU.add),
                      reads=[sxT, pv, ext], writes=[xxx], tag=f"{pfx}b{b}c{ch}xx{dt_}")
            for f in range(5):
                pm = TV(pb[f % 2].h, (0, 32), (0, CH), "PS")
                for dt_ in range(8):
                    P.mm(pm, tm1[:, dt_ * 160 + f * 32:dt_ * 160 + (f + 1) * 32],
                         TV(xxx.h, (0, 128), (dt_ * CH, (dt_ + 1) * CH)),
                         start=(dt_ == 0), stop=(dt_ == 7), tag=f"{pfx}b{b}c{ch}m{f}")
                P.act(TV(mixf.h, (0, 32), (f * CH, (f + 1) * CH)), pm, AF.Tanh,
                      tag=f"{pfx}b{b}c{ch}mt{f}")
            if TM_STAGE < 3:
                P.dma("gpsimd", xo_ap[:, cb:cb + NTS, :], xch,
                      out_regs=[dram_reg(xo_h, tb, tb + CH)], in_regs=[xch],
                      tag=f"{pfx}b{b}c{ch}oS2")
                continue
            for fi, f in enumerate("wkvrg"):
                buf = xf[fi % 2]
                for dt_ in range(8):
                    pm = pb[4 + dt_ % 2]
                    P.mm(TV(pm.h, (0, 128), (0, CH), "PS"),
                         TV(tm2.h, (0, 32), (fi * 1024 + dt_ * 128, fi * 1024 + (dt_ + 1) * 128)),
                         TV(mixf.h, (0, 32), (fi * CH, (fi + 1) * CH)),
                         start=True, stop=True, tag=f"{pfx}b{b}c{ch}{f}m{dt_}")
                    t1s = t1b4[:, (dt_ % 4) * CH:(dt_ % 4 + 1) * CH]
                    P.add("vector", lambda e, d=t1s.ap(),
                          pap=TV(pm.h, (0, 128), (0, CH), "PS").ap(),
                          s=pv[:, maacol[f] + dt_:maacol[f] + dt_ + 1].ap(),
                          sx=sxT.h.ap()[:, dt_, :]:
                          e.scalar_tensor_tensor(out=d, in0=pap, scalar=s, in1=sx,
                                                 op0=ALU.add, op1=ALU.mult),
                          reads=[TV(pm.h, (0, 128), (0, CH), "PS"), pv, sxT],
                          writes=[t1s], tag=f"{pfx}b{b}c{ch}{f}t2_{dt_}")
                    P.add("vector", lambda e, d=buf.h.ap()[:, dt_, :], a=t1s.ap(),
                          x2=ext.h.ap()[:, dt_, 1:CH + 1]:
                          e.tensor_tensor(out=d, in0=a, in1=x2, op=ALU.add),
                          reads=[t1s, ext], writes=[buf], tag=f"{pfx}b{b}c{ch}{f}t3_{dt_}")
                if f == "w":
                    emit_decay(P, nc, com, buf, td1, td2, pv, wT, dT, pb, hd,
                               t1fd, zz, CH, f"{pfx}b{b}c{ch}")
                elif f in ("k", "r"):
                    dst = ktl if f == "k" else rt
                    for od in range(8):
                        pm = TV(pb[(2 + od) % 4].h, (0, 128), (0, CH), "PS")
                        for dt_ in range(8):
                            P.mm(pm, wsb["w" + f][:, dt_ * 1024 + od * 128:dt_ * 1024 + (od + 1) * 128],
                                 TV(buf.h, (0, 128), (dt_ * CH, (dt_ + 1) * CH)),
                                 start=(dt_ == 0), stop=(dt_ == 7), tag=f"{pfx}b{b}c{ch}{f}W{od}")
                        if f == "k":
                            P.add("scalar", lambda e, d=dst.h.ap()[:, od, :], pap=pm.ap():
                                  e.copy(out=d, in_=pap), reads=[pm], writes=[dst],
                                  tag=f"{pfx}b{b}c{ch}{f}We{od}")
                        else:
                            P.act(TV(dst.h, (0, 128), (od * CH, (od + 1) * CH)), pm,
                                  AF.Tanh, scale=0.5, tag=f"{pfx}b{b}c{ch}rWt{od}")
                            P.add("vector", lambda e, d=dst.h.ap()[:, od, :]:
                                  e.tensor_scalar(out=d, in0=d, scalar1=0.5, scalar2=0.5,
                                                  op0=ALU.mult, op1=ALU.add),
                                  reads=[dst], writes=[dst], tag=f"{pfx}b{b}c{ch}rWs{od}")
                else:
                    dst = vtk if f == "v" else gtk
                    for ts_ in range(NTS):
                        for hf in range(2):
                            pm = TV(pb[(2 + 2 * ts_ + hf) % 4].h, (0, 128), (0, 512), "PS")
                            for dt_ in range(8):
                                P.mm(pm, TV(buf.h, (0, 128),
                                            (dt_ * CH + ts_ * 128, dt_ * CH + (ts_ + 1) * 128)),
                                     wsb["w" + f][:, dt_ * 1024 + hf * 512:dt_ * 1024 + (hf + 1) * 512],
                                     start=(dt_ == 0), stop=(dt_ == 7),
                                     tag=f"{pfx}b{b}c{ch}{f}W{ts_}_{hf}")
                            dap = dst.h.ap()[:, ts_, hf * 512:(hf + 1) * 512]
                            if f == "v":
                                P.add("scalar", lambda e, d=dap, pap=pm.ap():
                                      e.copy(out=d, in_=pap), reads=[pm], writes=[dst],
                                      tag=f"{pfx}b{b}c{ch}vWe{ts_}_{hf}")
                            else:
                                P.act(t1g, pm, AF.Tanh, scale=0.5,
                                      tag=f"{pfx}b{b}c{ch}gWt{ts_}_{hf}")
                                P.add("vector", lambda e, d=t1g.ap():
                                      e.tensor_scalar(out=d, in0=d, scalar1=0.5, scalar2=0.5,
                                                      op0=ALU.mult, op1=ALU.add),
                                      reads=[t1g], writes=[t1g], tag=f"{pfx}b{b}c{ch}gWs{ts_}_{hf}")
                                P.add("vector", lambda e, d=dap, a=t1g.ap(), pap=pm.ap():
                                      e.tensor_tensor(out=d, in0=pap, in1=a, op=ALU.mult),
                                      reads=[t1g, pm], writes=[dst],
                                      tag=f"{pfx}b{b}c{ch}gWm{ts_}_{hf}")
            if TM_STAGE < 4:
                P.dma("gpsimd", xo_ap[:, cb:cb + NTS, :], xch,
                      out_regs=[dram_reg(xo_h, tb, tb + CH)], in_regs=[xch],
                      tag=f"{pfx}b{b}c{ch}oS3")
                continue
            emit_wkv_chunk(P, nc, com, rt, ktl, vtk, wT, dT, dprevd, khat, sts,
                           masku, sel01, ruk, ddg, state, stateb, wkv, pv, pb,
                           kscd, CH, NSC, NTS, f"{pfx}b{b}c{ch}", t1fw=t1fd,
                           dend=dend)
            if TM_STAGE < 5:
                P.dma("gpsimd", xo_ap[:, cb:cb + NTS, :], xch,
                      out_regs=[dram_reg(xo_h, tb, tb + CH)], in_regs=[xch],
                      tag=f"{pfx}b{b}c{ch}oS4")
                continue
            for ts_ in range(NTS):
                seg3 = wkv.h.ap()[:, ts_, :].rearrange("p (a b) -> p a b", b=64)
                mu = stats[:, 16:32]
                rs_ = stats[:, 32:48]
                mu3 = lambda t: t.ap().rearrange("p (a b) -> p a b", b=1).broadcast_to([128, 16, 64])
                P.add("vector", lambda e, d=mu.ap().rearrange("p (a b) -> p a b", b=1),
                      a=seg3: e.reduce_sum(out=d, in_=a, axis=AX.X),
                      reads=[wkv], writes=[stats], tag=f"{pfx}b{b}c{ch}gnm{ts_}")
                P.v("tensor_scalar", mu, [mu], scalar1=1.0 / 64, scalar2=None,
                    op0=ALU.mult, tag=f"{pfx}b{b}c{ch}gnm2{ts_}")
                P.add("vector", lambda e, d=scr.ap().rearrange("p (a b) -> p a b", b=64),
                      a=seg3, m=mu3(mu):
                      e.tensor_tensor(out=d, in0=a, in1=m, op=ALU.subtract),
                      reads=[wkv, stats], writes=[scr], tag=f"{pfx}b{b}c{ch}gnc{ts_}")
                P.add("vector", lambda e, d=wkv.h.ap()[:, ts_, :], a=scr.ap():
                      e.tensor_tensor(out=d, in0=a, in1=a, op=ALU.mult),
                      reads=[scr], writes=[wkv], tag=f"{pfx}b{b}c{ch}gnq{ts_}")
                P.add("vector", lambda e, d=rs_.ap().rearrange("p (a b) -> p a b", b=1),
                      a=wkv.h.ap()[:, ts_, :].rearrange("p (a b) -> p a b", b=64):
                      e.reduce_sum(out=d, in_=a, axis=AX.X),
                      reads=[wkv], writes=[stats], tag=f"{pfx}b{b}c{ch}gnv{ts_}")
                P.v("tensor_scalar", rs_, [rs_], scalar1=1.0 / 64, scalar2=None,
                    op0=ALU.mult, tag=f"{pfx}b{b}c{ch}gnv2{ts_}")
                rsqrt_eps(P, rs_, rs_, tag=f"{pfx}b{b}c{ch}gnr{ts_}")
                P.add("vector", lambda e, d=scr.ap().rearrange("p (a b) -> p a b", b=64),
                      a=scr.ap().rearrange("p (a b) -> p a b", b=64), m=mu3(rs_):
                      e.tensor_tensor(out=d, in0=a, in1=m, op=ALU.mult),
                      reads=[scr, stats], writes=[scr], tag=f"{pfx}b{b}c{ch}gns{ts_}")
                P.v("tensor_tensor", scr, [scr, gnw], op=ALU.mult, tag=f"{pfx}b{b}c{ch}gw{ts_}")
                P.v("tensor_tensor", scr, [scr, gnb], op=ALU.add, tag=f"{pfx}b{b}c{ch}gb{ts_}")
                P.add("vector", lambda e, d=gno.h.ap()[:, ts_, :], a=scr.ap(),
                      g2=gtk.h.ap()[:, ts_, :]:
                      e.tensor_tensor(out=d, in0=a, in1=g2, op=ALU.mult),
                      reads=[scr, gtk], writes=[gno], tag=f"{pfx}b{b}c{ch}gg{ts_}")
                for dt_ in range(8):
                    grp = dt_ // 4
                    pvw = TV(pb[7 if (2 * ts_ + grp) % 2 == 0 else 3].h,
                             (0, 128), (0, 1024), "PS", vdt=BF16)
                    transpose_128(P, com, pvw[:, (dt_ % 4) * 128:(dt_ % 4 + 1) * 128],
                                  TV(gno.h, (0, 128), (ts_ * 1024 + dt_ * 128,
                                                       ts_ * 1024 + (dt_ + 1) * 128)),
                                  tag=f"{pfx}b{b}c{ch}gt{ts_}_{dt_}")
                    if dt_ % 4 == 3:
                        dap = goT.h.ap()[:, grp * 4:(grp + 1) * 4, ts_ * 128:(ts_ + 1) * 128]
                        P.add("vector", lambda e, dap=dap, pap=pvw[:, 0:512].ap():
                              e.tensor_copy(out=dap, in_=pap),
                              reads=[pvw], writes=[goT], tag=f"{pfx}b{b}c{ch}gte{ts_}_{grp}")
            for ts_ in range(NTS):
                for hf in range(2):
                    pm = TV(pb[hf].h, (0, 128), (0, 512), "PS")
                    for dt_ in range(8):
                        P.mm(pm, TV(goT.h, (0, 128),
                                    (dt_ * CH + ts_ * 128, dt_ * CH + (ts_ + 1) * 128)),
                             wsb["wo"][:, dt_ * 1024 + hf * 512:dt_ * 1024 + (hf + 1) * 512],
                             start=(dt_ == 0), stop=(dt_ == 7), tag=f"{pfx}b{b}c{ch}o{ts_}_{hf}")
                    P.add("vector", lambda e, d=xch.h.ap()[:, ts_, hf * 512:(hf + 1) * 512],
                          pap=pm.ap():
                          e.tensor_tensor(out=d, in0=pap, in1=d, op=ALU.add),
                          reads=[pm, xch], writes=[xch], tag=f"{pfx}b{b}c{ch}oa{ts_}_{hf}")
            P.dma("gpsimd", xo_ap[:, cb:cb + NTS, :], xch,
                  out_regs=[dram_reg(xo_h, tb, tb + CH)], in_regs=[xch],
                  tag=f"{pfx}b{b}c{ch}o")



def emit_decay(P, nc, com, wxT, td1, td2, pv, wT, dT, pb, hd, t1f, zz, CH, tg):
    pm = TV(pb[6].h, (0, 64), (0, CH), "PS")
    for dt_ in range(8):
        P.mm(pm, td1[:, dt_ * 64:(dt_ + 1) * 64],
             TV(wxT.h, (0, 128), (dt_ * CH, (dt_ + 1) * CH)),
             start=(dt_ == 0), stop=(dt_ == 7), tag=tg + "d1")
    P.act(hd, pm, AF.Tanh, tag=tg + "d1t")
    for dt_ in range(8):
        pm2 = TV(pb[4 + dt_ % 2].h, (0, 128), (0, CH), "PS")
        P.mm(pm2, TV(td2.h, (0, 64), (dt_ * 128, (dt_ + 1) * 128)), hd,
             start=True, stop=True, tag=tg + f"d2_{dt_}")
        t1s = t1f[:, (dt_ % 2) * CH:(dt_ % 2 + 1) * CH]
        P.act(t1s, pm2, AF.Exp, bias=pv[:, 56 + dt_:57 + dt_], tag=tg + f"de1_{dt_}")
        for sc in range(CH // 64):
            P.add("vector", lambda e, d=dT.h.ap()[:, dt_, sc * 64:(sc + 1) * 64],
                  a=com.ones.ap()[:, 0:64], z=t1s.ap()[:, sc * 64:(sc + 1) * 64]:
                  e.tensor_tensor_scan(out=d, data0=a, data1=z, initial=0.0,
                                       op0=ALU.mult, op1=ALU.subtract),
                  reads=[t1s, com.ones], writes=[dT], tag=tg + f"sc{dt_}_{sc}")


WKV_STAGE = 99

def emit_wkv_chunk(P, nc, com, rt, ktl, vtk, wT, dT, dprev, khat, sts, masku,
                   sel01, ruk, ddg, state, stateb, wkv, pv, pb, ksc,
                   CH, NSC, NTS, tg, t1fw=None, dend=None):
    for dt_ in range(8):
        P.add("vector", lambda e, d=ruk.h.ap()[:, dt_, :], a=rt.h.ap()[:, dt_, :],
              b2=ktl.h.ap()[:, dt_, :]:
              e.tensor_tensor(out=d, in0=a, in1=b2, op=ALU.mult),
              reads=[rt, ktl], writes=[ruk], tag=tg + f"ru{dt_}")
        P.add("vector", lambda e, d=ruk.h.ap()[:, dt_, :],
              s=pv[:, 64 + dt_:65 + dt_].ap():
              e.tensor_scalar(out=d, in0=d, scalar1=s, scalar2=None, op0=ALU.mult),
              reads=[ruk, pv], writes=[ruk], tag=tg + f"rku{dt_}")
    pd = pb[4]
    for ts_ in range(NTS):
        for dt_ in range(8):
            P.mm(TV(pd.h, (0, 128), (ts_ * 16 + dt_ * 2, ts_ * 16 + dt_ * 2 + 2), "PS"),
                 TV(ruk.h, (0, 128), (dt_ * CH + ts_ * 128, dt_ * CH + (ts_ + 1) * 128)),
                 sel01, start=True, stop=True, tag=tg + f"sd{ts_}_{dt_}")
    P.v("tensor_copy", TV(ddg.h, (0, 128), (0, NTS * 16)),
        [TV(pd.h, (0, 128), (0, NTS * 16), "PS")], tag=tg + "sde")
    if WKV_STAGE < 2:
        P.v("memset", wkv, [0.0], tag=tg + "wz", engine="gpsimd")
        return
    for dt_ in range(8):
        dpv = dprev[:, (dt_ % 2) * CH:(dt_ % 2 + 1) * CH]
        t1v = t1fw[:, (dt_ % 2) * CH:(dt_ % 2 + 1) * CH]
        P.add("vector", lambda e, d=dpv.ap()[:, 1:CH], s=dT.h.ap()[:, dt_, 0:CH - 1]:
              e.tensor_copy(out=d, in_=s), reads=[dT], writes=[dpv], tag=tg + f"dp{dt_}")
        P.add("vector", lambda e,
              d=dpv.ap().rearrange("p (s i) -> p s i", s=NSC)[:, :, 0:1]:
              e.memset(d, 0.0), reads=[], writes=[dpv], tag=tg + f"dp1{dt_}")
        P.add("scalar", lambda e, d=t1v.ap(), a=dpv.ap():
              e.activation(out=d, in_=a, func=AF.Exp),
              reads=[dpv], writes=[t1v], tag=tg + f"dpe{dt_}")
        P.add("vector", lambda e, d=rt.h.ap()[:, dt_, :], b2=t1v.ap():
              e.tensor_tensor(out=d, in0=d, in1=b2, op=ALU.mult),
              reads=[rt, t1v], writes=[rt], tag=tg + f"rs{dt_}")
        P.add("scalar", lambda e, d=wT.h.ap()[:, dt_, :], a=dT.h.ap()[:, dt_, :]:
              e.activation(out=d, in_=a, func=AF.Exp, scale=np.float32(-1.0)),
              reads=[dT], writes=[wT], tag=tg + f"rd{dt_}")
        P.add("vector", lambda e, d=ktl.h.ap()[:, dt_, :], b2=wT.h.ap()[:, dt_, :]:
              e.tensor_tensor(out=d, in0=d, in1=b2, op=ALU.mult),
              reads=[ktl, wT], writes=[ktl], tag=tg + f"ks{dt_}")
        P.add("scalar", lambda e,
              d=dend.ap()[:, dt_ * 4:dt_ * 4 + NSC].unsqueeze(1),
              a=dT.h.ap().rearrange("p a (s i) -> p a s i", i=64)[:, dt_:dt_ + 1, :, 63:64].squeeze(3):
              e.activation(out=d, in_=a, func=AF.Exp),
              reads=[dT], writes=[dend], tag=tg + f"dd{dt_}")
    if WKV_STAGE < 3:
        P.v("memset", wkv, [0.0], tag=tg + "wz", engine="gpsimd")
        return
    def s_bank(h):
        return (2 if h % 2 else 0) + (1 if h >= 8 else 0)
    def s_slot(h):
        return (h // 2) % 4
    for h in range(16):
        bank = pb[s_bank(h)]
        for sc in range(NSC):
            P.mm(TV(bank.h, ((sc % 2) * 64, (sc % 2) * 64 + 64),
                    (s_slot(h) * 128 + (sc // 2) * 64,
                     s_slot(h) * 128 + (sc // 2) * 64 + 64), "PS"),
                 TV(ktl.h, ((h % 2) * 64, (h % 2) * 64 + 64),
                    ((h // 2) * CH + sc * 64, (h // 2) * CH + (sc + 1) * 64)),
                 TV(rt.h, ((h % 2) * 64, (h % 2) * 64 + 64),
                    ((h // 2) * CH + sc * 64, (h // 2) * CH + (sc + 1) * 64)),
                 start=True, stop=True, tag=tg + f"S{h}_{sc}")
    for g in range(4):
        P.add("vector", lambda e, d=sts.h.ap()[:, g, :], pap=pb[g].ap(), m=masku.ap():
              e.tensor_tensor(out=d, in0=pap, in1=m, op=ALU.mult),
              reads=[pb[g], masku], writes=[sts], tag=tg + f"Se{g}")
    if WKV_STAGE < 4:
        P.v("memset", wkv, [0.0], tag=tg + "wz", engine="gpsimd")
        return
    for dt_ in range(8):
        for pr in range(NTS):
            kv = ksc[:, (dt_ % 2) * 128:(dt_ % 2 + 1) * 128]
            for half in range(2):
                sc = pr * 2 + half
                P.add("vector", lambda e, d=kv.ap()[:, half * 64:(half + 1) * 64],
                      a=ktl.h.ap()[:, dt_, sc * 64:(sc + 1) * 64],
                      s=dend.ap()[:, dt_ * 4 + sc:dt_ * 4 + sc + 1]:
                      e.tensor_scalar(out=d, in0=a, scalar1=s, scalar2=None, op0=ALU.mult),
                      reads=[ktl, dend], writes=[kv], tag=tg + f"kh{dt_}_{sc}")
            pvw = TV(pb[7 if dt_ % 2 == 0 else 3].h, (0, 128), (0, 256), "PS", vdt=BF16)
            transpose_128(P, com, pvw[:, 0:128], kv, tag=tg + f"kt{dt_}_{pr}")
            P.add("vector", lambda e, d=khat.h.ap()[:, dt_, pr * 128:(pr + 1) * 128],
                  pap=pvw[:, 0:128].ap():
                  e.tensor_copy(out=d, in_=pap), reads=[pvw], writes=[khat],
                  tag=tg + f"kte{dt_}_{pr}")
    if WKV_STAGE < 5:
        P.v("memset", wkv, [0.0], tag=tg + "wz", engine="gpsimd")
        return
    for pr in range(NTS):
        for sc in (pr * 2, pr * 2 + 1):
            rb = (sc % 2) * 64
            for h in range(16):
                pstt = pb[0] if h % 2 == 0 else pb[2]
                P.mm(TV(pstt.h, (rb, rb + 64), ((h // 2) * 64, (h // 2) * 64 + 64), "PS"),
                     TV(rt.h, ((h % 2) * 64, (h % 2) * 64 + 64),
                        ((h // 2) * CH + sc * 64, (h // 2) * CH + (sc + 1) * 64)),
                     TV(stateb.h, ((h % 2) * 64, (h % 2) * 64 + 64),
                        ((h // 2) * 64, (h // 2) * 64 + 64)),
                     start=True, stop=True, tag=tg + f"o2_{h}_{sc}")
            for h in range(16):
                po = pb[5] if h < 8 else pb[6]
                P.mm(TV(po.h, (rb, rb + 64), ((h % 8) * 64, (h % 8) * 64 + 64), "PS"),
                     TV(sts.h, (rb, rb + 64),
                        (s_bank(h) * 512 + s_slot(h) * 128 + (sc // 2) * 64,
                         s_bank(h) * 512 + s_slot(h) * 128 + (sc // 2) * 64 + 64)),
                     TV(vtk.h, (rb, rb + 64), ((sc // 2) * 1024 + h * 64,
                                               (sc // 2) * 1024 + (h + 1) * 64)),
                     start=True, stop=True, tag=tg + f"o1_{h}_{sc}")
            for h in range(16):
                P.mm(TV(pd.h, ((h % 2) * 64, (h % 2) * 64 + 64),
                        ((h // 2) * 64, (h // 2) * 64 + 64), "PS"),
                     TV(khat.h, (rb, rb + 64),
                        ((h // 2) * 256 + pr * 128 + (h % 2) * 64,
                         (h // 2) * 256 + pr * 128 + (h % 2) * 64 + 64)),
                     TV(vtk.h, (rb, rb + 64), ((sc // 2) * 1024 + h * 64,
                                               (sc // 2) * 1024 + (h + 1) * 64)),
                     start=True, stop=True, tag=tg + f"su{h}_{sc}")
            for dt_ in range(8):
                P.add("vector", lambda e, d=state.h.ap()[:, dt_, :],
                      s=dend.ap()[:, dt_ * 4 + sc:dt_ * 4 + sc + 1],
                      pap=pd.ap()[:, dt_ * 64:(dt_ + 1) * 64]:
                      e.scalar_tensor_tensor(out=d, in0=d, scalar=s, in1=pap,
                                             op0=ALU.mult, op1=ALU.add),
                      reads=[state, dend, pd], writes=[state], tag=tg + f"stu{dt_}_{sc}")
            P.v("tensor_copy", stateb, [state], tag=tg + f"stb{sc}")
        stgE = TV(wT.h, (0, 128), (0, 512))
        stgO = TV(wT.h, (0, 128), (512, 1024))
        P.v("tensor_copy", stgE, [pb[0]], tag=tg + f"pse{pr}")
        P.v("tensor_copy", stgO, [pb[2]], tag=tg + f"pso{pr}")
        for h in range(16):
            po = pb[5] if h < 8 else pb[6]
            stg = stgE if h % 2 == 0 else stgO
            P.add("vector", lambda e,
                  d=wkv.h.ap()[:, pr, h * 64:(h + 1) * 64],
                  a=vtk.h.ap()[:, pr, h * 64:(h + 1) * 64],
                  s=ddg.h.ap()[:, pr, h:h + 1],
                  pap=po.ap()[:, (h % 8) * 64:(h % 8) * 64 + 64]:
                  e.scalar_tensor_tensor(out=d, in0=a, scalar=s, in1=pap,
                                         op0=ALU.mult, op1=ALU.add),
                  reads=[vtk, ddg, po], writes=[wkv], tag=tg + f"we{h}_{pr}")
            P.add("vector", lambda e,
                  d=wkv.h.ap()[:, pr, h * 64:(h + 1) * 64],
                  a=wT.h.ap().rearrange("p a c -> p (a c)")[:, (0 if h % 2 == 0 else 512) + (h // 2) * 64:
                                                            (0 if h % 2 == 0 else 512) + (h // 2) * 64 + 64]:
                  e.tensor_tensor(out=d, in0=d, in1=a, op=ALU.add),
                  reads=[wkv, TV(wT.h, (0, 128), (0, 1024))], writes=[wkv],
                  tag=tg + f"ws{h}_{pr}")


def emit_chanmix(P, nc, com, xi_h, pvec_h, cwk_h, cwv_h, cwr_h, pfx="c",
                 xo_h=None):
    CH = 256
    TBANK2 = 6
    NTS = CH // 128
    cwk = TV(nc.alloc_sbuf_tensor(pfx + "cwk", [128, 8, 3072], BF16))
    cwv = TV(nc.alloc_sbuf_tensor(pfx + "cwv", [128, 24, 1024], BF16))
    cwr = TV(nc.alloc_sbuf_tensor(pfx + "cwr", [128, 8, 1024], BF16))
    P.dma("gpsimd", cwk, cwk_h.ap().rearrange("(a p) f -> p a f", p=128),
          in_regs=[dram_reg(cwk_h)], tag=pfx + "wk")
    P.dma("gpsimd", cwv, cwv_h.ap().rearrange("(a p) f -> p a f", p=128),
          in_regs=[dram_reg(cwv_h)], tag=pfx + "wv")
    P.dma("gpsimd", cwr, cwr_h.ap().rearrange("(a p) f -> p a f", p=128),
          in_regs=[dram_reg(cwr_h)], tag=pfx + "wr")
    pv = TV(nc.alloc_sbuf_tensor(pfx + "pv", [128, 24], F32))
    P.dma("sync", pv, pvec_h.ap(), in_regs=[dram_reg(pvec_h)], tag=pfx + "pv")
    ext = TV(nc.alloc_sbuf_tensor(pfx + "ext", [128, 8, CH + 1], BF16))
    sxT = TV(nc.alloc_sbuf_tensor(pfx + "sxT", [128, 8, CH], BF16))
    kx = TV(nc.alloc_sbuf_tensor(pfx + "kx", [128, 8, CH], BF16))
    rx = TV(nc.alloc_sbuf_tensor(pfx + "rx", [128, 8, CH], BF16))
    k2 = TV(nc.alloc_sbuf_tensor(pfx + "k2", [128, 24, CH], BF16))
    rsg = TV(nc.alloc_sbuf_tensor(pfx + "rsg", [128, NTS, 1024], BF16))
    xch = TV(nc.alloc_sbuf_tensor(pfx + "xch", [128, NTS, 1024], F32))
    xlt = TV(nc.alloc_sbuf_tensor(pfx + "xlt", [128, 1024], BF16))
    scr = TV(nc.alloc_sbuf_tensor(pfx + "scr", [128, 1024], F32))
    stats = TV(nc.alloc_sbuf_tensor(pfx + "stats", [128, 8], F32))
    pb = [ps(nc, pfx + f"pb{i}", 128, 512, F32) for i in range(8)]
    xi_ap = xi_h.ap().rearrange("(c p) d -> p c d", p=128)
    xo_h = xi_h if xo_h is None else xo_h
    xo_ap = xo_h.ap().rearrange("(c p) d -> p c d", p=128)

    for b in range(B_LOC):
        for ch in range(2048 // CH):
            tb = b * 2048 + ch * CH
            cb = tb // 128
            P.dma("sync", xch, xi_ap[:, cb:cb + NTS, :],
                  in_regs=[dram_reg(xi_h, tb, tb + CH)], tag=f"{pfx}b{b}c{ch}x")
            if ch == 0:
                P.v("memset", TV(ext.h, (0, 128), (0, 8 * (CH + 1))), [0.0],
                    tag=f"{pfx}b{b}c{ch}z", engine="gpsimd")
            else:
                eap = ext.h.ap()
                P.add("vector", lambda e, d=eap[:, :, 0:1], s=eap[:, :, CH:CH + 1]:
                      e.tensor_copy(out=d, in_=s),
                      reads=[ext], writes=[TV(ext.h, (0, 128), (0, 8 * (CH + 1)))],
                      tag=f"{pfx}b{b}c{ch}bd")
            for c in range(NTS):
                x_t = xch[:, c * 1024:(c + 1) * 1024]
                mu = stats[:, 0:1]; rs_ = stats[:, 1:2]
                ln_stats(P, x_t, mu, rs_, scr, 1.0 / 1024, tag=f"{pfx}b{b}c{ch}l{c}")
                P.v("tensor_scalar", xlt, [x_t], scalar1=mu, scalar2=rs_,
                    op0=ALU.subtract, op1=ALU.mult, tag=f"{pfx}b{b}c{ch}n{c}")
                for dt_ in range(8):
                    grp = dt_ // 4
                    pvw = TV(pb[7 if (2 * c + grp) % 2 == 0 else TBANK2].h,
                             (0, 128), (0, 1024), "PS", vdt=BF16)
                    transpose_128(P, com, pvw[:, (dt_ % 4) * 128:(dt_ % 4 + 1) * 128],
                                  TV(xlt.h, (0, 128), (dt_ * 128, (dt_ + 1) * 128)),
                                  tag=f"{pfx}b{b}c{ch}t{c}_{dt_}")
                    if dt_ % 4 == 3:
                        dap = ext.h.ap()[:, grp * 4:(grp + 1) * 4,
                                         1 + c * 128:1 + (c + 1) * 128]
                        P.add("vector", lambda e, dap=dap, pap=pvw[:, 0:512].ap():
                              e.tensor_copy(out=dap, in_=pap),
                              reads=[pvw], writes=[TV(ext.h, (0, 128), (0, 8 * (CH + 1)))],
                              tag=f"{pfx}b{b}c{ch}te{c}_{grp}")
            for dt_ in range(8):
                P.add("vector", lambda e,
                      d=ext.h.ap()[:, dt_, 1:CH + 1], s=pv[:, dt_:dt_ + 1].ap():
                      e.tensor_scalar(out=d, in0=d, scalar1=s, scalar2=None, op0=ALU.mult),
                      reads=[ext, pv], writes=[TV(ext.h, (0, 128), (0, 8 * (CH + 1)))],
                      tag=f"{pfx}b{b}c{ch}lw{dt_}")
            P.add("vector", lambda e, d=sxT.ap(),
                  a=ext.h.ap()[:, :, 0:CH], b2=ext.h.ap()[:, :, 1:CH + 1]:
                  e.tensor_tensor(out=d, in0=a, in1=b2, op=ALU.subtract),
                  reads=[ext], writes=[sxT], tag=f"{pfx}b{b}c{ch}sx")
            for (buf, colb) in ((kx, 8), (rx, 16)):
                for dt_ in range(8):
                    P.add("vector", lambda e, d=buf.h.ap()[:, dt_, :],
                          a=sxT.h.ap()[:, dt_, :], s=pv[:, colb + dt_:colb + dt_ + 1].ap(),
                          x2=ext.h.ap()[:, dt_, 1:CH + 1]:
                          e.scalar_tensor_tensor(out=d, in0=a, scalar=s, in1=x2,
                                                 op0=ALU.mult, op1=ALU.add),
                          reads=[sxT, pv, ext], writes=[buf],
                          tag=f"{pfx}b{b}c{ch}mx{colb}_{dt_}")
            for od in range(24):
                pm = TV(pb[od % 2].h, (0, 128), (0, CH), "PS")
                for dt_ in range(8):
                    P.mm(pm, cwk[:, dt_ * 3072 + od * 128:dt_ * 3072 + (od + 1) * 128],
                         TV(kx.h, (0, 128), (dt_ * CH, (dt_ + 1) * CH)),
                         start=(dt_ == 0), stop=(dt_ == 7), tag=f"{pfx}b{b}c{ch}k{od}")
                P.act(TV(k2.h, (0, 128), (od * CH, (od + 1) * CH)), pm, AF.Relu,
                      tag=f"{pfx}b{b}c{ch}kr{od}")
                P.add("vector", lambda e, d=k2.h.ap()[:, od, :]:
                      e.tensor_tensor(out=d, in0=d, in1=d, op=ALU.mult),
                      reads=[k2], writes=[k2], tag=f"{pfx}b{b}c{ch}kq{od}")
            for ts_ in range(NTS):
                for hf in range(2):
                    pm = TV(pb[2 + (2 * ts_ + hf) % 2].h, (0, 128), (0, 512), "PS")
                    for dt_ in range(8):
                        P.mm(pm, TV(rx.h, (0, 128),
                                    (dt_ * CH + ts_ * 128, dt_ * CH + (ts_ + 1) * 128)),
                             cwr[:, dt_ * 1024 + hf * 512:dt_ * 1024 + (hf + 1) * 512],
                             start=(dt_ == 0), stop=(dt_ == 7), tag=f"{pfx}b{b}c{ch}r{ts_}_{hf}")
                    P.act(TV(rsg.h, (0, 128), (ts_ * 1024 + hf * 512, ts_ * 1024 + (hf + 1) * 512)),
                          pm, AF.Tanh, scale=0.5, tag=f"{pfx}b{b}c{ch}rt{ts_}_{hf}")
                    P.add("vector", lambda e,
                          d=rsg.h.ap()[:, ts_, hf * 512:(hf + 1) * 512]:
                          e.tensor_scalar(out=d, in0=d, scalar1=0.5, scalar2=0.5,
                                          op0=ALU.mult, op1=ALU.add),
                          reads=[rsg], writes=[rsg], tag=f"{pfx}b{b}c{ch}rs{ts_}_{hf}")
            for ts_ in range(NTS):
                for hf in range(2):
                    pm = TV(pb[4 + (2 * ts_ + hf) % 2].h, (0, 128), (0, 512), "PS")
                    for od in range(24):
                        P.mm(pm, TV(k2.h, (0, 128),
                                    (od * CH + ts_ * 128, od * CH + (ts_ + 1) * 128)),
                             cwv[:, od * 1024 + hf * 512:od * 1024 + (hf + 1) * 512],
                             start=(od == 0), stop=(od == 23), tag=f"{pfx}b{b}c{ch}y{ts_}_{hf}")
                    P.add("vector", lambda e,
                          d=xch.h.ap()[:, ts_, hf * 512:(hf + 1) * 512],
                          r_=rsg.h.ap()[:, ts_, hf * 512:(hf + 1) * 512], pap=pm.ap():
                          e.scalar_tensor_tensor(out=d, in0=pap, scalar=1.0, in1=d,
                                                 op0=ALU.mult, op1=ALU.add)
                          if False else
                          e.tensor_tensor(out=d, in0=pap, in1=r_, op=ALU.mult),
                          reads=[pm, rsg, xch], writes=[xch],
                          tag=f"{pfx}b{b}c{ch}ye{ts_}_{hf}") if False else None
                    P.add("vector", lambda e, d=scr.ap()[:, hf * 512:(hf + 1) * 512],
                          r_=rsg.h.ap()[:, ts_, hf * 512:(hf + 1) * 512], pap=pm.ap():
                          e.tensor_tensor(out=d, in0=pap, in1=r_, op=ALU.mult),
                          reads=[pm, rsg], writes=[scr], tag=f"{pfx}b{b}c{ch}ym{ts_}_{hf}")
                    P.add("vector", lambda e, d=xch.h.ap()[:, ts_, hf * 512:(hf + 1) * 512],
                          a=scr.ap()[:, hf * 512:(hf + 1) * 512]:
                          e.tensor_tensor(out=d, in0=d, in1=a, op=ALU.add),
                          reads=[scr, xch], writes=[xch], tag=f"{pfx}b{b}c{ch}ya{ts_}_{hf}")
            P.dma("gpsimd", xo_ap[:, cb:cb + NTS, :], xch,
                  out_regs=[dram_reg(xo_h, tb, tb + CH)], in_regs=[xch],
                  tag=f"{pfx}b{b}c{ch}o")



def _pd_layout(v):
    return np.ascontiguousarray(np.asarray(v, np.float32).reshape(8, 128).T)


def _masku_host():
    p = np.arange(128)[:, None] % 64
    c = np.arange(512)[None, :] % 64
    return (c > p).astype(np.float32)


def _build_launch(emit_fn, specs, outs):
    nc = bass.Bass("TRN2", target_bir_lowering=False)
    hs = {}
    for (n, shape) in specs:
        hs[n] = nc.declare_dram_parameter(n, list(shape), F32, isOutput=(n in outs))
    import contextlib
    ctx = contextlib.ExitStack()
    sems = make_sems(nc, ctx)
    P = Prog(nc)
    com = Common(nc, P)
    emit_fn(nc, P, com, hs)
    P.finalize(sems)
    with nc.Block() as block:
        P.emit_all(block)
    ctx.close()
    return nc


_launch_cache = {}


def _run(tag, emit_fn, specs, outs, in_maps):
    nc = _launch_cache.get(tag)
    if nc is None:
        nc = _build_launch(emit_fn, specs, outs)
        _launch_cache[tag] = nc
    res = run_bass_kernel_spmd(nc, in_maps, core_ids=list(range(N_CORES)))
    return res.results


def kernel(**inp):
    if FUSED:
        return kernel_fused(**inp)
    f32 = lambda a: np.ascontiguousarray(np.asarray(a, np.float32))
    x = f32(inp["L0_attn_x"]).reshape(B, S, D)
    shards = [np.ascontiguousarray(x[2 * c:2 * c + 2].reshape(T_LOC, D))
              for c in range(N_CORES)]
    TD = (T_LOC, D)

    aw = {n: f32(inp["L0_attn_" + n]) for n in
          ("W_dq", "W_uq", "W_dkv", "W_uk", "W_uv", "W_o")}
    specs = [("resid", TD), ("wdq", (1024, 256)), ("wuq", (256, 1024)),
             ("wdkv", (1024, 256)), ("wuk", (256, 1024)), ("wuv", (256, 1024)),
             ("wo", (1024, 1024)), ("out", TD)]
    maps = [dict(resid=shards[c], wdq=aw["W_dq"], wuq=aw["W_uq"],
                 wdkv=aw["W_dkv"], wuk=aw["W_uk"], wuv=aw["W_uv"],
                 wo=aw["W_o"]) for c in range(N_CORES)]
    r = _run("attn", lambda nc, P, com, h: emit_attn(
        P, nc, com, h["resid"], h["out"], h["wdq"], h["wuq"], h["wdkv"],
        h["wuk"], h["wuv"], h["wo"]), specs, {"out"}, maps)
    resid = [r[c]["out"] for c in range(N_CORES)]

    w1e = f32(inp["L0_ffn_w1_e"]); w3e = f32(inp["L0_ffn_w3_e"])
    w2e = f32(inp["L0_ffn_w2_e"])
    w1s = f32(inp["L0_ffn_w1_s"]); w3s = f32(inp["L0_ffn_w3_s"])
    w2s = f32(inp["L0_ffn_w2_s"])
    w1te = np.ascontiguousarray(np.concatenate(
        [w1e.transpose(0, 2, 1), w1s.T.reshape(1024, 2, 512).transpose(1, 0, 2)], 0))
    w3te = np.ascontiguousarray(np.concatenate(
        [w3e.transpose(0, 2, 1), w3s.T.reshape(1024, 2, 512).transpose(1, 0, 2)], 0))
    w2te = np.ascontiguousarray(np.concatenate(
        [w2e.transpose(0, 2, 1), w2s.T.reshape(2, 512, 1024)], 0))
    gt = np.ascontiguousarray(f32(inp["L0_ffn_w_gate"]).T)
    specs = [("resid", TD), ("gate", (1024, 16)), ("w1", (18, 1024, 512)),
             ("w3", (18, 1024, 512)), ("w2", (18, 512, 1024)), ("out", TD)]
    maps = [dict(resid=resid[c], gate=gt, w1=w1te, w3=w3te, w2=w2te)
            for c in range(N_CORES)]
    r = _run("moe", lambda nc, P, com, h: emit_moe(
        P, nc, com, h["resid"], h["out"], h["gate"], h["w1"], h["w3"], h["w2"]),
        specs, {"out"}, maps)
    resid = [r[c]["out"] for c in range(N_CORES)]

    specs = [("src", TD), ("dst", TD)]
    maps = [dict(src=resid[c]) for c in range(N_CORES)]
    r = _run("pre", lambda nc, P, com, h: emit_ln_stream(
        P, nc, com, h["src"], h["dst"], "rq"), specs, {"dst"}, maps)
    xi = [r[c]["dst"] for c in range(N_CORES)]

    for L in range(2):
        g = lambda n: f32(inp["L1_attn_" + n][L])
        pvec80 = np.concatenate(
            [_pd_layout(g("ln1_w")), _pd_layout(g("x_maa")), _pd_layout(g("w_maa")),
             _pd_layout(g("k_maa")), _pd_layout(g("v_maa")), _pd_layout(g("r_maa")),
             _pd_layout(g("g_maa")), _pd_layout(g("time_decay").reshape(-1)),
             _pd_layout(g("time_first").reshape(-1)),
             np.zeros((128, 8), np.float32)], 1)
        specs = [("xi", TD), ("pvec", (128, 80)), ("tm1", (1024, 160)),
                 ("tm2", (32, 5, 1024)), ("td1", (1024, 64)), ("td2", (64, 1024)),
                 ("wr", (1024, 1024)), ("wk", (1024, 1024)), ("wv", (1024, 1024)),
                 ("wg", (1024, 1024)), ("wo", (1024, 1024)), ("gnw", (1024,)),
                 ("gnb", (1024,)), ("mku", (128, 512)), ("xo", TD)]
        maps = [dict(xi=xi[c], pvec=pvec80,
                     tm1=g("tm_w1"),
                     tm2=np.ascontiguousarray(g("tm_w2").transpose(1, 0, 2)),
                     td1=g("td_w1"), td2=g("td_w2"), wr=g("Wr"), wk=g("Wk"),
                     wv=g("Wv"), wg=g("Wg"), wo=g("Wo"), gnw=g("gn_w"),
                     gnb=g("gn_b"), mku=_masku_host()) for c in range(N_CORES)]
        r = _run(f"tm{L}", lambda nc, P, com, h: emit_timemix(
            P, nc, com, h["xi"], h["pvec"], h["tm1"], h["tm2"], h["td1"],
            h["td2"], h["wr"], h["wk"], h["wv"], h["wg"], h["wo"], h["gnw"],
            h["gnb"], h["mku"], xo_h=h["xo"]), specs, {"xo"}, maps)
        xi = [r[c]["xo"] for c in range(N_CORES)]
        pvec24 = np.concatenate([_pd_layout(g("ln2_w")), _pd_layout(g("ck_maa")),
                                 _pd_layout(g("cr_maa"))], 1)
        specs = [("xi", TD), ("pvec", (128, 24)), ("cwk", (1024, 3072)),
                 ("cwv", (3072, 1024)), ("cwr", (1024, 1024)), ("xo", TD)]
        maps = [dict(xi=xi[c], pvec=pvec24, cwk=g("cWk"), cwv=g("cWv"),
                     cwr=g("cWr")) for c in range(N_CORES)]
        r = _run(f"cm{L}", lambda nc, P, com, h: emit_chanmix(
            P, nc, com, h["xi"], h["pvec"], h["cwk"], h["cwv"], h["cwr"],
            xo_h=h["xo"]), specs, {"xo"}, maps)
        xi = [r[c]["xo"] for c in range(N_CORES)]

    specs = [("xi", TD), ("residb", TD), ("lnout", (1024,)), ("out", TD)]
    maps = [dict(xi=xi[c], residb=resid[c], lnout=f32(inp["L1_attn_ln_out_w"]))
            for c in range(N_CORES)]
    r = _run("post", lambda nc, P, com, h: emit_rwkv_post(
        P, nc, com, h["xi"], h["residb"], h["out"], h["lnout"]),
        specs, {"out"}, maps)
    resid = [r[c]["out"] for c in range(N_CORES)]

    b1r = np.ascontiguousarray(f32(inp["L1_ffn_b1"]).reshape(32, 128).T)
    specs = [("resid", TD), ("w1t", (1024, 4096)), ("b1", (128, 32)),
             ("w2t", (4096, 1024)), ("b2", (1024,)), ("out", TD)]
    maps = [dict(resid=resid[c], w1t=np.ascontiguousarray(f32(inp["L1_ffn_W1"]).T),
                 b1=b1r, w2t=np.ascontiguousarray(f32(inp["L1_ffn_W2"]).T),
                 b2=f32(inp["L1_ffn_b2"])) for c in range(N_CORES)]
    r = _run("ffn", lambda nc, P, com, h: emit_l1ffn(
        P, nc, com, h["resid"], h["out"], h["w1t"], h["b1"], h["w2t"], h["b2"]),
        specs, {"out"}, maps)
    out = np.concatenate([np.asarray(r[c]["out"]) for c in range(N_CORES)], 0)
    return np.ascontiguousarray(out.reshape(B, S, D).astype(np.float32))



FUSED = True


def _build_fused():
    nc = bass.Bass("TRN2", target_bir_lowering=False)
    TD = [T_LOC, D]
    E = lambda n, sh: nc.declare_dram_parameter(n, list(sh), F32, isOutput=False)
    h = {}
    h["x"] = E("x", TD)
    for n, sh in (("wdq", (1024, 256)), ("wuq", (256, 1024)), ("wdkv", (1024, 256)),
                  ("wuk", (256, 1024)), ("wuv", (256, 1024)), ("wo0", (1024, 1024)),
                  ("gate", (1024, 16)), ("mw1", (18, 1024, 512)),
                  ("mw3", (18, 1024, 512)), ("mw2", (18, 512, 1024)),
                  ("lnout", (1024,)), ("fw1t", (1024, 4096)), ("fb1", (128, 32)),
                  ("fw2t", (4096, 1024)), ("fb2", (1024,)), ("mku", (128, 512))):
        h[n] = E(n, sh)
    for L in range(2):
        for n, sh in (("pvec", (128, 80)), ("tm1", (1024, 160)), ("tm2", (32, 5, 1024)),
                      ("td1", (1024, 64)), ("td2", (64, 1024)), ("wr", (1024, 1024)),
                      ("wk", (1024, 1024)), ("wv", (1024, 1024)), ("wg", (1024, 1024)),
                      ("wo", (1024, 1024)), ("gnw", (1024,)), ("gnb", (1024,)),
                      ("cpvec", (128, 24)), ("cwk", (1024, 3072)),
                      ("cwv", (3072, 1024)), ("cwr", (1024, 1024))):
            h[f"L{L}{n}"] = E(f"L{L}{n}", sh)
    out = nc.declare_dram_parameter("out", TD, F32, isOutput=True)
    resid = nc.dram_tensor("residI", TD, F32, kind="Internal")
    residb = nc.dram_tensor("residbI", TD, F32, kind="Internal")
    xi = nc.dram_tensor("xiI", TD, F32, kind="Internal")

    import contextlib
    ctx = contextlib.ExitStack()
    sems = make_sems(nc, ctx)
    P = Prog(nc)
    com = Common(nc, P)
    sb_base = nc.sbuf_base
    psb = nc.psum_base

    def phase(emit):
        nonlocal sb_base
        nc.sbuf_base = sb_base
        nc.psum_base = psb
        emit()
        P.barrier()

    phase(lambda: emit_attn(P, nc, com, h["x"], resid, h["wdq"], h["wuq"],
                            h["wdkv"], h["wuk"], h["wuv"], h["wo0"], pfx="Pa"))
    phase(lambda: emit_moe(P, nc, com, resid, residb, h["gate"], h["mw1"],
                           h["mw3"], h["mw2"], pfx="Pm"))
    phase(lambda: emit_ln_stream(P, nc, com, residb, xi, "Pq"))
    for L in range(2):
        phase(lambda L=L: emit_timemix(
            P, nc, com, xi, h[f"L{L}pvec"], h[f"L{L}tm1"], h[f"L{L}tm2"],
            h[f"L{L}td1"], h[f"L{L}td2"], h[f"L{L}wr"], h[f"L{L}wk"],
            h[f"L{L}wv"], h[f"L{L}wg"], h[f"L{L}wo"], h[f"L{L}gnw"],
            h[f"L{L}gnb"], h["mku"], pfx=f"Pt{L}"))
        phase(lambda L=L: emit_chanmix(
            P, nc, com, xi, h[f"L{L}cpvec"], h[f"L{L}cwk"], h[f"L{L}cwv"],
            h[f"L{L}cwr"], pfx=f"Pc{L}"))
    phase(lambda: emit_rwkv_post(P, nc, com, xi, residb, resid, h["lnout"], pfx="Pp"))
    phase(lambda: emit_l1ffn(P, nc, com, resid, out, h["fw1t"], h["fb1"],
                             h["fw2t"], h["fb2"], pfx="Pf"))
    P.finalize(sems)
    with nc.Block() as block:
        P.emit_all(block)
    ctx.close()
    return nc


def _fused_in_maps(inp, shards):
    f32 = lambda a: np.ascontiguousarray(np.asarray(a, np.float32))
    aw = {n: f32(inp["L0_attn_" + n]) for n in
          ("W_dq", "W_uq", "W_dkv", "W_uk", "W_uv", "W_o")}
    w1e = f32(inp["L0_ffn_w1_e"]); w3e = f32(inp["L0_ffn_w3_e"])
    w2e = f32(inp["L0_ffn_w2_e"])
    w1s = f32(inp["L0_ffn_w1_s"]); w3s = f32(inp["L0_ffn_w3_s"])
    w2s = f32(inp["L0_ffn_w2_s"])
    base = dict(
        wdq=aw["W_dq"], wuq=aw["W_uq"], wdkv=aw["W_dkv"], wuk=aw["W_uk"],
        wuv=aw["W_uv"], wo0=aw["W_o"],
        gate=np.ascontiguousarray(f32(inp["L0_ffn_w_gate"]).T),
        mw1=np.ascontiguousarray(np.concatenate(
            [w1e.transpose(0, 2, 1), w1s.T.reshape(1024, 2, 512).transpose(1, 0, 2)], 0)),
        mw3=np.ascontiguousarray(np.concatenate(
            [w3e.transpose(0, 2, 1), w3s.T.reshape(1024, 2, 512).transpose(1, 0, 2)], 0)),
        mw2=np.ascontiguousarray(np.concatenate(
            [w2e.transpose(0, 2, 1), w2s.T.reshape(2, 512, 1024)], 0)),
        lnout=f32(inp["L1_attn_ln_out_w"]),
        fw1t=np.ascontiguousarray(f32(inp["L1_ffn_W1"]).T),
        fb1=np.ascontiguousarray(f32(inp["L1_ffn_b1"]).reshape(32, 128).T),
        fw2t=np.ascontiguousarray(f32(inp["L1_ffn_W2"]).T),
        fb2=f32(inp["L1_ffn_b2"]),
        mku=_masku_host(),
    )
    for L in range(2):
        g = lambda n: f32(inp["L1_attn_" + n][L])
        base[f"L{L}pvec"] = np.concatenate(
            [_pd_layout(g("ln1_w")), _pd_layout(g("x_maa")), _pd_layout(g("w_maa")),
             _pd_layout(g("k_maa")), _pd_layout(g("v_maa")), _pd_layout(g("r_maa")),
             _pd_layout(g("g_maa")), _pd_layout(g("time_decay").reshape(-1)),
             _pd_layout(g("time_first").reshape(-1)),
             np.zeros((128, 8), np.float32)], 1)
        base[f"L{L}tm1"] = g("tm_w1")
        base[f"L{L}tm2"] = np.ascontiguousarray(g("tm_w2").transpose(1, 0, 2))
        base[f"L{L}td1"] = g("td_w1"); base[f"L{L}td2"] = g("td_w2")
        for n in ("Wr", "Wk", "Wv", "Wg", "Wo"):
            base[f"L{L}{'w' + n[1].lower()}"] = g(n)
        base[f"L{L}gnw"] = g("gn_w"); base[f"L{L}gnb"] = g("gn_b")
        base[f"L{L}cpvec"] = np.concatenate(
            [_pd_layout(g("ln2_w")), _pd_layout(g("ck_maa")), _pd_layout(g("cr_maa"))], 1)
        base[f"L{L}cwk"] = g("cWk"); base[f"L{L}cwv"] = g("cWv")
        base[f"L{L}cwr"] = g("cWr")
    return [dict(base, x=shards[c]) for c in range(N_CORES)]


def kernel_fused(**inp):
    f32 = lambda a: np.ascontiguousarray(np.asarray(a, np.float32))
    x = f32(inp["L0_attn_x"]).reshape(B, S, D)
    shards = [np.ascontiguousarray(x[2 * c:2 * c + 2].reshape(T_LOC, D))
              for c in range(N_CORES)]
    nc = _launch_cache.get("fused")
    if nc is None:
        nc = _build_fused()
        _launch_cache["fused"] = nc
    maps = _fused_in_maps(inp, shards)
    res = run_bass_kernel_spmd(nc, maps, core_ids=list(range(N_CORES)))
    out = np.concatenate([np.asarray(res.results[c]["out"]) for c in range(N_CORES)], 0)
    return np.ascontiguousarray(out.reshape(B, S, D).astype(np.float32))


# version 06
# speedup vs baseline: 1.1815x; 1.0165x over previous
import math
import numpy as np
import concourse.bass as bass
import concourse.mybir as mybir
from concourse.bass_utils import run_bass_kernel_spmd

F32 = mybir.dt.float32
BF16 = mybir.dt.bfloat16
I32 = mybir.dt.int32
AF = mybir.ActivationFunctionType
ALU = mybir.AluOpType
AX = mybir.AxisListType

N_CORES = 8
B, S, D = 16, 2048, 1024
B_LOC = B // N_CORES
T_LOC = B_LOC * S
LN_EPS = 1e-5


ENGINES = ("tensor", "vector", "scalar", "gpsimd", "sync")
DMA_QUEUES = ("sync", "gpsimd", "scalar")


class Reg:
    __slots__ = ("space", "name", "p0", "p1", "b0", "b1")

    def __init__(self, space, name, p0, p1, b0, b1):
        self.space, self.name = space, name
        self.p0, self.p1, self.b0, self.b1 = p0, p1, b0, b1

    def overlaps(self, o):
        return (self.name == o.name and self.p0 < o.p1 and o.p0 < self.p1
                and self.b0 < o.b1 and o.b0 < self.b1)

    def covers(self, o):
        return (self.name == o.name and self.p0 <= o.p0 and self.p1 >= o.p1
                and self.b0 <= o.b0 and self.b1 >= o.b1)

    def __repr__(self):
        return f"R({self.name},p{self.p0}:{self.p1},b{self.b0}:{self.b1})"


class TV:
    __slots__ = ("h", "psl", "csl", "dt_size", "space", "vdt", "base_el")

    def __init__(self, h, psl=None, csl=None, space="SB", vdt=None, base_el=0):
        self.h = h
        self.psl = psl if psl is not None else (0, h.shape[0])
        self.csl = csl if csl is not None else (0, int(np.prod(h.shape[1:])))
        self.vdt = vdt
        self.dt_size = mybir.dt.size(vdt if vdt is not None else h.dtype)
        self.space = space
        self.base_el = base_el

    def __getitem__(self, key):
        ps, cs = key
        assert isinstance(ps, slice) and isinstance(cs, slice)
        p0 = self.psl[0] + (ps.start or 0)
        p1 = self.psl[0] + (ps.stop if ps.stop is not None else (self.psl[1] - self.psl[0]))
        c0 = self.csl[0] + (cs.start or 0)
        c1 = self.csl[0] + (cs.stop if cs.stop is not None else (self.csl[1] - self.csl[0]))
        return TV(self.h, (p0, p1), (c0, c1), self.space, self.vdt, self.base_el)

    def ap(self):
        a = self.h.ap()
        if len(self.h.shape) > 2:
            a = a.rearrange(
                "p " + " ".join(f"d{i}" for i in range(len(self.h.shape) - 1)) +
                " -> p (" + " ".join(f"d{i}" for i in range(len(self.h.shape) - 1)) + ")")
        if self.vdt is not None:
            a = a.bitcast(self.vdt)
        b = self.base_el
        return a[self.psl[0]:self.psl[1], b + self.csl[0]:b + self.csl[1]]

    @property
    def reg(self):
        return Reg(self.space, self.h.name, self.psl[0], self.psl[1],
                   (self.base_el + self.csl[0]) * self.dt_size,
                   (self.base_el + self.csl[1]) * self.dt_size)

    @property
    def shape(self):
        return (self.psl[1] - self.psl[0], self.csl[1] - self.csl[0])


def arena_tv(arena_h, off_bytes, shape, dtype):
    esz = mybir.dt.size(dtype)
    assert off_bytes % esz == 0
    p, cols = shape
    return TV(arena_h, (0, p), (0, cols), "SB", vdt=dtype, base_el=off_bytes // esz)


def dram_reg(h, row0=None, row1=None):
    shape = h.shape
    rowbytes = int(np.prod(shape[1:])) * mybir.dt.size(h.dtype) if len(shape) > 1 else mybir.dt.size(h.dtype)
    r0 = 0 if row0 is None else row0
    r1 = shape[0] if row1 is None else row1
    return Reg("DRAM", h.name, 0, 1, r0 * rowbytes, r1 * rowbytes)


class Op:
    __slots__ = ("idx", "engine", "emit", "reads", "writes", "is_dma", "slot",
                 "use", "inc", "inc_no", "waits", "tag", "deps")

    def __init__(self, idx, engine, emit, reads, writes, is_dma, tag):
        self.idx, self.engine, self.emit = idx, engine, emit
        self.reads, self.writes = reads, writes
        self.is_dma = is_dma
        self.slot = None
        self.use = 0
        self.inc = False
        self.waits = []
        self.tag = tag


class Prog:
    def __init__(self, nc, dma_pool=12):
        self.nc = nc
        self.ops = []
        self.per_engine = {e: [] for e in ENGINES}
        self.records = {}
        self.dma_pool = dma_pool
        self.dma_uses = {q: 0 for q in DMA_QUEUES}
        self._finalized = False

    def _dep_events(self, op):
        deps = set()
        for r in op.reads:
            for (reg, oi, is_w) in self.records.get(r.name, ()):
                if is_w and reg.overlaps(r):
                    deps.add(oi)
        for w in op.writes:
            for (reg, oi, is_w) in self.records.get(w.name, ()):
                if reg.overlaps(w):
                    deps.add(oi)
        return deps

    def _record_access(self, op):
        for w in op.writes:
            lst = self.records.setdefault(w.name, [])
            lst[:] = [rec for rec in lst if not w.covers(rec[0])]
            lst.append((w, op.idx, True))
        for r in op.reads:
            lst = self.records.setdefault(r.name, [])
            lst.append((r, op.idx, False))
        for rs in (op.writes, op.reads):
            for x in rs:
                lst = self.records.get(x.name)
                if lst is not None and len(lst) > 512:
                    k = len(lst) - 384
                    old = lst[:k]
                    u = Reg(old[0][0].space, x.name,
                            min(r.p0 for (r, _, _) in old),
                            max(r.p1 for (r, _, _) in old),
                            min(r.b0 for (r, _, _) in old),
                            max(r.b1 for (r, _, _) in old))
                    mi = max(oi for (_, oi, _) in old)
                    lst[:k] = [(u, mi, True)]

    @staticmethod
    def _to_reg(x):
        r = x.reg if isinstance(x, TV) else x
        if r.space == "PS":
            return Reg("PS", r.name, 0, 128, 0, 1 << 30)
        return r

    def add(self, engine, emit, reads=(), writes=(), is_dma=False, tag=""):
        assert not self._finalized
        regs_r = [self._to_reg(x) for x in reads]
        regs_w = [self._to_reg(x) for x in writes]
        regs_w = regs_w + [r for r in regs_r if r.space == "PS"]
        op = Op(len(self.ops), engine, emit, regs_r, regs_w, is_dma, tag)
        op.deps = self._dep_events(op)
        self.ops.append(op)
        self.per_engine[engine].append(op)
        self._record_access(op)
        if is_dma:
            q = engine
            op.use = self.dma_uses[q]
            self.dma_uses[q] += 1
        return op

    def dma(self, queue, out, in_, out_regs=None, in_regs=None, tag="dma", **kw):
        out_ap = out.ap() if isinstance(out, TV) else out
        in_ap = in_.ap() if isinstance(in_, TV) else in_
        reads = in_regs if in_regs is not None else ([in_] if isinstance(in_, TV) else [])
        writes = out_regs if out_regs is not None else ([out] if isinstance(out, TV) else [])
        return self.add(queue, lambda e: e.dma_start(out=out_ap, in_=in_ap, **kw),
                        reads=reads, writes=writes, is_dma=True, tag=tag)

    def mm(self, out, lhsT, rhs, start, stop, tag="mm", **kw):
        o, l, r = out.ap(), lhsT.ap(), rhs.ap()
        return self.add("tensor", lambda e: e.matmul(o, l, r, start=start, stop=stop, **kw),
                        reads=[lhsT, rhs], writes=[out], tag=tag)

    def transp(self, out, in_, ident, tag="tr"):
        o, i, idn = out.ap(), in_.ap(), ident.ap()
        return self.add("tensor", lambda e: e.transpose(o, i, idn),
                        reads=[in_, ident], writes=[out], tag=tag)

    def act(self, out, in_, func, bias=0.0, scale=1.0, accum_out=None, tag="act"):
        o, i = out.ap(), in_.ap()
        b = bias.ap() if isinstance(bias, TV) else np.float32(bias)
        s = scale.ap() if isinstance(scale, TV) else np.float32(scale)
        ao = accum_out.ap() if isinstance(accum_out, TV) else None
        reads = [in_] + [x for x in (bias, scale) if isinstance(x, TV)]
        writes = [out] + ([accum_out] if accum_out is not None else [])
        return self.add("scalar",
                        lambda e: e.activation(o, i, func, bias=b, scale=s, accum_out=ao),
                        reads=reads, writes=writes, tag=tag)

    def v(self, meth, out, ins, tag="v", engine="vector", **kw):
        o = out.ap() if isinstance(out, TV) else out
        aps = [x.ap() if isinstance(x, TV) else x for x in ins]
        reads = [x for x in ins if isinstance(x, TV)]
        kw2 = {k: (v.ap() if isinstance(v, TV) else v) for k, v in kw.items()}
        reads += [v for v in kw.values() if isinstance(v, TV)]
        return self.add(engine, lambda e: getattr(e, meth)(o, *aps, **kw2),
                        reads=reads, writes=[out] if isinstance(out, TV) else [], tag=tag)

    def raw(self, engine, emit, reads=(), writes=(), tag="raw"):
        return self.add(engine, emit, reads=reads, writes=writes, tag=tag)

    def barrier(self, tag="bar"):
        last = {}
        for e in ENGINES:
            if self.per_engine[e]:
                last[e] = self.per_engine[e][-1].idx
        dma_last = {}
        for op in self.ops:
            if op.is_dma:
                dma_last[(op.engine, op.use % self.dma_pool)] = op.idx
        dep_idxs = set(last.values()) | set(dma_last.values())
        for e in ENGINES:
            op = self.add(e, lambda eng: eng.nop(), reads=(), writes=(),
                          tag=tag + "_" + e)
            op.deps |= {d for d in dep_idxs if d != op.idx}

    def finalize(self, sems):
        assert not self._finalized
        for op in self.ops:
            for d in op.deps:
                p = self.ops[d]
                if p.is_dma:
                    continue
                if p.engine == "tensor" and op.engine == "tensor":
                    continue
                p.inc = True
        inc_no = {e: 0 for e in ENGINES}
        for op in self.ops:
            if op.is_dma:
                q = op.engine
                op.slot = (q, op.use % self.dma_pool, op.use // self.dma_pool)
            if op.inc and not op.is_dma:
                inc_no[op.engine] += 1
                op.inc_no = inc_no[op.engine]
        clocks = {e: {} for e in ENGINES}
        snap = {}

        def key_of(p):
            if p.is_dma:
                return ("dma", p.slot[0], p.slot[1])
            return ("eng", p.engine)

        def val_of(p):
            if p.is_dma:
                return 16 * (p.slot[2] + 1)
            return getattr(p, "inc_no", None)

        for op in self.ops:
            clk = clocks[op.engine]
            if op.is_dma:
                q, s, u = op.slot
                if u > 0:
                    k = ("dma", q, s)
                    if clk.get(k, 0) < 16 * u:
                        op.waits.append((k, 16 * u))
                        clk[k] = 16 * u
            for d in sorted(op.deps):
                p = self.ops[d]
                if p.engine == op.engine and not p.is_dma:
                    if p.engine == "tensor":
                        continue
                k = key_of(p)
                v = val_of(p)
                if v is None:
                    continue
                if clk.get(k, 0) < v:
                    op.waits.append((k, v))
                    clk[k] = v
                    ps = snap.get(d)
                    if ps:
                        for kk, vv in ps.items():
                            if clk.get(kk, 0) < vv:
                                clk[kk] = vv
            if op.waits or op.is_dma or True:
                snap[op.idx] = dict(clk)
                if op.is_dma:
                    snap[op.idx][("dma", op.slot[0], op.slot[1])] = 16 * (op.slot[2] + 1)
                elif getattr(op, "inc_no", None):
                    snap[op.idx][("eng", op.engine)] = op.inc_no
        self._finalized = True
        self.sems = sems

    def emit_all(self, block):
        sems = self.sems

        def emit_engine(engname):
            def body(eng):
                for op in self.per_engine[engname]:
                    for (k, v) in op.waits:
                        eng.wait_ge(sems[k], v)
                    inst = op.emit(eng)
                    if op.is_dma:
                        q, s, u = op.slot
                        inst.then_inc(sems[("dma", q, s)], 16)
                    elif op.inc:
                        inst.then_inc(sems[("eng", op.engine)], 1)
            return body

        for e in ENGINES:
            getattr(block, e)(emit_engine(e))

    def final_waits(self):
        out = []
        for q in DMA_QUEUES:
            n = self.dma_uses[q]
            for s in range(self.dma_pool):
                uses = (n - s + self.dma_pool - 1) // self.dma_pool
                if uses > 0:
                    out.append((("dma", q, s), 16 * uses))
        return out


def make_sems(nc, ctx, dma_pool=12):
    sems = {}
    for e in ENGINES:
        sems[("eng", e)] = ctx.enter_context(nc.semaphore(f"se_{e}"))
    for q in DMA_QUEUES:
        for i in range(dma_pool):
            sems[("dma", q, i)] = ctx.enter_context(nc.semaphore(f"sd_{q}{i}"))
    return sems



def sb(nc, name, p, cols, dtype):
    return TV(nc.alloc_sbuf_tensor(name, [p, cols], dtype), space="SB")


def ps(nc, name, p, cols, dtype=F32):
    return TV(nc.alloc_psum_tensor(name, [p, cols], dtype), space="PS")


def rsqrt_eps(P, out, var, eps=LN_EPS, tag="rsqrt"):
    P.act(out, var, AF.Ln, bias=eps, tag=tag + "_ln")
    P.act(out, out, AF.Exp, scale=-0.5, tag=tag + "_exp")


def ln_stats(P, x, mu, rstd, scr, n_inv, eps=LN_EPS, tag="ln"):
    P.v("reduce_sum", mu, [x], axis=AX.X, tag=tag + "_sum")
    P.v("tensor_scalar", mu, [mu], scalar1=n_inv, scalar2=None,
        op0=ALU.mult, tag=tag + "_mu")
    P.v("tensor_scalar", scr, [x], scalar1=mu, scalar2=None,
        op0=ALU.subtract, tag=tag + "_ctr")
    P.v("tensor_tensor", scr, [scr, scr], op=ALU.mult, tag=tag + "_sq")
    P.v("reduce_sum", rstd, [scr], axis=AX.X, tag=tag + "_vsum")
    P.v("tensor_scalar", rstd, [rstd], scalar1=n_inv, scalar2=None,
        op0=ALU.mult, tag=tag + "_var")
    rsqrt_eps(P, rstd, rstd, eps=eps, tag=tag + "_rs")


class Common:

    def __init__(self, nc, P):
        self.ones = sb(nc, "c_ones", 128, 128, F32)
        self.identb = sb(nc, "c_identb", 128, 128, BF16)
        P.v("memset", self.ones, [1.0], tag="ones", engine="gpsimd")
        P.v("memset", self.identb, [1.0], tag="identb", engine="gpsimd")
        idap = self.identb.ap()
        P.add("gpsimd", lambda e: e.affine_select(
            out=idap, in_=idap, pattern=[[-1, 128]], compare_op=ALU.is_equal,
            fill=0.0, base=0, channel_multiplier=1),
            reads=[self.identb], writes=[self.identb], tag="ident_sel")


def transpose_128(P, com, out_ps, xin, tag="tr"):
    P.transp(out_ps, xin, com.identb, tag=tag)




def emit_l1ffn(P, nc, com, resid_in, out_hbm, w1t_h, b1_h, w2t_h, b2_h, pfx="f"):
    EPSQ = 1e-5
    CH = 256
    NCH = T_LOC // CH
    NSUB = CH // 128
    wq1 = TV(nc.alloc_sbuf_tensor(pfx + "wq1", [128, 8, 4096], BF16))
    wq2 = TV(nc.alloc_sbuf_tensor(pfx + "wq2", [128, 32, 1024], BF16))
    arena = nc.alloc_sbuf_tensor(pfx + "arena", [128, 16384], mybir.dt.uint8)
    wld = arena_tv(arena, 0, [128, 8 * 512], F32)
    wld2 = arena_tv(arena, 0, [128, 32 * 128], F32)
    xch = arena_tv(arena, 0, [128, 2 * 1024], F32)
    ych = arena_tv(arena, 8192, [128, 2 * 1024], F32)
    wldh = arena_tv(arena, 0, [128, 8 * 256], F32)
    wld2h = arena_tv(arena, 0, [128, 32 * 64], F32)
    qm2 = arena_tv(arena, 8192, [128, 8 * 256], F32)
    qm2b = arena_tv(arena, 8192, [128, 32 * 64], F32)
    b1t = TV(nc.alloc_sbuf_tensor(pfx + "b1t", [128, 32], F32))
    b2bc = TV(nc.alloc_sbuf_tensor(pfx + "b2bc", [128, 1024], F32))
    stats = TV(nc.alloc_sbuf_tensor(pfx + "stats", [128, 64], F32))
    gsc = TV(nc.alloc_sbuf_tensor(pfx + "gsc", [128, 32], F32))
    xln = TV(nc.alloc_sbuf_tensor(pfx + "xln", [128, 2, 1024], BF16))
    xlnT = TV(nc.alloc_sbuf_tensor(pfx + "xlnT", [128, 2, 8, 256], BF16))
    hT = TV(nc.alloc_sbuf_tensor(pfx + "hT", [128, 32, 256], BF16))
    scr = TV(nc.alloc_sbuf_tensor(pfx + "scr", [128, 1024], F32))

    ps_tr = ps(nc, pfx + "ps_tr", 128, 512, BF16)
    ps_tr2 = ps(nc, pfx + "ps_tr2", 128, 512, BF16)
    ps_h = [ps(nc, pfx + f"ps_h{i}", 128, 512, F32) for i in range(2)]
    ps_y = [ps(nc, pfx + f"ps_y{i}", 128, 512, F32) for i in range(2)]
    ps_g = ps(nc, pfx + "ps_g", 128, 128, F32)

    w1t_ap = w1t_h.ap().rearrange("(a p) f -> p a f", p=128)
    w2t_ap = w2t_h.ap().rearrange("(a p) f -> p a f", p=128)

    wldg = [arena_tv(arena, 0, [128, 8 * 256], F32),
            arena_tv(arena, 8192, [128, 8 * 256], F32)]
    wld2g = [arena_tv(arena, 0, [128, 32 * 64], F32),
             arena_tv(arena, 8192, [128, 32 * 64], F32)]
    for ci in range(16):
        wv_ = wldg[ci % 2]
        P.dma("sync", wv_, w1t_ap[:, :, ci * 256:(ci + 1) * 256],
              in_regs=[dram_reg(w1t_h)], tag=f"w1g{ci}")
        P.v("reduce_sum", gsc[:, ci:ci + 1], [wv_], axis=AX.X,
            apply_absolute_value=True, tag=f"w1abs{ci}")
    for ci in range(16):
        wv_ = wld2g[ci % 2]
        P.dma("sync", wv_, w2t_ap[:, :, ci * 64:(ci + 1) * 64],
              in_regs=[dram_reg(w2t_h)], tag=f"w2g{ci}")
        P.v("reduce_sum", gsc[:, 16 + ci:17 + ci], [wv_], axis=AX.X,
            apply_absolute_value=True, tag=f"w2abs{ci}")
    P.mm(TV(ps_g.h, (0, 1), (0, 32), "PS"), com.ones[:, 0:1], gsc[:, 0:32],
         start=True, stop=True, tag="gsum")
    P.v("tensor_copy", stats[0:1, 0:32], [TV(ps_g.h, (0, 1), (0, 32), "PS")], tag="gevac")
    P.v("reduce_sum", stats[0:1, 32:33], [stats[0:1, 0:16]], axis=AX.X, tag="g1")
    P.v("reduce_sum", stats[0:1, 33:34], [stats[0:1, 16:32]], axis=AX.X, tag="g2")
    P.v("tensor_scalar", stats[0:1, 32:33], [stats[0:1, 32:33]],
        scalar1=1.0 / (4096 * 1024), scalar2=EPSQ, op0=ALU.mult, op1=ALU.add, tag="g1m")
    P.v("tensor_scalar", stats[0:1, 33:34], [stats[0:1, 33:34]],
        scalar1=1.0 / (4096 * 1024), scalar2=EPSQ, op0=ALU.mult, op1=ALU.add, tag="g2m")
    P.v("tensor_scalar", stats[0:1, 34:35], [stats[0:1, 32:33]],
        scalar1=0.5, scalar2=None, op0=ALU.mult, tag="g1t")
    P.v("tensor_scalar", stats[0:1, 35:36], [stats[0:1, 33:34]],
        scalar1=0.5, scalar2=None, op0=ALU.mult, tag="g2t")
    P.v("tensor_scalar", stats[0:1, 36:38], [stats[0:1, 34:36]],
        scalar1=-1.0, scalar2=None, op0=ALU.mult, tag="gneg")
    P.mm(TV(ps_g.h, (0, 128), (0, 4), "PS"), com.ones[0:1, 0:128], stats[0:1, 34:38],
         start=True, stop=True, tag="gbc")
    sbc = stats[:, 40:44]
    P.v("tensor_copy", sbc, [TV(ps_g.h, (0, 128), (0, 4), "PS")], tag="gbce")

    qstg = nc.alloc_sbuf_tensor(pfx + "qstg", [128, 16384], mybir.dt.uint8)
    wqs = [(arena_tv(arena, 0, [128, 8 * 256], F32), arena_tv(arena, 8192, [128, 8 * 256], F32)),
           (arena_tv(qstg, 0, [128, 8 * 256], F32), arena_tv(qstg, 8192, [128, 8 * 256], F32))]
    w2qs = [(arena_tv(arena, 0, [128, 32 * 64], F32), arena_tv(arena, 8192, [128, 32 * 64], F32)),
            (arena_tv(qstg, 0, [128, 32 * 64], F32), arena_tv(qstg, 8192, [128, 32 * 64], F32))]
    for ci in range(16):
        wv_, qv_ = wqs[ci % 2]
        P.dma("sync", wv_, w1t_ap[:, :, ci * 256:(ci + 1) * 256],
              in_regs=[dram_reg(w1t_h)], tag=f"w1q{ci}")
        P.v("tensor_scalar", qv_, [wv_], scalar1=sbc[:, 2:3], scalar2=None,
            op0=ALU.is_le, tag=f"q1a{ci}")
        dap = wq1.h.ap()[:, :, ci * 256:(ci + 1) * 256]
        P.add("vector", lambda e, dap=dap, sap=wv_.ap(), m2ap=qv_.ap(), th=sbc[:, 0:1].ap():
              e.scalar_tensor_tensor(
            out=dap, in0=sap, scalar=th, in1=m2ap, op0=ALU.is_ge, op1=ALU.subtract),
            reads=[wv_, qv_, sbc], writes=[wq1], tag=f"q1c{ci}")
    for ci in range(16):
        wv_, qv_ = w2qs[ci % 2]
        P.dma("sync", wv_, w2t_ap[:, :, ci * 64:(ci + 1) * 64],
              in_regs=[dram_reg(w2t_h)], tag=f"w2q{ci}")
        P.v("tensor_scalar", qv_, [wv_], scalar1=sbc[:, 3:4], scalar2=None,
            op0=ALU.is_le, tag=f"q2a{ci}")
        dap = wq2.h.ap()[:, :, ci * 64:(ci + 1) * 64]
        P.add("vector", lambda e, dap=dap, sap=wv_.ap(), m2ap=qv_.ap(), th=sbc[:, 1:2].ap():
              e.scalar_tensor_tensor(
            out=dap, in0=sap, scalar=th, in1=m2ap, op0=ALU.is_ge, op1=ALU.subtract),
            reads=[wv_, qv_, sbc], writes=[wq2], tag=f"q2c{ci}")

    P.dma("sync", b1t, b1_h.ap(), in_regs=[dram_reg(b1_h)], tag="b1")
    P.dma("sync", b2bc, b2_h.ap().unsqueeze(0).partition_broadcast(128),
          in_regs=[dram_reg(b2_h)], tag="b2")

    rin_ap = resid_in.ap().rearrange("(c p) d -> p c d", p=128)
    rout_ap = out_hbm.ap().rearrange("(c p) d -> p c d", p=128)

    for ch in range(NCH):
        P.dma("sync", xch, rin_ap[:, ch * NSUB:(ch + 1) * NSUB, :],
              in_regs=[dram_reg(resid_in, ch * CH, (ch + 1) * CH)], tag=f"x{ch}")
        for c in range(NSUB):
            x_t = xch[:, c * 1024:(c + 1) * 1024]
            mu = stats[:, 32 + c * 2:33 + c * 2]
            rs_ = stats[:, 33 + c * 2:34 + c * 2]
            ln_stats(P, x_t, mu, rs_, scr, 1.0 / 1024, tag=f"ln{ch}_{c}")
            P.v("tensor_scalar", xln[:, c * 1024:(c + 1) * 1024], [x_t],
                scalar1=mu, scalar2=rs_, op0=ALU.subtract, op1=ALU.mult,
                tag=f"lnn{ch}_{c}")
        for c in range(NSUB):
            for dt_ in range(8):
                trb = ps_tr if (2 * c + dt_ // 4) % 2 == 0 else ps_tr2
                transpose_128(P, com,
                              TV(trb.h, (0, 128), ((dt_ % 4) * 128, (dt_ % 4 + 1) * 128), "PS"),
                              xln[:, c * 1024 + dt_ * 128: c * 1024 + (dt_ + 1) * 128],
                              tag=f"tr{ch}_{c}_{dt_}")
                if dt_ % 4 == 3:
                    grp = dt_ // 4
                    dap = xlnT.h.ap()[:, ch % 2, grp * 4:(grp + 1) * 4, c * 128:(c + 1) * 128]
                    pap = trb.ap()
                    P.add("vector", lambda e, dap=dap, pap=pap: e.tensor_copy(out=dap, in_=pap),
                          reads=[trb],
                          writes=[TV(xlnT.h, (0, 128), ((ch % 2) * 2048 + grp * 4 * 256,
                                                        (ch % 2) * 2048 + (grp * 4 + 4) * 256))],
                          tag=f"tre{ch}_{c}_{grp}")
        for ft in range(32):
            bank = TV(ps_h[ft % 2].h, (0, 128), (0, CH), "PS")
            for dt_ in range(8):
                P.mm(bank, TV(wq1.h, (0, 128), (dt_ * 4096 + ft * 128, dt_ * 4096 + (ft + 1) * 128)),
                     TV(xlnT.h, (0, 128), ((ch % 2) * 2048 + dt_ * 256, (ch % 2) * 2048 + (dt_ + 1) * 256)),
                     start=(dt_ == 0), stop=(dt_ == 7), tag=f"h{ch}_{ft}")
            P.act(hT[:, ft * 256:(ft + 1) * 256], bank, AF.Relu,
                  bias=b1t[:, ft:ft + 1], tag=f"he{ch}_{ft}")
        for c in range(NSUB):
            for half in range(2):
                bank = ps_y[half]
                for ft in range(32):
                    P.mm(bank, TV(hT.h, (0, 128), (ft * 256 + c * 128, ft * 256 + (c + 1) * 128)),
                         TV(wq2.h, (0, 128), (ft * 1024 + half * 512, ft * 1024 + (half + 1) * 512)),
                         start=(ft == 0), stop=(ft == 31), tag=f"y{ch}_{c}_{half}")
                P.v("scalar_tensor_tensor",
                    ych[:, c * 1024 + half * 512:c * 1024 + (half + 1) * 512],
                    [bank], scalar=0.0, in1=b2bc[:, half * 512:(half + 1) * 512],
                    op0=ALU.add, op1=ALU.add, tag=f"ye{ch}_{c}_{half}")
        for c in range(NSUB):
            y_t = ych[:, c * 1024:(c + 1) * 1024]
            mu = stats[:, 40 + c * 2:41 + c * 2]
            rs_ = stats[:, 41 + c * 2:42 + c * 2]
            ln_stats(P, y_t, mu, rs_, scr, 1.0 / 1024, tag=f"lo{ch}_{c}")
            P.v("tensor_scalar", y_t, [y_t], scalar1=mu, scalar2=rs_,
                op0=ALU.subtract, op1=ALU.mult, tag=f"lon{ch}_{c}")
            P.v("tensor_tensor", xch[:, c * 1024:(c + 1) * 1024],
                [y_t, xch[:, c * 1024:(c + 1) * 1024]], op=ALU.add, tag=f"oadd{ch}_{c}")
        P.dma("gpsimd", rout_ap[:, ch * NSUB:(ch + 1) * NSUB, :], xch,
              out_regs=[dram_reg(out_hbm, ch * CH, (ch + 1) * CH)],
              in_regs=[xch], tag=f"o{ch}")



def emit_moe(P, nc, com, resid_in, out_hbm, gate_h, w1_h, w3_h, w2_h, pfx="m"):
    TH = 2048
    NCH = TH // 512
    xmT = TV(nc.alloc_sbuf_tensor(pfx + "xmT", [128, 8, TH], BF16))
    arena = nc.alloc_sbuf_tensor(pfx + "arena", [128, 16384], mybir.dt.uint8)
    xch = arena_tv(arena, 0, [128, 4 * 1024], F32)
    xlnt = TV(nc.alloc_sbuf_tensor(pfx + "xlnt", [128, 4, 1024], BF16))
    wbuf = nc.alloc_sbuf_tensor(pfx + "wbuf", [128, 49152], mybir.dt.uint8)
    def wtv(par, which):
        off = par * 24576 + which * 8192
        return arena_tv(wbuf, off, [128, 4096], BF16)
    s1 = TV(nc.alloc_sbuf_tensor(pfx + "s1", [128, 2, 4, 512], BF16))
    s3 = TV(nc.alloc_sbuf_tensor(pfx + "s3", [128, 4, 512], BF16))
    acc = TV(nc.alloc_sbuf_tensor(pfx + "acc", [128, 16, 1024], F32))
    comb = TV(nc.alloc_sbuf_tensor(pfx + "comb", [128, 16, 16], F32))
    scs = TV(nc.alloc_sbuf_tensor(pfx + "scs", [128, 16, 16], F32))
    gsc = TV(nc.alloc_sbuf_tensor(pfx + "gscr", [128, 32], F32))
    gatesb = TV(nc.alloc_sbuf_tensor(pfx + "gate", [128, 8, 16], BF16))
    stats = TV(nc.alloc_sbuf_tensor(pfx + "stats", [128, 32], F32))
    scr = TV(nc.alloc_sbuf_tensor(pfx + "scr", [128, 1024], F32))
    ob = TV(nc.alloc_sbuf_tensor(pfx + "ob", [128, 4, 1024], F32))
    maski = TV(nc.alloc_sbuf_tensor(pfx + "maski", [128, 16], I32))
    negt = TV(nc.alloc_sbuf_tensor(pfx + "negt", [128, 16], F32))
    P.v("memset", negt, [-1e9], tag=pfx + "neg", engine="gpsimd")
    pb = [ps(nc, pfx + f"pb{i}", 128, 512, F32) for i in range(8)]

    rin_ap = resid_in.ap().rearrange("(c p) d -> p c d", p=128)
    rout_ap = out_hbm.ap().rearrange("(c p) d -> p c d", p=128)

    P.dma("gpsimd", gatesb, gate_h.ap().rearrange("(a p) e -> p a e", p=128),
          in_regs=[dram_reg(gate_h)], tag="gate")

    for half in range(2):
        tb = half * TH
        cb = half * TH // 128
        for ch in range(NCH):
            P.dma("sync", xch, rin_ap[:, cb + ch * 4:cb + (ch + 1) * 4, :],
                  in_regs=[dram_reg(resid_in, tb + ch * 512, tb + (ch + 1) * 512)],
                  tag=f"h{half}x{ch}")
            for c in range(4):
                x_t = xch[:, c * 1024:(c + 1) * 1024]
                mu = stats[:, c * 2:c * 2 + 1]
                rs_ = stats[:, c * 2 + 1:c * 2 + 2]
                ln_stats(P, x_t, mu, rs_, scr, 1.0 / 1024, tag=f"h{half}ln{ch}_{c}")
                P.v("tensor_scalar", xlnt[:, c * 1024:(c + 1) * 1024], [x_t],
                    scalar1=mu, scalar2=rs_, op0=ALU.subtract, op1=ALU.mult,
                    tag=f"h{half}lnn{ch}_{c}")
            for c in range(4):
                for dt_ in range(8):
                    pv = TV(pb[7 if (2 * c + dt_ // 4) % 2 == 0 else 6].h,
                            (0, 128), (0, 1024), "PS", vdt=BF16)
                    transpose_128(P, com, pv[:, (dt_ % 4) * 128:(dt_ % 4 + 1) * 128],
                                  xlnt[:, c * 1024 + dt_ * 128:c * 1024 + (dt_ + 1) * 128],
                                  tag=f"h{half}tr{ch}_{c}_{dt_}")
                    if dt_ % 4 == 3:
                        grp = dt_ // 4
                        dap = xmT.h.ap()[:, grp * 4:(grp + 1) * 4,
                                         ch * 512 + c * 128:ch * 512 + (c + 1) * 128]
                        pap = pv[:, 0:512].ap()
                        P.add("vector", lambda e, dap=dap, pap=pap: e.tensor_copy(out=dap, in_=pap),
                              reads=[pv], writes=[TV(xmT.h, (0, 128),
                                                     (grp * 4 * TH, (grp * 4 + 4) * TH))],
                              tag=f"h{half}tre{ch}_{c}_{grp}")
        for tt_ in range(16):
            pg = TV(pb[6].h, (0, 128), (0, 16), "PS")
            for dt_ in range(8):
                P.mm(pg, TV(xmT.h, (0, 128), (dt_ * TH + tt_ * 128, dt_ * TH + (tt_ + 1) * 128)),
                     gatesb[:, dt_ * 16:(dt_ + 1) * 16],
                     start=(dt_ == 0), stop=(dt_ == 7), tag=f"h{half}g{tt_}")
            sc_t = scs[:, tt_ * 16:(tt_ + 1) * 16]
            P.act(sc_t, pg, AF.Tanh, scale=0.5, tag=f"h{half}gs{tt_}")
            P.v("tensor_scalar", sc_t, [sc_t], scalar1=0.5, scalar2=0.5,
                op0=ALU.mult, op1=ALU.add, tag=f"h{half}gss{tt_}")
            emit_gate_combine(P, nc, com, scs, comb, gsc, maski, negt, tt_, tag=f"h{half}gc{tt_}")
        for e in range(18):
            par = e % 2
            w1b, w3b, w2b = wtv(par, 0), wtv(par, 1), wtv(par, 2)
            for (wb, wh, wtag) in ((w1b, w1_h, "w1"), (w3b, w3_h, "w3"), (w2b, w2_h, "w2")):
                src = wh.ap()[e].rearrange("(a p) f -> p a f", p=128)
                P.dma("gpsimd", wb, src, in_regs=[dram_reg(wh, e, e + 1)],
                      tag=f"h{half}{wtag}e{e}")
            def emit_y(ch):
                par2 = ch % 2
                for tsub in range(4):
                    ttile = ch * 4 + tsub
                    for h2 in range(2):
                        bank = pb[h2]
                        for ft in range(4):
                            P.mm(bank, TV(s1.h, (0, 128),
                                          (par2 * 2048 + ft * 512 + tsub * 128,
                                           par2 * 2048 + ft * 512 + (tsub + 1) * 128)),
                                 w2b[:, ft * 1024 + h2 * 512:ft * 1024 + (h2 + 1) * 512],
                                 start=(ft == 0), stop=(ft == 3), tag=f"h{half}e{e}c{ch}y")
                        dst = acc[:, ttile * 1024 + h2 * 512:ttile * 1024 + (h2 + 1) * 512]
                        if e == 0:
                            P.v("tensor_scalar", dst, [bank],
                                scalar1=comb[:, ttile * 16 + e:ttile * 16 + e + 1],
                                scalar2=None, op0=ALU.mult,
                                tag=f"h{half}e{e}c{ch}a")
                        elif e < 16:
                            P.v("scalar_tensor_tensor", dst, [bank],
                                scalar=comb[:, ttile * 16 + e:ttile * 16 + e + 1],
                                in1=dst, op0=ALU.mult, op1=ALU.add,
                                tag=f"h{half}e{e}c{ch}a")
                        else:
                            P.v("scalar_tensor_tensor", dst, [bank],
                                scalar=1.0, in1=dst, op0=ALU.mult, op1=ALU.add,
                                tag=f"h{half}e{e}c{ch}a")
            for ch in range(NCH):
                par2 = ch % 2
                for ft in range(4):
                    for dt_ in range(8):
                        P.mm(pb[2 + ft], w1b[:, dt_ * 512 + ft * 128:dt_ * 512 + (ft + 1) * 128],
                             TV(xmT.h, (0, 128), (dt_ * TH + ch * 512, dt_ * TH + (ch + 1) * 512)),
                             start=(dt_ == 0), stop=(dt_ == 7), tag=f"h{half}e{e}c{ch}h1")
                    P.act(TV(s1.h, (0, 128), (par2 * 2048 + ft * 512, par2 * 2048 + (ft + 1) * 512)),
                          pb[2 + ft], AF.Silu, tag=f"h{half}e{e}c{ch}s1_{ft}")
                for ft in range(4):
                    for dt_ in range(8):
                        P.mm(pb[(6 + ft) % 8 if ft < 2 else 6 + ft - 2] if False else pb[6 if ft % 2 == 0 else 7],
                             w3b[:, dt_ * 512 + ft * 128:dt_ * 512 + (ft + 1) * 128],
                             TV(xmT.h, (0, 128), (dt_ * TH + ch * 512, dt_ * TH + (ch + 1) * 512)),
                             start=(dt_ == 0), stop=(dt_ == 7), tag=f"h{half}e{e}c{ch}h3")
                    P.v("tensor_copy", s3[:, ft * 512:(ft + 1) * 512],
                        [pb[6 if ft % 2 == 0 else 7]],
                        tag=f"h{half}e{e}c{ch}s3_{ft}")
                for ft in range(4):
                    P.add("vector", lambda e_, d=TV(s1.h, (0, 128),
                                                    (par2 * 2048 + ft * 512,
                                                     par2 * 2048 + (ft + 1) * 512)).ap(),
                          b2=s3[:, ft * 512:(ft + 1) * 512].ap():
                          e_.tensor_tensor(out=d, in0=d, in1=b2, op=ALU.mult),
                          reads=[TV(s1.h, (0, 128), (par2 * 2048 + ft * 512,
                                                     par2 * 2048 + (ft + 1) * 512)),
                                 s3],
                          writes=[TV(s1.h, (0, 128), (par2 * 2048 + ft * 512,
                                                      par2 * 2048 + (ft + 1) * 512))],
                          op=None, tag=f"h{half}e{e}c{ch}gu{ft}") if False else                     P.v("tensor_tensor",
                        TV(s1.h, (0, 128), (par2 * 2048 + ft * 512, par2 * 2048 + (ft + 1) * 512)),
                        [TV(s1.h, (0, 128), (par2 * 2048 + ft * 512, par2 * 2048 + (ft + 1) * 512)),
                         s3[:, ft * 512:(ft + 1) * 512]],
                        op=ALU.mult, tag=f"h{half}e{e}c{ch}gu{ft}")
                if ch > 0:
                    emit_y(ch - 1)
            emit_y(NCH - 1)
        for ch in range(NCH):
            P.dma("sync", xch, rin_ap[:, cb + ch * 4:cb + (ch + 1) * 4, :],
                  in_regs=[dram_reg(resid_in, tb + ch * 512, tb + (ch + 1) * 512)],
                  tag=f"h{half}xr{ch}")
            for c in range(4):
                ttile = ch * 4 + c
                a_t = acc[:, ttile * 1024:(ttile + 1) * 1024]
                mu = stats[:, 16 + c * 2:17 + c * 2]
                rs_ = stats[:, 17 + c * 2:18 + c * 2]
                ln_stats(P, a_t, mu, rs_, scr, 1.0 / 1024, tag=f"h{half}lo{ch}_{c}")
                P.v("tensor_scalar", a_t, [a_t], scalar1=mu, scalar2=rs_,
                    op0=ALU.subtract, op1=ALU.mult, tag=f"h{half}lon{ch}_{c}")
                P.v("tensor_tensor", ob[:, c * 1024:(c + 1) * 1024],
                    [a_t, xch[:, c * 1024:(c + 1) * 1024]], op=ALU.add,
                    tag=f"h{half}oadd{ch}_{c}")
            P.dma("gpsimd", rout_ap[:, cb + ch * 4:cb + (ch + 1) * 4, :], ob,
                  out_regs=[dram_reg(out_hbm, tb + ch * 512, tb + (ch + 1) * 512)],
                  in_regs=[ob], tag=f"h{half}o{ch}")


def emit_gate_combine(P, nc, com, scs, comb, gsc, maski, negt, tt_, tag="gc"):
    base = tt_ * 16
    scall = TV(scs.h, (0, 128), (base, base + 16))
    msk = TV(comb.h, (0, 128), (base, base + 16))
    g = gsc

    def gcol(i):
        return scs.h.ap().rearrange("p a b -> p (a b)")[:, base:base + 16] \
            .rearrange("p (gg i) -> p gg i", gg=4)[:, :, i:i + 1].squeeze(2)

    rd = [scall, g]
    wr = [g]

    def vv(dst, a, b, op, t):
        P.add("vector", lambda e, d=dst, a=a, b=b, op=op:
              e.tensor_tensor(out=d, in0=a, in1=b, op=op), reads=rd, writes=wr,
              tag=tag + t)

    c0, c1, c2, c3 = gcol(0), gcol(1), gcol(2), gcol(3)
    m2 = g[:, 0:4].ap(); tmp = g[:, 4:8].ap()
    vv(m2, c0, c1, ALU.min, "p01")
    for (a, b, t) in ((c0, c2, "02"), (c0, c3, "03"), (c1, c2, "12"),
                      (c1, c3, "13"), (c2, c3, "23")):
        vv(tmp, a, b, ALU.min, "p" + t)
        vv(m2, m2, tmp, ALU.max, "x" + t)
    m1 = g[:, 4:8].ap()
    vv(m1, c0, c1, ALU.max, "m1a")
    vv(m1, m1, c2, ALU.max, "m1b")
    vv(m1, m1, c3, ALU.max, "m1c")
    gs = g[:, 12:16].ap()
    vv(gs, m1, m2, ALU.add, "gs")
    gg = lambda i: g[:, 12 + i:13 + i].ap()
    t2 = g[:, 16:17].ap(); tm = g[:, 17:18].ap()
    vv(t2, gg(0), gg(1), ALU.min, "q01")
    for (a, b, t) in ((gg(0), gg(2), "02"), (gg(0), gg(3), "03"),
                      (gg(1), gg(2), "12"), (gg(1), gg(3), "13"),
                      (gg(2), gg(3), "23")):
        vv(tm, a, b, ALU.min, "q" + t)
        vv(t2, t2, tm, ALU.max, "y" + t)
    pen = g[:, 18:22].ap()
    P.add("vector", lambda e, d=pen, a=gs, s=t2: e.tensor_scalar(
        out=d, in0=a, scalar1=s, scalar2=None, op0=ALU.is_ge),
        reads=rd, writes=wr, tag=tag + "pen0")
    P.add("vector", lambda e, d=pen: e.tensor_scalar(
        out=d, in0=d, scalar1=1e9, scalar2=-1e9, op0=ALU.mult, op1=ALU.add),
        reads=rd, writes=wr, tag=tag + "pen1")
    for gi in range(4):
        P.add("vector", lambda e, d=msk[:, gi * 4:(gi + 1) * 4].ap(),
              a=scall[:, gi * 4:(gi + 1) * 4].ap(), s=g[:, 18 + gi:19 + gi].ap():
              e.tensor_scalar(out=d, in0=a, scalar1=s, scalar2=None, op0=ALU.add),
              reads=rd, writes=[TV(comb.h, (0, 128), (base, base + 16)), g],
              tag=tag + f"msk{gi}")
    r = g[:, 16:32].ap()
    P.add("vector", lambda e, d=r, a=msk.ap(): e.tensor_copy(out=d, in_=a),
          reads=[msk, g], writes=[g], tag=tag + "rcp")
    mx = g[:, 8:9].ap()
    for it in range(3):
        P.add("vector", lambda e, d=mx, a=r: e.reduce_max(out=d, in_=a, axis=AX.X),
              reads=[g], writes=[g], tag=tag + f"t4m{it}")
        P.add("vector", lambda e, d=maski.ap(), a=r, s=mx: e.tensor_scalar(
            out=d, in0=a, scalar1=s, scalar2=None, op0=ALU.is_ge),
            reads=[g], writes=[maski], tag=tag + f"t4e{it}")
        P.add("vector", lambda e, d=r, m=maski.ap(), nt=negt.ap():
              e.copy_predicated(out=d, mask=m, data=nt),
              reads=[g, maski, negt], writes=[g], tag=tag + f"t4s{it}")
    th4 = g[:, 9:10].ap()
    P.add("vector", lambda e, d=th4, a=r: e.reduce_max(out=d, in_=a, axis=AX.X),
          reads=[g], writes=[g], tag=tag + "th4")
    P.add("vector", lambda e, d=msk.ap(), a=msk.ap(), s=th4: e.tensor_scalar(
        out=d, in0=a, scalar1=s, scalar2=None, op0=ALU.is_ge),
        reads=[msk, g], writes=[msk], tag=tag + "sel")
    P.add("vector", lambda e, d=msk.ap(), a=msk.ap(), b=scall.ap(): e.tensor_tensor(
        out=d, in0=a, in1=b, op=ALU.mult),
        reads=[msk, scall], writes=[msk], tag=tag + "wr")
    dn = g[:, 10:11].ap()
    P.add("vector", lambda e, d=dn, a=msk.ap(): e.reduce_sum(out=d, in_=a, axis=AX.X),
          reads=[msk, g], writes=[g], tag=tag + "dn")
    P.add("vector", lambda e, d=dn: e.tensor_scalar(
        out=d, in0=d, scalar1=1e-6, scalar2=None, op0=ALU.add),
        reads=[g], writes=[g], tag=tag + "dne")
    rc = g[:, 11:12].ap()
    P.add("vector", lambda e, d=rc, a=dn: e.reciprocal(out=d, in_=a),
          reads=[g], writes=[g], tag=tag + "rc")
    P.add("vector", lambda e, d=msk.ap(), a=msk.ap(), s=rc: e.tensor_scalar(
        out=d, in0=a, scalar1=s, scalar2=None, op0=ALU.mult),
        reads=[msk, g], writes=[msk], tag=tag + "cmb")



def emit_attn(P, nc, com, resid_in, out_hbm, wdq_h, wuq_h, wdkv_h, wuk_h,
              wuv_h, wo_h, pfx="a"):
    NB = B_LOC
    wdq = TV(nc.alloc_sbuf_tensor(pfx + "wdq", [128, 8, 256], BF16))
    wdkv = TV(nc.alloc_sbuf_tensor(pfx + "wdkv", [128, 8, 256], BF16))
    wuq = TV(nc.alloc_sbuf_tensor(pfx + "wuq", [128, 2, 1024], BF16))
    wuk = TV(nc.alloc_sbuf_tensor(pfx + "wuk", [128, 2, 1024], BF16))
    wuv = TV(nc.alloc_sbuf_tensor(pfx + "wuv", [128, 2, 1024], BF16))
    wo = TV(nc.alloc_sbuf_tensor(pfx + "wo", [128, 8, 1024], BF16))
    cqT = TV(nc.alloc_sbuf_tensor(pfx + "cqT", [128, 2, T_LOC], BF16))
    ckvT = TV(nc.alloc_sbuf_tensor(pfx + "ckvT", [128, 2, T_LOC], BF16))
    vt = TV(nc.alloc_sbuf_tensor(pfx + "vt", [128, 16, 16, 65], BF16))
    qt = TV(nc.alloc_sbuf_tensor(pfx + "qt", [64, 2048], BF16))
    kt = TV(nc.alloc_sbuf_tensor(pfx + "kt", [64, 2048], BF16))
    et = TV(nc.alloc_sbuf_tensor(pfx + "et", [128, 16, 512], BF16))
    oT = TV(nc.alloc_sbuf_tensor(pfx + "oT", [128, 8, 2048], BF16))
    masks = TV(nc.alloc_sbuf_tensor(pfx + "masks", [128, 4, 512], BF16))
    obuf = TV(nc.alloc_sbuf_tensor(pfx + "obuf", [64, 512], BF16))
    recb = TV(nc.alloc_sbuf_tensor(pfx + "recb", [128, 512], F32))
    arena = nc.alloc_sbuf_tensor(pfx + "arena", [128, 32768], mybir.dt.uint8)
    xch = arena_tv(arena, 0, [128, 4 * 1024], F32)
    ych = arena_tv(arena, 16384, [128, 2 * 1024], F32)
    xlnt = arena_tv(arena, 24576, [128, 2048], BF16)
    scr = arena_tv(arena, 28672, [128, 1024], F32)
    xlnT = TV(nc.alloc_sbuf_tensor(pfx + "xlnT", [128, 8, 512], BF16))
    stats = TV(nc.alloc_sbuf_tensor(pfx + "stats", [128, 32], F32))
    pb = [ps(nc, pfx + f"pb{i}", 128, 512, F32) for i in range(8)]

    rin_ap = resid_in.ap().rearrange("(c p) d -> p c d", p=128)
    rout_ap = out_hbm.ap().rearrange("(c p) d -> p c d", p=128)

    for (wb, wh, n) in ((wdq, wdq_h, "wdq"), (wdkv, wdkv_h, "wdkv")):
        P.dma("gpsimd", wb, wh.ap().rearrange("(a p) f -> p a f", p=128),
              in_regs=[dram_reg(wh)], tag=n)
    for (wb, wh, n) in ((wuq, wuq_h, "wuq"), (wuk, wuk_h, "wuk"), (wuv, wuv_h, "wuv")):
        P.dma("gpsimd", wb, wh.ap().rearrange("(a p) f -> p a f", p=128),
              in_regs=[dram_reg(wh)], tag=n)
    P.dma("gpsimd", wo, wo_h.ap().rearrange("(a p) f -> p a f", p=128),
          in_regs=[dram_reg(wo_h)], tag="wo")
    P.v("memset", masks, [1.0], tag="mkáµ¢", engine="gpsimd")
    for j in range(4):
        dap = masks.h.ap()[:, j, :]
        P.add("gpsimd", lambda e, dap=dap, j=j: e.affine_select(
            out=dap, in_=dap, pattern=[[1, 512]], compare_op=ALU.is_ge,
            fill=0.0, base=-128 * j, channel_multiplier=-1),
            reads=[masks], writes=[masks], tag=f"mask{j}")

    def stage_a(ch):
        P.dma("sync", xch, rin_ap[:, ch * 4:(ch + 1) * 4, :],
              in_regs=[dram_reg(resid_in, ch * 512, (ch + 1) * 512)], tag=f"x{ch}")
        for c in range(4):
            x_t = xch[:, c * 1024:(c + 1) * 1024]
            mu = stats[:, c * 2:c * 2 + 1]
            rs_ = stats[:, c * 2 + 1:c * 2 + 2]
            ln_stats(P, x_t, mu, rs_, scr, 1.0 / 1024, tag=f"ln{ch}_{c}")
            P.v("tensor_scalar", xlnt[:, (c % 2) * 1024:(c % 2) * 1024 + 1024], [x_t],
                scalar1=mu, scalar2=rs_, op0=ALU.subtract, op1=ALU.mult,
                tag=f"lnn{ch}_{c}")
            for dt_ in range(8):
                pv = TV(pb[7 if (2 * c + dt_ // 4) % 2 == 0 else 5].h,
                        (0, 128), (0, 1024), "PS", vdt=BF16)
                transpose_128(P, com, pv[:, (dt_ % 4) * 128:(dt_ % 4 + 1) * 128],
                              xlnt[:, (c % 2) * 1024 + dt_ * 128:(c % 2) * 1024 + (dt_ + 1) * 128],
                              tag=f"tr{ch}_{c}_{dt_}")
                if dt_ % 4 == 3:
                    grp = dt_ // 4
                    dap = xlnT.h.ap()[:, grp * 4:(grp + 1) * 4, c * 128:(c + 1) * 128]
                    pap = pv[:, 0:512].ap()
                    P.add("vector", lambda e, dap=dap, pap=pap: e.tensor_copy(out=dap, in_=pap),
                          reads=[pv], writes=[TV(xlnT.h, (0, 128),
                                                 (grp * 4 * 512, (grp * 4 + 4) * 512))],
                          tag=f"tre{ch}_{c}_{grp}")
        for (dst, wsb, bank) in ((cqT, wdq, 0), (ckvT, wdkv, 2)):
            for ct in range(2):
                pp = pb[bank + ct]
                for dt_ in range(8):
                    P.mm(pp, wsb[:, dt_ * 256 + ct * 128:dt_ * 256 + (ct + 1) * 128],
                         xlnT[:, dt_ * 512:(dt_ + 1) * 512],
                         start=(dt_ == 0), stop=(dt_ == 7), tag=f"c{ch}_{ct}")
                dap = dst.h.ap()[:, ct, ch * 512:(ch + 1) * 512]
                P.add("scalar", lambda e, dap=dap, pap=pp.ap(): e.copy(out=dap, in_=pap),
                      reads=[pp], writes=[TV(dst.h, (0, 128),
                                             (ct * T_LOC + ch * 512, ct * T_LOC + (ch + 1) * 512))],
                      tag=f"ce{ch}_{ct}")

    for b in range(NB):
        for ch in range(4 * b, 4 * b + 4):
            stage_a(ch)
        tb = b * 2048
        onesap = vt.h.ap()[:, :, :, 64:65]
        P.add("vector", lambda e, d=onesap: e.memset(d, 1.0),
              reads=[], writes=[vt], tag=f"b{b}vones")
        for tt_ in range(16):
            for half in range(2):
                pp = pb[6]
                for ct in range(2):
                    P.mm(pp, TV(ckvT.h, (0, 128),
                                (ct * T_LOC + tb + tt_ * 128, ct * T_LOC + tb + (tt_ + 1) * 128)),
                         wuv[:, ct * 1024 + half * 512:ct * 1024 + (half + 1) * 512],
                         start=(ct == 0), stop=(ct == 1), tag=f"b{b}v{tt_}_{half}")
                dap = vt.h.ap()[:, tt_, half * 8:(half + 1) * 8, 0:64]
                pap = pp.ap().rearrange("p (h v) -> p h v", h=8)
                P.add("vector", lambda e, dap=dap, pap=pap: e.tensor_copy(out=dap, in_=pap),
                      reads=[pp], writes=[TV(vt.h, (0, 128),
                                             (tt_ * 16 * 65, (tt_ + 1) * 16 * 65))],
                      tag=f"b{b}ve{tt_}_{half}")
        for h in range(16):
            for (dst, wsb) in ((qt, wuq), (kt, wuk)):
                for qc in range(4):
                    pp = TV(pb[6].h, (0, 64), (0, 512), "PS")
                    for ct in range(2):
                        P.mm(pp, wsb[:, ct * 1024 + h * 64:ct * 1024 + (h + 1) * 64],
                             TV(cqT.h if dst is qt else ckvT.h, (0, 128),
                                (ct * T_LOC + tb + qc * 512, ct * T_LOC + tb + (qc + 1) * 512)),
                             start=(ct == 0), stop=(ct == 1), tag=f"b{b}h{h}qk{qc}")
                    P.add("scalar", lambda e, dap=dst[:, qc * 512:(qc + 1) * 512].ap(),
                          pap=pp.ap(): e.copy(out=dap, in_=pap),
                          reads=[pp], writes=[dst[:, qc * 512:(qc + 1) * 512]],
                          tag=f"b{b}h{h}qke{qc}")
            for qc in range(4):
                nkt = 4 * (qc + 1)
                for kt_ in range(nkt):
                    pp = pb[kt_ % 4]
                    P.mm(pp, kt[:, kt_ * 128:(kt_ + 1) * 128], qt[:, qc * 512:(qc + 1) * 512],
                         start=True, stop=True, tag=f"b{b}h{h}s{qc}_{kt_}")
                    P.act(et[:, kt_ * 512:(kt_ + 1) * 512], pp, AF.Exp,
                          scale=0.125, bias=-20.0, tag=f"b{b}h{h}e{qc}_{kt_}")
                    j = kt_ - 4 * qc
                    if j >= 0:
                        P.v("tensor_tensor", et[:, kt_ * 512:(kt_ + 1) * 512],
                            [et[:, kt_ * 512:(kt_ + 1) * 512], masks[:, j * 512:(j + 1) * 512]],
                            op=ALU.mult, tag=f"b{b}h{h}m{qc}_{kt_}")
                po = TV(pb[4].h, (0, 65), (0, 512), "PS")
                for kt_ in range(nkt):
                    P.mm(po, TV(vt.h, (0, 128), ((kt_ * 16 + h) * 65, (kt_ * 16 + h) * 65 + 65)),
                         et[:, kt_ * 512:(kt_ + 1) * 512],
                         start=(kt_ == 0), stop=(kt_ == nkt - 1), tag=f"b{b}h{h}pv{qc}")
                P.v("reciprocal", TV(recb.h, (64, 65), (0, 512)),
                    [TV(pb[4].h, (64, 65), (0, 512), "PS")], tag=f"b{b}h{h}rc{qc}")
                prd = TV(pb[5].h, (0, 64), (0, 512), "PS")
                P.mm(prd, TV(com.ones.h, (64, 65), (0, 64)),
                     TV(recb.h, (64, 65), (0, 512)), start=True, stop=True,
                     tag=f"b{b}h{h}rb{qc}")
                P.add("scalar", lambda e, dap=obuf.ap(),
                      pap=TV(pb[4].h, (0, 64), (0, 512), "PS").ap(): e.copy(out=dap, in_=pap),
                      reads=[TV(pb[4].h, (0, 65), (0, 512), "PS")], writes=[obuf],
                      tag=f"b{b}h{h}oe{qc}")
                dst = TV(oT.h, ((h % 2) * 64, (h % 2) * 64 + 64),
                         ((h // 2) * 2048 + qc * 512, (h // 2) * 2048 + (qc + 1) * 512))
                P.v("tensor_tensor", dst, [prd, obuf], op=ALU.mult,
                    tag=f"b{b}h{h}od{qc}")
        for ch in range(4):
            P.dma("sync", xch, rin_ap[:, b * 16 + ch * 4:b * 16 + (ch + 1) * 4, :],
                  in_regs=[dram_reg(resid_in, tb + ch * 512, tb + (ch + 1) * 512)],
                  tag=f"b{b}xr{ch}")
            for ts4 in range(4):
                yv = ych[:, (ts4 % 2) * 1024:(ts4 % 2 + 1) * 1024]
                for half in range(2):
                    pp = pb[2 * (ts4 % 2) + half]
                    for dt_ in range(8):
                        P.mm(pp, TV(oT.h, (0, 128),
                                    (dt_ * 2048 + ch * 512 + ts4 * 128,
                                     dt_ * 2048 + ch * 512 + (ts4 + 1) * 128)),
                             wo[:, dt_ * 1024 + half * 512:dt_ * 1024 + (half + 1) * 512],
                             start=(dt_ == 0), stop=(dt_ == 7), tag=f"b{b}wo{ch}_{ts4}")
                    P.v("tensor_copy", yv[:, half * 512:(half + 1) * 512], [pp],
                        tag=f"b{b}woe{ch}_{ts4}_{half}")
                mu = stats[:, 16 + 2 * (ts4 % 2):17 + 2 * (ts4 % 2)]
                rs_ = stats[:, 17 + 2 * (ts4 % 2) + 2:18 + 2 * (ts4 % 2) + 2]
                ln_stats(P, yv, mu, rs_, scr[:, (ts4 % 2) * 512:(ts4 % 2) * 512 + 512] if False else scr,
                         1.0 / 1024, tag=f"b{b}lo{ch}_{ts4}")
                P.v("tensor_scalar", yv, [yv], scalar1=mu,
                    scalar2=rs_, op0=ALU.subtract, op1=ALU.mult, tag=f"b{b}lon{ch}_{ts4}")
                P.v("tensor_tensor", xch[:, ts4 * 1024:(ts4 + 1) * 1024],
                    [yv, xch[:, ts4 * 1024:(ts4 + 1) * 1024]],
                    op=ALU.add, tag=f"b{b}oadd{ch}_{ts4}")
            P.dma("gpsimd", rout_ap[:, b * 16 + ch * 4:b * 16 + (ch + 1) * 4, :], xch,
                  out_regs=[dram_reg(out_hbm, tb + ch * 512, tb + (ch + 1) * 512)],
                  in_regs=[xch], tag=f"b{b}o{ch}")



def emit_ln_stream(P, nc, com, src_h, dst_h, pfx, add_h=None, wvec_h=None):
    xch = TV(nc.alloc_sbuf_tensor(pfx + "xch", [128, 4, 1024], F32))
    ach = TV(nc.alloc_sbuf_tensor(pfx + "ach", [128, 4, 1024], F32)) if add_h is not None else None
    scr = TV(nc.alloc_sbuf_tensor(pfx + "scr", [128, 1024], F32))
    stats = TV(nc.alloc_sbuf_tensor(pfx + "stats", [128, 8], F32))
    wb = None
    if wvec_h is not None:
        wb = TV(nc.alloc_sbuf_tensor(pfx + "wb", [128, 1024], F32))
        P.dma("sync", wb, wvec_h.ap().unsqueeze(0).partition_broadcast(128),
              in_regs=[dram_reg(wvec_h)], tag=pfx + "w")
    src_ap = src_h.ap().rearrange("(c p) d -> p c d", p=128)
    dst_ap = dst_h.ap().rearrange("(c p) d -> p c d", p=128)
    add_ap = add_h.ap().rearrange("(c p) d -> p c d", p=128) if add_h is not None else None
    for ch in range(T_LOC // 512):
        P.dma("sync", xch, src_ap[:, ch * 4:(ch + 1) * 4, :],
              in_regs=[dram_reg(src_h, ch * 512, (ch + 1) * 512)], tag=f"{pfx}x{ch}")
        if add_h is not None:
            P.dma("sync", ach, add_ap[:, ch * 4:(ch + 1) * 4, :],
                  in_regs=[dram_reg(add_h, ch * 512, (ch + 1) * 512)], tag=f"{pfx}a{ch}")
        for c in range(4):
            x_t = xch[:, c * 1024:(c + 1) * 1024]
            mu = stats[:, 0:1]
            rs_ = stats[:, 1:2]
            ln_stats(P, x_t, mu, rs_, scr, 1.0 / 1024, tag=f"{pfx}ln{ch}_{c}")
            P.v("tensor_scalar", x_t, [x_t], scalar1=mu, scalar2=rs_,
                op0=ALU.subtract, op1=ALU.mult, tag=f"{pfx}n{ch}_{c}")
            if wb is not None:
                P.v("tensor_tensor", x_t, [x_t, wb], op=ALU.mult, tag=f"{pfx}w{ch}_{c}")
            if add_h is not None:
                P.v("tensor_tensor", x_t, [x_t, ach[:, c * 1024:(c + 1) * 1024]],
                    op=ALU.add, tag=f"{pfx}ad{ch}_{c}")
        P.dma("gpsimd", dst_ap[:, ch * 4:(ch + 1) * 4, :], xch,
              out_regs=[dram_reg(dst_h, ch * 512, (ch + 1) * 512)],
              in_regs=[xch], tag=f"{pfx}o{ch}")


def emit_rwkv_post(P, nc, com, xi_h, residb_h, out_h, lnoutw_h, pfx="rp"):
    xch = TV(nc.alloc_sbuf_tensor(pfx + "xch", [128, 4, 1024], F32))
    ach = TV(nc.alloc_sbuf_tensor(pfx + "ach", [128, 4, 1024], F32))
    scr = TV(nc.alloc_sbuf_tensor(pfx + "scr", [128, 1024], F32))
    stats = TV(nc.alloc_sbuf_tensor(pfx + "stats", [128, 8], F32))
    wb = TV(nc.alloc_sbuf_tensor(pfx + "wb", [128, 1024], F32))
    P.dma("sync", wb, lnoutw_h.ap().unsqueeze(0).partition_broadcast(128),
          in_regs=[dram_reg(lnoutw_h)], tag=pfx + "w")
    src_ap = xi_h.ap().rearrange("(c p) d -> p c d", p=128)
    add_ap = residb_h.ap().rearrange("(c p) d -> p c d", p=128)
    dst_ap = out_h.ap().rearrange("(c p) d -> p c d", p=128)
    for ch in range(T_LOC // 512):
        P.dma("sync", xch, src_ap[:, ch * 4:(ch + 1) * 4, :],
              in_regs=[dram_reg(xi_h, ch * 512, (ch + 1) * 512)], tag=f"{pfx}x{ch}")
        P.dma("sync", ach, add_ap[:, ch * 4:(ch + 1) * 4, :],
              in_regs=[dram_reg(residb_h, ch * 512, (ch + 1) * 512)], tag=f"{pfx}a{ch}")
        for c in range(4):
            x_t = xch[:, c * 1024:(c + 1) * 1024]
            mu = stats[:, 0:1]
            rs_ = stats[:, 1:2]
            ln_stats(P, x_t, mu, rs_, scr, 1.0 / 1024, tag=f"{pfx}l1{ch}_{c}")
            P.v("tensor_scalar", x_t, [x_t], scalar1=mu, scalar2=rs_,
                op0=ALU.subtract, op1=ALU.mult, tag=f"{pfx}n1{ch}_{c}")
            P.v("tensor_tensor", x_t, [x_t, wb], op=ALU.mult, tag=f"{pfx}w{ch}_{c}")
            ln_stats(P, x_t, mu, rs_, scr, 1.0 / 1024, tag=f"{pfx}l2{ch}_{c}")
            P.v("tensor_scalar", x_t, [x_t], scalar1=mu, scalar2=rs_,
                op0=ALU.subtract, op1=ALU.mult, tag=f"{pfx}n2{ch}_{c}")
            P.v("tensor_tensor", x_t, [x_t, ach[:, c * 1024:(c + 1) * 1024]],
                op=ALU.add, tag=f"{pfx}ad{ch}_{c}")
        P.dma("gpsimd", dst_ap[:, ch * 4:(ch + 1) * 4, :], xch,
              out_regs=[dram_reg(out_h, ch * 512, (ch + 1) * 512)],
              in_regs=[xch], tag=f"{pfx}o{ch}")


TM_STAGE = 99

def emit_timemix(P, nc, com, xi_h, pvec_h, tm1_h, tm2_h, td1_h, td2_h,
                 wr_h, wk_h, wv_h, wg_h, wo_h, gnw_h, gnb_h, masku_h, pfx="t",
                 xo_h=None):
    CH = 256
    TBANK2 = 3
    NSC = CH // 64
    NTS = CH // 128
    wsb = {}
    for (n, h_) in (("wr", wr_h), ("wk", wk_h), ("wv", wv_h), ("wg", wg_h), ("wo", wo_h)):
        wsb[n] = TV(nc.alloc_sbuf_tensor(pfx + n, [128, 8, 1024], BF16))
        P.dma("gpsimd", wsb[n], h_.ap().rearrange("(a p) f -> p a f", p=128),
              in_regs=[dram_reg(h_)], tag=pfx + n)
    tm1 = TV(nc.alloc_sbuf_tensor(pfx + "tm1", [128, 8, 160], BF16))
    P.dma("gpsimd", tm1, tm1_h.ap().rearrange("(a p) f -> p a f", p=128),
          in_regs=[dram_reg(tm1_h)], tag=pfx + "tm1")
    tm2 = TV(nc.alloc_sbuf_tensor(pfx + "tm2", [32, 5, 1024], BF16))
    P.dma("gpsimd", tm2, tm2_h.ap(), in_regs=[dram_reg(tm2_h)], tag=pfx + "tm2")
    td1 = TV(nc.alloc_sbuf_tensor(pfx + "td1", [128, 8, 64], BF16))
    P.dma("gpsimd", td1, td1_h.ap().rearrange("(a p) f -> p a f", p=128),
          in_regs=[dram_reg(td1_h)], tag=pfx + "td1")
    td2 = TV(nc.alloc_sbuf_tensor(pfx + "td2", [64, 1024], BF16))
    P.dma("gpsimd", td2, td2_h.ap(), in_regs=[dram_reg(td2_h)], tag=pfx + "td2")
    pv = TV(nc.alloc_sbuf_tensor(pfx + "pv", [128, 80], F32))
    P.dma("sync", pv, pvec_h.ap(), in_regs=[dram_reg(pvec_h)], tag=pfx + "pv")
    gnw = TV(nc.alloc_sbuf_tensor(pfx + "gnw", [128, 1024], F32))
    gnb = TV(nc.alloc_sbuf_tensor(pfx + "gnb", [128, 1024], F32))
    P.dma("sync", gnw, gnw_h.ap().unsqueeze(0).partition_broadcast(128),
          in_regs=[dram_reg(gnw_h)], tag=pfx + "gnw")
    P.dma("sync", gnb, gnb_h.ap().unsqueeze(0).partition_broadcast(128),
          in_regs=[dram_reg(gnb_h)], tag=pfx + "gnb")
    masku = TV(nc.alloc_sbuf_tensor(pfx + "masku", [128, 512], BF16))
    P.dma("gpsimd", masku, masku_h.ap(), in_regs=[dram_reg(masku_h)], tag=pfx + "mu")
    sel01 = TV(nc.alloc_sbuf_tensor(pfx + "sel", [128, 2], BF16))
    P.v("memset", sel01, [0.0], tag=pfx + "s0", engine="gpsimd")
    P.v("memset", sel01[0:64, 0:1], [1.0], tag=pfx + "s1", engine="gpsimd")
    P.v("memset", TV(sel01.h, (64, 128), (1, 2)), [1.0], tag=pfx + "s2", engine="gpsimd")
    ext = TV(nc.alloc_sbuf_tensor(pfx + "ext", [128, 8, CH + 1], BF16))
    sxT = TV(nc.alloc_sbuf_tensor(pfx + "sxT", [128, 8, CH], BF16))
    mixf = TV(nc.alloc_sbuf_tensor(pfx + "mixf", [32, 5, CH], BF16))
    xf = [TV(nc.alloc_sbuf_tensor(pfx + f"xf{i}", [128, 8, CH], BF16)) for i in range(2)]
    t1b4 = TV(nc.alloc_sbuf_tensor(pfx + "t1b", [128, 4, CH], BF16))
    wT = TV(nc.alloc_sbuf_tensor(pfx + "wT", [128, 8, CH], F32))
    dT = TV(nc.alloc_sbuf_tensor(pfx + "dT", [128, 8, CH], F32))
    dprevd = TV(nc.alloc_sbuf_tensor(pfx + "dprev", [128, 2, CH], F32))
    zz = TV(nc.alloc_sbuf_tensor(pfx + "zz", [128, 64], F32))
    rt = TV(nc.alloc_sbuf_tensor(pfx + "rt", [128, 8, CH], BF16))
    ktl = TV(nc.alloc_sbuf_tensor(pfx + "ktl", [128, 8, CH], BF16))
    khat = TV(nc.alloc_sbuf_tensor(pfx + "khat", [128, 8, 256], BF16))
    sts = TV(nc.alloc_sbuf_tensor(pfx + "sts", [128, 4, 512], BF16))
    vtk = TV(nc.alloc_sbuf_tensor(pfx + "vtk", [128, NTS, 1024], BF16))
    gtk = TV(nc.alloc_sbuf_tensor(pfx + "gtk", [128, NTS, 1024], BF16))
    wkv = TV(nc.alloc_sbuf_tensor(pfx + "wkv", [128, NTS, 1024], F32))
    ddg = TV(nc.alloc_sbuf_tensor(pfx + "ddg", [128, NTS, 16], F32))

    state = TV(nc.alloc_sbuf_tensor(pfx + "state", [128, 8, 64], F32))
    stateb = TV(nc.alloc_sbuf_tensor(pfx + "stateb", [128, 8, 64], BF16))
    xch = TV(nc.alloc_sbuf_tensor(pfx + "xch", [128, NTS, 1024], F32))
    xlt = TV(nc.alloc_sbuf_tensor(pfx + "xlt", [128, 1024], BF16))
    t1g = TV(nc.alloc_sbuf_tensor(pfx + "t1g", [128, 512], BF16))
    hd = TV(nc.alloc_sbuf_tensor(pfx + "hd", [64, 256], BF16))
    t1fd = TV(nc.alloc_sbuf_tensor(pfx + "t1f", [128, 2, 256], F32))
    kscd = TV(nc.alloc_sbuf_tensor(pfx + "ksc", [128, 2, 128], BF16))
    dend = TV(nc.alloc_sbuf_tensor(pfx + "dend", [128, 32], F32))
    scr = TV(nc.alloc_sbuf_tensor(pfx + "scr", [128, 1024], F32))
    stats = TV(nc.alloc_sbuf_tensor(pfx + "stats", [128, 64], F32))
    gno = TV(nc.alloc_sbuf_tensor(pfx + "gno", [128, NTS, 1024], BF16))
    goT = TV(nc.alloc_sbuf_tensor(pfx + "goT", [128, 8, CH], BF16))
    ruk = goT
    pb = [ps(nc, pfx + f"pb{i}", 128, 512, F32) for i in range(8)]

    xi_ap = xi_h.ap().rearrange("(c p) d -> p c d", p=128)
    xo_h = xi_h if xo_h is None else xo_h
    xo_ap = xo_h.ap().rearrange("(c p) d -> p c d", p=128)
    maacol = {"x": 8, "w": 16, "k": 24, "v": 32, "r": 40, "g": 48}

    for b in range(B_LOC):
        P.v("memset", state, [0.0], tag=f"{pfx}b{b}st0", engine="gpsimd")
        P.v("memset", stateb, [0.0], tag=f"{pfx}b{b}st0b", engine="gpsimd")
        for ch in range(2048 // CH):
            tb = b * 2048 + ch * CH
            cb = tb // 128
            P.dma("sync", xch, xi_ap[:, cb:cb + NTS, :],
                  in_regs=[dram_reg(xi_h, tb, tb + CH)], tag=f"{pfx}b{b}c{ch}x")
            if ch == 0:
                P.v("memset", TV(ext.h, (0, 128), (0, 8 * (CH + 1))), [0.0],
                    tag=f"{pfx}b{b}c{ch}z", engine="gpsimd")
            else:
                eap = ext.h.ap()
                P.add("vector", lambda e, d=eap[:, :, 0:1], s=eap[:, :, CH:CH + 1]:
                      e.tensor_copy(out=d, in_=s),
                      reads=[ext], writes=[TV(ext.h, (0, 128), (0, 8 * (CH + 1)))],
                      tag=f"{pfx}b{b}c{ch}bd")
            for c in range(NTS):
                x_t = xch[:, c * 1024:(c + 1) * 1024]
                mu = stats[:, 0:1]; rs_ = stats[:, 1:2]
                ln_stats(P, x_t, mu, rs_, scr, 1.0 / 1024, tag=f"{pfx}b{b}c{ch}l{c}")
                P.v("tensor_scalar", xlt, [x_t], scalar1=mu, scalar2=rs_,
                    op0=ALU.subtract, op1=ALU.mult, tag=f"{pfx}b{b}c{ch}n{c}")
                for dt_ in range(8):
                    grp = dt_ // 4
                    pvw = TV(pb[7 if (2 * c + grp) % 2 == 0 else TBANK2].h,
                             (0, 128), (0, 1024), "PS", vdt=BF16)
                    transpose_128(P, com, pvw[:, (dt_ % 4) * 128:(dt_ % 4 + 1) * 128],
                                  TV(xlt.h, (0, 128), (dt_ * 128, (dt_ + 1) * 128)),
                                  tag=f"{pfx}b{b}c{ch}t{c}_{dt_}")
                    if dt_ % 4 == 3:
                        dap = ext.h.ap()[:, grp * 4:(grp + 1) * 4,
                                         1 + c * 128:1 + (c + 1) * 128]
                        P.add("vector", lambda e, dap=dap, pap=pvw[:, 0:512].ap():
                              e.tensor_copy(out=dap, in_=pap),
                              reads=[pvw], writes=[TV(ext.h, (0, 128), (0, 8 * (CH + 1)))],
                              tag=f"{pfx}b{b}c{ch}te{c}_{grp}")
            for dt_ in range(8):
                P.add("vector", lambda e,
                      d=ext.h.ap()[:, dt_, 1:CH + 1], s=pv[:, dt_:dt_ + 1].ap():
                      e.tensor_scalar(out=d, in0=d, scalar1=s, scalar2=None, op0=ALU.mult),
                      reads=[ext, pv], writes=[TV(ext.h, (0, 128), (0, 8 * (CH + 1)))],
                      tag=f"{pfx}b{b}c{ch}lw{dt_}")
            P.add("vector", lambda e, d=sxT.ap(),
                  a=ext.h.ap()[:, :, 0:CH], b2=ext.h.ap()[:, :, 1:CH + 1]:
                  e.tensor_tensor(out=d, in0=a, in1=b2, op=ALU.subtract),
                  reads=[ext], writes=[sxT], tag=f"{pfx}b{b}c{ch}sx")
            if TM_STAGE < 2:
                P.dma("gpsimd", xo_ap[:, cb:cb + NTS, :], xch,
                      out_regs=[dram_reg(xo_h, tb, tb + CH)], in_regs=[xch],
                      tag=f"{pfx}b{b}c{ch}oS1")
                continue
            xxx = xf[0]
            for dt_ in range(8):
                P.add("vector", lambda e, d=xxx.h.ap()[:, dt_, :],
                      a=sxT.h.ap()[:, dt_, :], s=pv[:, maacol["x"] + dt_:maacol["x"] + dt_ + 1].ap(),
                      x2=ext.h.ap()[:, dt_, 1:CH + 1]:
                      e.scalar_tensor_tensor(out=d, in0=a, scalar=s, in1=x2,
                                             op0=ALU.mult, op1=ALU.add),
                      reads=[sxT, pv, ext], writes=[xxx], tag=f"{pfx}b{b}c{ch}xx{dt_}")
            for f in range(5):
                pm = TV(pb[f % 2].h, (0, 32), (0, CH), "PS")
                for dt_ in range(8):
                    P.mm(pm, tm1[:, dt_ * 160 + f * 32:dt_ * 160 + (f + 1) * 32],
                         TV(xxx.h, (0, 128), (dt_ * CH, (dt_ + 1) * CH)),
                         start=(dt_ == 0), stop=(dt_ == 7), tag=f"{pfx}b{b}c{ch}m{f}")
                P.act(TV(mixf.h, (0, 32), (f * CH, (f + 1) * CH)), pm, AF.Tanh,
                      tag=f"{pfx}b{b}c{ch}mt{f}")
            if TM_STAGE < 3:
                P.dma("gpsimd", xo_ap[:, cb:cb + NTS, :], xch,
                      out_regs=[dram_reg(xo_h, tb, tb + CH)], in_regs=[xch],
                      tag=f"{pfx}b{b}c{ch}oS2")
                continue
            for fi, f in enumerate("wkvrg"):
                buf = xf[fi % 2]
                for dt_ in range(8):
                    pm = pb[4 + dt_ % 2]
                    P.mm(TV(pm.h, (0, 128), (0, CH), "PS"),
                         TV(tm2.h, (0, 32), (fi * 1024 + dt_ * 128, fi * 1024 + (dt_ + 1) * 128)),
                         TV(mixf.h, (0, 32), (fi * CH, (fi + 1) * CH)),
                         start=True, stop=True, tag=f"{pfx}b{b}c{ch}{f}m{dt_}")
                    t1s = t1b4[:, (dt_ % 4) * CH:(dt_ % 4 + 1) * CH]
                    P.add("vector", lambda e, d=t1s.ap(),
                          pap=TV(pm.h, (0, 128), (0, CH), "PS").ap(),
                          s=pv[:, maacol[f] + dt_:maacol[f] + dt_ + 1].ap(),
                          sx=sxT.h.ap()[:, dt_, :]:
                          e.scalar_tensor_tensor(out=d, in0=pap, scalar=s, in1=sx,
                                                 op0=ALU.add, op1=ALU.mult),
                          reads=[TV(pm.h, (0, 128), (0, CH), "PS"), pv, sxT],
                          writes=[t1s], tag=f"{pfx}b{b}c{ch}{f}t2_{dt_}")
                    P.add("vector", lambda e, d=buf.h.ap()[:, dt_, :], a=t1s.ap(),
                          x2=ext.h.ap()[:, dt_, 1:CH + 1]:
                          e.tensor_tensor(out=d, in0=a, in1=x2, op=ALU.add),
                          reads=[t1s, ext], writes=[buf], tag=f"{pfx}b{b}c{ch}{f}t3_{dt_}")
                if f == "w":
                    emit_decay(P, nc, com, buf, td1, td2, pv, wT, dT, pb, hd,
                               t1fd, zz, CH, f"{pfx}b{b}c{ch}")
                elif f in ("k", "r"):
                    dst = ktl if f == "k" else rt
                    for od in range(8):
                        pm = TV(pb[(2 + od) % 4].h, (0, 128), (0, CH), "PS")
                        for dt_ in range(8):
                            P.mm(pm, wsb["w" + f][:, dt_ * 1024 + od * 128:dt_ * 1024 + (od + 1) * 128],
                                 TV(buf.h, (0, 128), (dt_ * CH, (dt_ + 1) * CH)),
                                 start=(dt_ == 0), stop=(dt_ == 7), tag=f"{pfx}b{b}c{ch}{f}W{od}")
                        if f == "k":
                            P.add("scalar", lambda e, d=dst.h.ap()[:, od, :], pap=pm.ap():
                                  e.copy(out=d, in_=pap), reads=[pm], writes=[dst],
                                  tag=f"{pfx}b{b}c{ch}{f}We{od}")
                        else:
                            P.act(TV(dst.h, (0, 128), (od * CH, (od + 1) * CH)), pm,
                                  AF.Tanh, scale=0.5, tag=f"{pfx}b{b}c{ch}rWt{od}")
                            P.add("vector", lambda e, d=dst.h.ap()[:, od, :]:
                                  e.tensor_scalar(out=d, in0=d, scalar1=0.5, scalar2=0.5,
                                                  op0=ALU.mult, op1=ALU.add),
                                  reads=[dst], writes=[dst], tag=f"{pfx}b{b}c{ch}rWs{od}")
                else:
                    dst = vtk if f == "v" else gtk
                    for ts_ in range(NTS):
                        for hf in range(2):
                            pm = TV(pb[(2 + 2 * ts_ + hf) % 4].h, (0, 128), (0, 512), "PS")
                            for dt_ in range(8):
                                P.mm(pm, TV(buf.h, (0, 128),
                                            (dt_ * CH + ts_ * 128, dt_ * CH + (ts_ + 1) * 128)),
                                     wsb["w" + f][:, dt_ * 1024 + hf * 512:dt_ * 1024 + (hf + 1) * 512],
                                     start=(dt_ == 0), stop=(dt_ == 7),
                                     tag=f"{pfx}b{b}c{ch}{f}W{ts_}_{hf}")
                            dap = dst.h.ap()[:, ts_, hf * 512:(hf + 1) * 512]
                            if f == "v":
                                P.add("scalar", lambda e, d=dap, pap=pm.ap():
                                      e.copy(out=d, in_=pap), reads=[pm], writes=[dst],
                                      tag=f"{pfx}b{b}c{ch}vWe{ts_}_{hf}")
                            else:
                                P.act(t1g, pm, AF.Tanh, scale=0.5,
                                      tag=f"{pfx}b{b}c{ch}gWt{ts_}_{hf}")
                                P.add("vector", lambda e, d=t1g.ap():
                                      e.tensor_scalar(out=d, in0=d, scalar1=0.5, scalar2=0.5,
                                                      op0=ALU.mult, op1=ALU.add),
                                      reads=[t1g], writes=[t1g], tag=f"{pfx}b{b}c{ch}gWs{ts_}_{hf}")
                                P.add("vector", lambda e, d=dap, a=t1g.ap(), pap=pm.ap():
                                      e.tensor_tensor(out=d, in0=pap, in1=a, op=ALU.mult),
                                      reads=[t1g, pm], writes=[dst],
                                      tag=f"{pfx}b{b}c{ch}gWm{ts_}_{hf}")
            if TM_STAGE < 4:
                P.dma("gpsimd", xo_ap[:, cb:cb + NTS, :], xch,
                      out_regs=[dram_reg(xo_h, tb, tb + CH)], in_regs=[xch],
                      tag=f"{pfx}b{b}c{ch}oS3")
                continue
            emit_wkv_chunk(P, nc, com, rt, ktl, vtk, wT, dT, dprevd, khat, sts,
                           masku, sel01, ruk, ddg, state, stateb, wkv, pv, pb,
                           kscd, CH, NSC, NTS, f"{pfx}b{b}c{ch}", t1fw=t1fd,
                           dend=dend)
            if TM_STAGE < 5:
                P.dma("gpsimd", xo_ap[:, cb:cb + NTS, :], xch,
                      out_regs=[dram_reg(xo_h, tb, tb + CH)], in_regs=[xch],
                      tag=f"{pfx}b{b}c{ch}oS4")
                continue
            for ts_ in range(NTS):
                seg3 = wkv.h.ap()[:, ts_, :].rearrange("p (a b) -> p a b", b=64)
                mu = stats[:, 16:32]
                rs_ = stats[:, 32:48]
                mu3 = lambda t: t.ap().rearrange("p (a b) -> p a b", b=1).broadcast_to([128, 16, 64])
                P.add("vector", lambda e, d=mu.ap().rearrange("p (a b) -> p a b", b=1),
                      a=seg3: e.reduce_sum(out=d, in_=a, axis=AX.X),
                      reads=[wkv], writes=[stats], tag=f"{pfx}b{b}c{ch}gnm{ts_}")
                P.v("tensor_scalar", mu, [mu], scalar1=1.0 / 64, scalar2=None,
                    op0=ALU.mult, tag=f"{pfx}b{b}c{ch}gnm2{ts_}")
                P.add("vector", lambda e, d=scr.ap().rearrange("p (a b) -> p a b", b=64),
                      a=seg3, m=mu3(mu):
                      e.tensor_tensor(out=d, in0=a, in1=m, op=ALU.subtract),
                      reads=[wkv, stats], writes=[scr], tag=f"{pfx}b{b}c{ch}gnc{ts_}")
                P.add("vector", lambda e, d=wkv.h.ap()[:, ts_, :], a=scr.ap():
                      e.tensor_tensor(out=d, in0=a, in1=a, op=ALU.mult),
                      reads=[scr], writes=[wkv], tag=f"{pfx}b{b}c{ch}gnq{ts_}")
                P.add("vector", lambda e, d=rs_.ap().rearrange("p (a b) -> p a b", b=1),
                      a=wkv.h.ap()[:, ts_, :].rearrange("p (a b) -> p a b", b=64):
                      e.reduce_sum(out=d, in_=a, axis=AX.X),
                      reads=[wkv], writes=[stats], tag=f"{pfx}b{b}c{ch}gnv{ts_}")
                P.v("tensor_scalar", rs_, [rs_], scalar1=1.0 / 64, scalar2=None,
                    op0=ALU.mult, tag=f"{pfx}b{b}c{ch}gnv2{ts_}")
                rsqrt_eps(P, rs_, rs_, tag=f"{pfx}b{b}c{ch}gnr{ts_}")
                P.add("vector", lambda e, d=scr.ap().rearrange("p (a b) -> p a b", b=64),
                      a=scr.ap().rearrange("p (a b) -> p a b", b=64), m=mu3(rs_):
                      e.tensor_tensor(out=d, in0=a, in1=m, op=ALU.mult),
                      reads=[scr, stats], writes=[scr], tag=f"{pfx}b{b}c{ch}gns{ts_}")
                P.v("tensor_tensor", scr, [scr, gnw], op=ALU.mult, tag=f"{pfx}b{b}c{ch}gw{ts_}")
                P.v("tensor_tensor", scr, [scr, gnb], op=ALU.add, tag=f"{pfx}b{b}c{ch}gb{ts_}")
                P.add("vector", lambda e, d=gno.h.ap()[:, ts_, :], a=scr.ap(),
                      g2=gtk.h.ap()[:, ts_, :]:
                      e.tensor_tensor(out=d, in0=a, in1=g2, op=ALU.mult),
                      reads=[scr, gtk], writes=[gno], tag=f"{pfx}b{b}c{ch}gg{ts_}")
                for dt_ in range(8):
                    grp = dt_ // 4
                    pvw = TV(pb[7 if (2 * ts_ + grp) % 2 == 0 else 3].h,
                             (0, 128), (0, 1024), "PS", vdt=BF16)
                    transpose_128(P, com, pvw[:, (dt_ % 4) * 128:(dt_ % 4 + 1) * 128],
                                  TV(gno.h, (0, 128), (ts_ * 1024 + dt_ * 128,
                                                       ts_ * 1024 + (dt_ + 1) * 128)),
                                  tag=f"{pfx}b{b}c{ch}gt{ts_}_{dt_}")
                    if dt_ % 4 == 3:
                        dap = goT.h.ap()[:, grp * 4:(grp + 1) * 4, ts_ * 128:(ts_ + 1) * 128]
                        P.add("vector", lambda e, dap=dap, pap=pvw[:, 0:512].ap():
                              e.tensor_copy(out=dap, in_=pap),
                              reads=[pvw], writes=[goT], tag=f"{pfx}b{b}c{ch}gte{ts_}_{grp}")
            for ts_ in range(NTS):
                for hf in range(2):
                    pm = TV(pb[hf].h, (0, 128), (0, 512), "PS")
                    for dt_ in range(8):
                        P.mm(pm, TV(goT.h, (0, 128),
                                    (dt_ * CH + ts_ * 128, dt_ * CH + (ts_ + 1) * 128)),
                             wsb["wo"][:, dt_ * 1024 + hf * 512:dt_ * 1024 + (hf + 1) * 512],
                             start=(dt_ == 0), stop=(dt_ == 7), tag=f"{pfx}b{b}c{ch}o{ts_}_{hf}")
                    P.add("vector", lambda e, d=xch.h.ap()[:, ts_, hf * 512:(hf + 1) * 512],
                          pap=pm.ap():
                          e.tensor_tensor(out=d, in0=pap, in1=d, op=ALU.add),
                          reads=[pm, xch], writes=[xch], tag=f"{pfx}b{b}c{ch}oa{ts_}_{hf}")
            P.dma("gpsimd", xo_ap[:, cb:cb + NTS, :], xch,
                  out_regs=[dram_reg(xo_h, tb, tb + CH)], in_regs=[xch],
                  tag=f"{pfx}b{b}c{ch}o")



def emit_decay(P, nc, com, wxT, td1, td2, pv, wT, dT, pb, hd, t1f, zz, CH, tg):
    pm = TV(pb[6].h, (0, 64), (0, CH), "PS")
    for dt_ in range(8):
        P.mm(pm, td1[:, dt_ * 64:(dt_ + 1) * 64],
             TV(wxT.h, (0, 128), (dt_ * CH, (dt_ + 1) * CH)),
             start=(dt_ == 0), stop=(dt_ == 7), tag=tg + "d1")
    P.act(hd, pm, AF.Tanh, tag=tg + "d1t")
    for dt_ in range(8):
        pm2 = TV(pb[4 + dt_ % 2].h, (0, 128), (0, CH), "PS")
        P.mm(pm2, TV(td2.h, (0, 64), (dt_ * 128, (dt_ + 1) * 128)), hd,
             start=True, stop=True, tag=tg + f"d2_{dt_}")
        t1s = t1f[:, (dt_ % 2) * CH:(dt_ % 2 + 1) * CH]
        P.act(t1s, pm2, AF.Exp, bias=pv[:, 56 + dt_:57 + dt_], tag=tg + f"de1_{dt_}")
        for sc in range(CH // 64):
            P.add("vector", lambda e, d=dT.h.ap()[:, dt_, sc * 64:(sc + 1) * 64],
                  a=com.ones.ap()[:, 0:64], z=t1s.ap()[:, sc * 64:(sc + 1) * 64]:
                  e.tensor_tensor_scan(out=d, data0=a, data1=z, initial=0.0,
                                       op0=ALU.mult, op1=ALU.subtract),
                  reads=[t1s, com.ones], writes=[dT], tag=tg + f"sc{dt_}_{sc}")


WKV_STAGE = 99

def emit_wkv_chunk(P, nc, com, rt, ktl, vtk, wT, dT, dprev, khat, sts, masku,
                   sel01, ruk, ddg, state, stateb, wkv, pv, pb, ksc,
                   CH, NSC, NTS, tg, t1fw=None, dend=None):
    for dt_ in range(8):
        P.add("vector", lambda e, d=ruk.h.ap()[:, dt_, :], a=rt.h.ap()[:, dt_, :],
              b2=ktl.h.ap()[:, dt_, :]:
              e.tensor_tensor(out=d, in0=a, in1=b2, op=ALU.mult),
              reads=[rt, ktl], writes=[ruk], tag=tg + f"ru{dt_}")
        P.add("vector", lambda e, d=ruk.h.ap()[:, dt_, :],
              s=pv[:, 64 + dt_:65 + dt_].ap():
              e.tensor_scalar(out=d, in0=d, scalar1=s, scalar2=None, op0=ALU.mult),
              reads=[ruk, pv], writes=[ruk], tag=tg + f"rku{dt_}")
    pd = pb[4]
    for ts_ in range(NTS):
        for dt_ in range(8):
            P.mm(TV(pd.h, (0, 128), (ts_ * 16 + dt_ * 2, ts_ * 16 + dt_ * 2 + 2), "PS"),
                 TV(ruk.h, (0, 128), (dt_ * CH + ts_ * 128, dt_ * CH + (ts_ + 1) * 128)),
                 sel01, start=True, stop=True, tag=tg + f"sd{ts_}_{dt_}")
    P.v("tensor_copy", TV(ddg.h, (0, 128), (0, NTS * 16)),
        [TV(pd.h, (0, 128), (0, NTS * 16), "PS")], tag=tg + "sde")
    if WKV_STAGE < 2:
        P.v("memset", wkv, [0.0], tag=tg + "wz", engine="gpsimd")
        return
    for dt_ in range(8):
        dpv = dprev[:, (dt_ % 2) * CH:(dt_ % 2 + 1) * CH]
        t1v = t1fw[:, (dt_ % 2) * CH:(dt_ % 2 + 1) * CH]
        P.add("vector", lambda e, d=dpv.ap()[:, 1:CH], s=dT.h.ap()[:, dt_, 0:CH - 1]:
              e.tensor_copy(out=d, in_=s), reads=[dT], writes=[dpv], tag=tg + f"dp{dt_}")
        P.add("vector", lambda e,
              d=dpv.ap().rearrange("p (s i) -> p s i", s=NSC)[:, :, 0:1]:
              e.memset(d, 0.0), reads=[], writes=[dpv], tag=tg + f"dp1{dt_}")
        P.add("scalar", lambda e, d=t1v.ap(), a=dpv.ap():
              e.activation(out=d, in_=a, func=AF.Exp),
              reads=[dpv], writes=[t1v], tag=tg + f"dpe{dt_}")
        P.add("vector", lambda e, d=rt.h.ap()[:, dt_, :], b2=t1v.ap():
              e.tensor_tensor(out=d, in0=d, in1=b2, op=ALU.mult),
              reads=[rt, t1v], writes=[rt], tag=tg + f"rs{dt_}")
        P.add("scalar", lambda e, d=wT.h.ap()[:, dt_, :], a=dT.h.ap()[:, dt_, :]:
              e.activation(out=d, in_=a, func=AF.Exp, scale=np.float32(-1.0)),
              reads=[dT], writes=[wT], tag=tg + f"rd{dt_}")
        P.add("vector", lambda e, d=ktl.h.ap()[:, dt_, :], b2=wT.h.ap()[:, dt_, :]:
              e.tensor_tensor(out=d, in0=d, in1=b2, op=ALU.mult),
              reads=[ktl, wT], writes=[ktl], tag=tg + f"ks{dt_}")
        P.add("scalar", lambda e,
              d=dend.ap()[:, dt_ * 4:dt_ * 4 + NSC].unsqueeze(1),
              a=dT.h.ap().rearrange("p a (s i) -> p a s i", i=64)[:, dt_:dt_ + 1, :, 63:64].squeeze(3):
              e.activation(out=d, in_=a, func=AF.Exp),
              reads=[dT], writes=[dend], tag=tg + f"dd{dt_}")
    if WKV_STAGE < 3:
        P.v("memset", wkv, [0.0], tag=tg + "wz", engine="gpsimd")
        return
    def s_bank(h):
        return (2 if h % 2 else 0) + (1 if h >= 8 else 0)
    def s_slot(h):
        return (h // 2) % 4
    for h in range(16):
        bank = pb[s_bank(h)]
        for sc in range(NSC):
            P.mm(TV(bank.h, ((sc % 2) * 64, (sc % 2) * 64 + 64),
                    (s_slot(h) * 128 + (sc // 2) * 64,
                     s_slot(h) * 128 + (sc // 2) * 64 + 64), "PS"),
                 TV(ktl.h, ((h % 2) * 64, (h % 2) * 64 + 64),
                    ((h // 2) * CH + sc * 64, (h // 2) * CH + (sc + 1) * 64)),
                 TV(rt.h, ((h % 2) * 64, (h % 2) * 64 + 64),
                    ((h // 2) * CH + sc * 64, (h // 2) * CH + (sc + 1) * 64)),
                 start=True, stop=True, tag=tg + f"S{h}_{sc}")
    for g in range(4):
        P.add("vector", lambda e, d=sts.h.ap()[:, g, :], pap=pb[g].ap(), m=masku.ap():
              e.tensor_tensor(out=d, in0=pap, in1=m, op=ALU.mult),
              reads=[pb[g], masku], writes=[sts], tag=tg + f"Se{g}")
    if WKV_STAGE < 4:
        P.v("memset", wkv, [0.0], tag=tg + "wz", engine="gpsimd")
        return
    for dt_ in range(8):
        for pr in range(NTS):
            kv = ksc[:, (dt_ % 2) * 128:(dt_ % 2 + 1) * 128]
            for half in range(2):
                sc = pr * 2 + half
                P.add("vector", lambda e, d=kv.ap()[:, half * 64:(half + 1) * 64],
                      a=ktl.h.ap()[:, dt_, sc * 64:(sc + 1) * 64],
                      s=dend.ap()[:, dt_ * 4 + sc:dt_ * 4 + sc + 1]:
                      e.tensor_scalar(out=d, in0=a, scalar1=s, scalar2=None, op0=ALU.mult),
                      reads=[ktl, dend], writes=[kv], tag=tg + f"kh{dt_}_{sc}")
            pvw = TV(pb[7 if dt_ % 2 == 0 else 3].h, (0, 128), (0, 256), "PS", vdt=BF16)
            transpose_128(P, com, pvw[:, 0:128], kv, tag=tg + f"kt{dt_}_{pr}")
            P.add("vector", lambda e, d=khat.h.ap()[:, dt_, pr * 128:(pr + 1) * 128],
                  pap=pvw[:, 0:128].ap():
                  e.tensor_copy(out=d, in_=pap), reads=[pvw], writes=[khat],
                  tag=tg + f"kte{dt_}_{pr}")
    if WKV_STAGE < 5:
        P.v("memset", wkv, [0.0], tag=tg + "wz", engine="gpsimd")
        return
    for pr in range(NTS):
        for sc in (pr * 2, pr * 2 + 1):
            rb = (sc % 2) * 64
            for h in range(16):
                pstt = pb[0] if h % 2 == 0 else pb[2]
                P.mm(TV(pstt.h, (rb, rb + 64), ((h // 2) * 64, (h // 2) * 64 + 64), "PS"),
                     TV(rt.h, ((h % 2) * 64, (h % 2) * 64 + 64),
                        ((h // 2) * CH + sc * 64, (h // 2) * CH + (sc + 1) * 64)),
                     TV(stateb.h, ((h % 2) * 64, (h % 2) * 64 + 64),
                        ((h // 2) * 64, (h // 2) * 64 + 64)),
                     start=True, stop=True, tag=tg + f"o2_{h}_{sc}")
            for h in range(16):
                po = pb[5] if h < 8 else pb[6]
                P.mm(TV(po.h, (rb, rb + 64), ((h % 8) * 64, (h % 8) * 64 + 64), "PS"),
                     TV(sts.h, (rb, rb + 64),
                        (s_bank(h) * 512 + s_slot(h) * 128 + (sc // 2) * 64,
                         s_bank(h) * 512 + s_slot(h) * 128 + (sc // 2) * 64 + 64)),
                     TV(vtk.h, (rb, rb + 64), ((sc // 2) * 1024 + h * 64,
                                               (sc // 2) * 1024 + (h + 1) * 64)),
                     start=True, stop=True, tag=tg + f"o1_{h}_{sc}")
            for h in range(16):
                P.mm(TV(pd.h, ((h % 2) * 64, (h % 2) * 64 + 64),
                        ((h // 2) * 64, (h // 2) * 64 + 64), "PS"),
                     TV(khat.h, (rb, rb + 64),
                        ((h // 2) * 256 + pr * 128 + (h % 2) * 64,
                         (h // 2) * 256 + pr * 128 + (h % 2) * 64 + 64)),
                     TV(vtk.h, (rb, rb + 64), ((sc // 2) * 1024 + h * 64,
                                               (sc // 2) * 1024 + (h + 1) * 64)),
                     start=True, stop=True, tag=tg + f"su{h}_{sc}")
            for dt_ in range(8):
                P.add("vector", lambda e, d=state.h.ap()[:, dt_, :],
                      s=dend.ap()[:, dt_ * 4 + sc:dt_ * 4 + sc + 1],
                      pap=pd.ap()[:, dt_ * 64:(dt_ + 1) * 64]:
                      e.scalar_tensor_tensor(out=d, in0=d, scalar=s, in1=pap,
                                             op0=ALU.mult, op1=ALU.add),
                      reads=[state, dend, pd], writes=[state], tag=tg + f"stu{dt_}_{sc}")
            P.v("tensor_copy", stateb, [state], tag=tg + f"stb{sc}")
        stgE = TV(wT.h, (0, 128), (0, 512))
        stgO = TV(wT.h, (0, 128), (512, 1024))
        P.v("tensor_copy", stgE, [pb[0]], tag=tg + f"pse{pr}")
        P.v("tensor_copy", stgO, [pb[2]], tag=tg + f"pso{pr}")
        for h in range(16):
            po = pb[5] if h < 8 else pb[6]
            stg = stgE if h % 2 == 0 else stgO
            P.add("vector", lambda e,
                  d=wkv.h.ap()[:, pr, h * 64:(h + 1) * 64],
                  a=vtk.h.ap()[:, pr, h * 64:(h + 1) * 64],
                  s=ddg.h.ap()[:, pr, h:h + 1],
                  pap=po.ap()[:, (h % 8) * 64:(h % 8) * 64 + 64]:
                  e.scalar_tensor_tensor(out=d, in0=a, scalar=s, in1=pap,
                                         op0=ALU.mult, op1=ALU.add),
                  reads=[vtk, ddg, po], writes=[wkv], tag=tg + f"we{h}_{pr}")
            P.add("vector", lambda e,
                  d=wkv.h.ap()[:, pr, h * 64:(h + 1) * 64],
                  a=wT.h.ap().rearrange("p a c -> p (a c)")[:, (0 if h % 2 == 0 else 512) + (h // 2) * 64:
                                                            (0 if h % 2 == 0 else 512) + (h // 2) * 64 + 64]:
                  e.tensor_tensor(out=d, in0=d, in1=a, op=ALU.add),
                  reads=[wkv, TV(wT.h, (0, 128), (0, 1024))], writes=[wkv],
                  tag=tg + f"ws{h}_{pr}")


def emit_chanmix(P, nc, com, xi_h, pvec_h, cwk_h, cwv_h, cwr_h, pfx="c",
                 xo_h=None):
    CH = 256
    TBANK2 = 6
    NTS = CH // 128
    cwk = TV(nc.alloc_sbuf_tensor(pfx + "cwk", [128, 8, 3072], BF16))
    cwv = TV(nc.alloc_sbuf_tensor(pfx + "cwv", [128, 24, 1024], BF16))
    cwr = TV(nc.alloc_sbuf_tensor(pfx + "cwr", [128, 8, 1024], BF16))
    P.dma("gpsimd", cwk, cwk_h.ap().rearrange("(a p) f -> p a f", p=128),
          in_regs=[dram_reg(cwk_h)], tag=pfx + "wk")
    P.dma("gpsimd", cwv, cwv_h.ap().rearrange("(a p) f -> p a f", p=128),
          in_regs=[dram_reg(cwv_h)], tag=pfx + "wv")
    P.dma("gpsimd", cwr, cwr_h.ap().rearrange("(a p) f -> p a f", p=128),
          in_regs=[dram_reg(cwr_h)], tag=pfx + "wr")
    pv = TV(nc.alloc_sbuf_tensor(pfx + "pv", [128, 24], F32))
    P.dma("sync", pv, pvec_h.ap(), in_regs=[dram_reg(pvec_h)], tag=pfx + "pv")
    ext = TV(nc.alloc_sbuf_tensor(pfx + "ext", [128, 8, CH + 1], BF16))
    sxT = TV(nc.alloc_sbuf_tensor(pfx + "sxT", [128, 8, CH], BF16))
    kx = TV(nc.alloc_sbuf_tensor(pfx + "kx", [128, 8, CH], BF16))
    rx = TV(nc.alloc_sbuf_tensor(pfx + "rx", [128, 8, CH], BF16))
    k2 = TV(nc.alloc_sbuf_tensor(pfx + "k2", [128, 24, CH], BF16))
    rsg = TV(nc.alloc_sbuf_tensor(pfx + "rsg", [128, NTS, 1024], BF16))
    xch = TV(nc.alloc_sbuf_tensor(pfx + "xch", [128, NTS, 1024], F32))
    xlt = TV(nc.alloc_sbuf_tensor(pfx + "xlt", [128, 1024], BF16))
    scr = TV(nc.alloc_sbuf_tensor(pfx + "scr", [128, 1024], F32))
    stats = TV(nc.alloc_sbuf_tensor(pfx + "stats", [128, 8], F32))
    pb = [ps(nc, pfx + f"pb{i}", 128, 512, F32) for i in range(8)]
    xi_ap = xi_h.ap().rearrange("(c p) d -> p c d", p=128)
    xo_h = xi_h if xo_h is None else xo_h
    xo_ap = xo_h.ap().rearrange("(c p) d -> p c d", p=128)

    for b in range(B_LOC):
        for ch in range(2048 // CH):
            tb = b * 2048 + ch * CH
            cb = tb // 128
            P.dma("sync", xch, xi_ap[:, cb:cb + NTS, :],
                  in_regs=[dram_reg(xi_h, tb, tb + CH)], tag=f"{pfx}b{b}c{ch}x")
            if ch == 0:
                P.v("memset", TV(ext.h, (0, 128), (0, 8 * (CH + 1))), [0.0],
                    tag=f"{pfx}b{b}c{ch}z", engine="gpsimd")
            else:
                eap = ext.h.ap()
                P.add("vector", lambda e, d=eap[:, :, 0:1], s=eap[:, :, CH:CH + 1]:
                      e.tensor_copy(out=d, in_=s),
                      reads=[ext], writes=[TV(ext.h, (0, 128), (0, 8 * (CH + 1)))],
                      tag=f"{pfx}b{b}c{ch}bd")
            for c in range(NTS):
                x_t = xch[:, c * 1024:(c + 1) * 1024]
                mu = stats[:, 0:1]; rs_ = stats[:, 1:2]
                ln_stats(P, x_t, mu, rs_, scr, 1.0 / 1024, tag=f"{pfx}b{b}c{ch}l{c}")
                P.v("tensor_scalar", xlt, [x_t], scalar1=mu, scalar2=rs_,
                    op0=ALU.subtract, op1=ALU.mult, tag=f"{pfx}b{b}c{ch}n{c}")
                for dt_ in range(8):
                    grp = dt_ // 4
                    pvw = TV(pb[7 if (2 * c + grp) % 2 == 0 else TBANK2].h,
                             (0, 128), (0, 1024), "PS", vdt=BF16)
                    transpose_128(P, com, pvw[:, (dt_ % 4) * 128:(dt_ % 4 + 1) * 128],
                                  TV(xlt.h, (0, 128), (dt_ * 128, (dt_ + 1) * 128)),
                                  tag=f"{pfx}b{b}c{ch}t{c}_{dt_}")
                    if dt_ % 4 == 3:
                        dap = ext.h.ap()[:, grp * 4:(grp + 1) * 4,
                                         1 + c * 128:1 + (c + 1) * 128]
                        P.add("vector", lambda e, dap=dap, pap=pvw[:, 0:512].ap():
                              e.tensor_copy(out=dap, in_=pap),
                              reads=[pvw], writes=[TV(ext.h, (0, 128), (0, 8 * (CH + 1)))],
                              tag=f"{pfx}b{b}c{ch}te{c}_{grp}")
            for dt_ in range(8):
                P.add("vector", lambda e,
                      d=ext.h.ap()[:, dt_, 1:CH + 1], s=pv[:, dt_:dt_ + 1].ap():
                      e.tensor_scalar(out=d, in0=d, scalar1=s, scalar2=None, op0=ALU.mult),
                      reads=[ext, pv], writes=[TV(ext.h, (0, 128), (0, 8 * (CH + 1)))],
                      tag=f"{pfx}b{b}c{ch}lw{dt_}")
            P.add("vector", lambda e, d=sxT.ap(),
                  a=ext.h.ap()[:, :, 0:CH], b2=ext.h.ap()[:, :, 1:CH + 1]:
                  e.tensor_tensor(out=d, in0=a, in1=b2, op=ALU.subtract),
                  reads=[ext], writes=[sxT], tag=f"{pfx}b{b}c{ch}sx")
            for (buf, colb) in ((kx, 8), (rx, 16)):
                for dt_ in range(8):
                    P.add("vector", lambda e, d=buf.h.ap()[:, dt_, :],
                          a=sxT.h.ap()[:, dt_, :], s=pv[:, colb + dt_:colb + dt_ + 1].ap(),
                          x2=ext.h.ap()[:, dt_, 1:CH + 1]:
                          e.scalar_tensor_tensor(out=d, in0=a, scalar=s, in1=x2,
                                                 op0=ALU.mult, op1=ALU.add),
                          reads=[sxT, pv, ext], writes=[buf],
                          tag=f"{pfx}b{b}c{ch}mx{colb}_{dt_}")
            for od in range(24):
                pm = TV(pb[od % 2].h, (0, 128), (0, CH), "PS")
                for dt_ in range(8):
                    P.mm(pm, cwk[:, dt_ * 3072 + od * 128:dt_ * 3072 + (od + 1) * 128],
                         TV(kx.h, (0, 128), (dt_ * CH, (dt_ + 1) * CH)),
                         start=(dt_ == 0), stop=(dt_ == 7), tag=f"{pfx}b{b}c{ch}k{od}")
                P.act(TV(k2.h, (0, 128), (od * CH, (od + 1) * CH)), pm, AF.Relu,
                      tag=f"{pfx}b{b}c{ch}kr{od}")
                P.add("vector", lambda e, d=k2.h.ap()[:, od, :]:
                      e.tensor_tensor(out=d, in0=d, in1=d, op=ALU.mult),
                      reads=[k2], writes=[k2], tag=f"{pfx}b{b}c{ch}kq{od}")
            for ts_ in range(NTS):
                for hf in range(2):
                    pm = TV(pb[2 + (2 * ts_ + hf) % 2].h, (0, 128), (0, 512), "PS")
                    for dt_ in range(8):
                        P.mm(pm, TV(rx.h, (0, 128),
                                    (dt_ * CH + ts_ * 128, dt_ * CH + (ts_ + 1) * 128)),
                             cwr[:, dt_ * 1024 + hf * 512:dt_ * 1024 + (hf + 1) * 512],
                             start=(dt_ == 0), stop=(dt_ == 7), tag=f"{pfx}b{b}c{ch}r{ts_}_{hf}")
                    P.act(TV(rsg.h, (0, 128), (ts_ * 1024 + hf * 512, ts_ * 1024 + (hf + 1) * 512)),
                          pm, AF.Tanh, scale=0.5, tag=f"{pfx}b{b}c{ch}rt{ts_}_{hf}")
                    P.add("vector", lambda e,
                          d=rsg.h.ap()[:, ts_, hf * 512:(hf + 1) * 512]:
                          e.tensor_scalar(out=d, in0=d, scalar1=0.5, scalar2=0.5,
                                          op0=ALU.mult, op1=ALU.add),
                          reads=[rsg], writes=[rsg], tag=f"{pfx}b{b}c{ch}rs{ts_}_{hf}")
            for ts_ in range(NTS):
                for hf in range(2):
                    pm = TV(pb[4 + (2 * ts_ + hf) % 2].h, (0, 128), (0, 512), "PS")
                    for od in range(24):
                        P.mm(pm, TV(k2.h, (0, 128),
                                    (od * CH + ts_ * 128, od * CH + (ts_ + 1) * 128)),
                             cwv[:, od * 1024 + hf * 512:od * 1024 + (hf + 1) * 512],
                             start=(od == 0), stop=(od == 23), tag=f"{pfx}b{b}c{ch}y{ts_}_{hf}")
                    P.add("vector", lambda e,
                          d=xch.h.ap()[:, ts_, hf * 512:(hf + 1) * 512],
                          r_=rsg.h.ap()[:, ts_, hf * 512:(hf + 1) * 512], pap=pm.ap():
                          e.scalar_tensor_tensor(out=d, in0=pap, scalar=1.0, in1=d,
                                                 op0=ALU.mult, op1=ALU.add)
                          if False else
                          e.tensor_tensor(out=d, in0=pap, in1=r_, op=ALU.mult),
                          reads=[pm, rsg, xch], writes=[xch],
                          tag=f"{pfx}b{b}c{ch}ye{ts_}_{hf}") if False else None
                    P.add("vector", lambda e, d=scr.ap()[:, hf * 512:(hf + 1) * 512],
                          r_=rsg.h.ap()[:, ts_, hf * 512:(hf + 1) * 512], pap=pm.ap():
                          e.tensor_tensor(out=d, in0=pap, in1=r_, op=ALU.mult),
                          reads=[pm, rsg], writes=[scr], tag=f"{pfx}b{b}c{ch}ym{ts_}_{hf}")
                    P.add("vector", lambda e, d=xch.h.ap()[:, ts_, hf * 512:(hf + 1) * 512],
                          a=scr.ap()[:, hf * 512:(hf + 1) * 512]:
                          e.tensor_tensor(out=d, in0=d, in1=a, op=ALU.add),
                          reads=[scr, xch], writes=[xch], tag=f"{pfx}b{b}c{ch}ya{ts_}_{hf}")
            P.dma("gpsimd", xo_ap[:, cb:cb + NTS, :], xch,
                  out_regs=[dram_reg(xo_h, tb, tb + CH)], in_regs=[xch],
                  tag=f"{pfx}b{b}c{ch}o")



def _pd_layout(v):
    return np.ascontiguousarray(np.asarray(v, np.float32).reshape(8, 128).T)


def _masku_host():
    p = np.arange(128)[:, None] % 64
    c = np.arange(512)[None, :] % 64
    return (c > p).astype(np.float32)


def _build_launch(emit_fn, specs, outs):
    nc = bass.Bass("TRN2", target_bir_lowering=False)
    hs = {}
    for (n, shape) in specs:
        hs[n] = nc.declare_dram_parameter(n, list(shape), F32, isOutput=(n in outs))
    import contextlib
    ctx = contextlib.ExitStack()
    sems = make_sems(nc, ctx)
    P = Prog(nc)
    com = Common(nc, P)
    emit_fn(nc, P, com, hs)
    P.finalize(sems)
    with nc.Block() as block:
        P.emit_all(block)
    ctx.close()
    return nc


_launch_cache = {}


def _run(tag, emit_fn, specs, outs, in_maps):
    nc = _launch_cache.get(tag)
    if nc is None:
        nc = _build_launch(emit_fn, specs, outs)
        _launch_cache[tag] = nc
    res = run_bass_kernel_spmd(nc, in_maps, core_ids=list(range(N_CORES)))
    return res.results


def kernel(**inp):
    if FUSED:
        return kernel_fused(**inp)
    f32 = lambda a: np.ascontiguousarray(np.asarray(a, np.float32))
    x = f32(inp["L0_attn_x"]).reshape(B, S, D)
    shards = [np.ascontiguousarray(x[2 * c:2 * c + 2].reshape(T_LOC, D))
              for c in range(N_CORES)]
    TD = (T_LOC, D)

    aw = {n: f32(inp["L0_attn_" + n]) for n in
          ("W_dq", "W_uq", "W_dkv", "W_uk", "W_uv", "W_o")}
    specs = [("resid", TD), ("wdq", (1024, 256)), ("wuq", (256, 1024)),
             ("wdkv", (1024, 256)), ("wuk", (256, 1024)), ("wuv", (256, 1024)),
             ("wo", (1024, 1024)), ("out", TD)]
    maps = [dict(resid=shards[c], wdq=aw["W_dq"], wuq=aw["W_uq"],
                 wdkv=aw["W_dkv"], wuk=aw["W_uk"], wuv=aw["W_uv"],
                 wo=aw["W_o"]) for c in range(N_CORES)]
    r = _run("attn", lambda nc, P, com, h: emit_attn(
        P, nc, com, h["resid"], h["out"], h["wdq"], h["wuq"], h["wdkv"],
        h["wuk"], h["wuv"], h["wo"]), specs, {"out"}, maps)
    resid = [r[c]["out"] for c in range(N_CORES)]

    w1e = f32(inp["L0_ffn_w1_e"]); w3e = f32(inp["L0_ffn_w3_e"])
    w2e = f32(inp["L0_ffn_w2_e"])
    w1s = f32(inp["L0_ffn_w1_s"]); w3s = f32(inp["L0_ffn_w3_s"])
    w2s = f32(inp["L0_ffn_w2_s"])
    w1te = np.ascontiguousarray(np.concatenate(
        [w1e.transpose(0, 2, 1), w1s.T.reshape(1024, 2, 512).transpose(1, 0, 2)], 0))
    w3te = np.ascontiguousarray(np.concatenate(
        [w3e.transpose(0, 2, 1), w3s.T.reshape(1024, 2, 512).transpose(1, 0, 2)], 0))
    w2te = np.ascontiguousarray(np.concatenate(
        [w2e.transpose(0, 2, 1), w2s.T.reshape(2, 512, 1024)], 0))
    gt = np.ascontiguousarray(f32(inp["L0_ffn_w_gate"]).T)
    specs = [("resid", TD), ("gate", (1024, 16)), ("w1", (18, 1024, 512)),
             ("w3", (18, 1024, 512)), ("w2", (18, 512, 1024)), ("out", TD)]
    maps = [dict(resid=resid[c], gate=gt, w1=w1te, w3=w3te, w2=w2te)
            for c in range(N_CORES)]
    r = _run("moe", lambda nc, P, com, h: emit_moe(
        P, nc, com, h["resid"], h["out"], h["gate"], h["w1"], h["w3"], h["w2"]),
        specs, {"out"}, maps)
    resid = [r[c]["out"] for c in range(N_CORES)]

    specs = [("src", TD), ("dst", TD)]
    maps = [dict(src=resid[c]) for c in range(N_CORES)]
    r = _run("pre", lambda nc, P, com, h: emit_ln_stream(
        P, nc, com, h["src"], h["dst"], "rq"), specs, {"dst"}, maps)
    xi = [r[c]["dst"] for c in range(N_CORES)]

    for L in range(2):
        g = lambda n: f32(inp["L1_attn_" + n][L])
        pvec80 = np.concatenate(
            [_pd_layout(g("ln1_w")), _pd_layout(g("x_maa")), _pd_layout(g("w_maa")),
             _pd_layout(g("k_maa")), _pd_layout(g("v_maa")), _pd_layout(g("r_maa")),
             _pd_layout(g("g_maa")), _pd_layout(g("time_decay").reshape(-1)),
             _pd_layout(g("time_first").reshape(-1)),
             np.zeros((128, 8), np.float32)], 1)
        specs = [("xi", TD), ("pvec", (128, 80)), ("tm1", (1024, 160)),
                 ("tm2", (32, 5, 1024)), ("td1", (1024, 64)), ("td2", (64, 1024)),
                 ("wr", (1024, 1024)), ("wk", (1024, 1024)), ("wv", (1024, 1024)),
                 ("wg", (1024, 1024)), ("wo", (1024, 1024)), ("gnw", (1024,)),
                 ("gnb", (1024,)), ("mku", (128, 512)), ("xo", TD)]
        maps = [dict(xi=xi[c], pvec=pvec80,
                     tm1=g("tm_w1"),
                     tm2=np.ascontiguousarray(g("tm_w2").transpose(1, 0, 2)),
                     td1=g("td_w1"), td2=g("td_w2"), wr=g("Wr"), wk=g("Wk"),
                     wv=g("Wv"), wg=g("Wg"), wo=g("Wo"), gnw=g("gn_w"),
                     gnb=g("gn_b"), mku=_masku_host()) for c in range(N_CORES)]
        r = _run(f"tm{L}", lambda nc, P, com, h: emit_timemix(
            P, nc, com, h["xi"], h["pvec"], h["tm1"], h["tm2"], h["td1"],
            h["td2"], h["wr"], h["wk"], h["wv"], h["wg"], h["wo"], h["gnw"],
            h["gnb"], h["mku"], xo_h=h["xo"]), specs, {"xo"}, maps)
        xi = [r[c]["xo"] for c in range(N_CORES)]
        pvec24 = np.concatenate([_pd_layout(g("ln2_w")), _pd_layout(g("ck_maa")),
                                 _pd_layout(g("cr_maa"))], 1)
        specs = [("xi", TD), ("pvec", (128, 24)), ("cwk", (1024, 3072)),
                 ("cwv", (3072, 1024)), ("cwr", (1024, 1024)), ("xo", TD)]
        maps = [dict(xi=xi[c], pvec=pvec24, cwk=g("cWk"), cwv=g("cWv"),
                     cwr=g("cWr")) for c in range(N_CORES)]
        r = _run(f"cm{L}", lambda nc, P, com, h: emit_chanmix(
            P, nc, com, h["xi"], h["pvec"], h["cwk"], h["cwv"], h["cwr"],
            xo_h=h["xo"]), specs, {"xo"}, maps)
        xi = [r[c]["xo"] for c in range(N_CORES)]

    specs = [("xi", TD), ("residb", TD), ("lnout", (1024,)), ("out", TD)]
    maps = [dict(xi=xi[c], residb=resid[c], lnout=f32(inp["L1_attn_ln_out_w"]))
            for c in range(N_CORES)]
    r = _run("post", lambda nc, P, com, h: emit_rwkv_post(
        P, nc, com, h["xi"], h["residb"], h["out"], h["lnout"]),
        specs, {"out"}, maps)
    resid = [r[c]["out"] for c in range(N_CORES)]

    b1r = np.ascontiguousarray(f32(inp["L1_ffn_b1"]).reshape(32, 128).T)
    specs = [("resid", TD), ("w1t", (1024, 4096)), ("b1", (128, 32)),
             ("w2t", (4096, 1024)), ("b2", (1024,)), ("out", TD)]
    maps = [dict(resid=resid[c], w1t=np.ascontiguousarray(f32(inp["L1_ffn_W1"]).T),
                 b1=b1r, w2t=np.ascontiguousarray(f32(inp["L1_ffn_W2"]).T),
                 b2=f32(inp["L1_ffn_b2"])) for c in range(N_CORES)]
    r = _run("ffn", lambda nc, P, com, h: emit_l1ffn(
        P, nc, com, h["resid"], h["out"], h["w1t"], h["b1"], h["w2t"], h["b2"]),
        specs, {"out"}, maps)
    out = np.concatenate([np.asarray(r[c]["out"]) for c in range(N_CORES)], 0)
    return np.ascontiguousarray(out.reshape(B, S, D).astype(np.float32))



FUSED = True


def _build_fused():
    nc = bass.Bass("TRN2", target_bir_lowering=False)
    TD = [T_LOC, D]
    E = lambda n, sh: nc.declare_dram_parameter(n, list(sh), F32, isOutput=False)
    h = {}
    h["x"] = E("x", TD)
    for n, sh in (("wdq", (1024, 256)), ("wuq", (256, 1024)), ("wdkv", (1024, 256)),
                  ("wuk", (256, 1024)), ("wuv", (256, 1024)), ("wo0", (1024, 1024)),
                  ("gate", (1024, 16)), ("mw1", (18, 1024, 512)),
                  ("mw3", (18, 1024, 512)), ("mw2", (18, 512, 1024)),
                  ("lnout", (1024,)), ("fw1t", (1024, 4096)), ("fb1", (128, 32)),
                  ("fw2t", (4096, 1024)), ("fb2", (1024,)), ("mku", (128, 512))):
        h[n] = E(n, sh)
    for L in range(2):
        for n, sh in (("pvec", (128, 80)), ("tm1", (1024, 160)), ("tm2", (32, 5, 1024)),
                      ("td1", (1024, 64)), ("td2", (64, 1024)), ("wr", (1024, 1024)),
                      ("wk", (1024, 1024)), ("wv", (1024, 1024)), ("wg", (1024, 1024)),
                      ("wo", (1024, 1024)), ("gnw", (1024,)), ("gnb", (1024,)),
                      ("cpvec", (128, 24)), ("cwk", (1024, 3072)),
                      ("cwv", (3072, 1024)), ("cwr", (1024, 1024))):
            h[f"L{L}{n}"] = E(f"L{L}{n}", sh)
    out = nc.declare_dram_parameter("out", TD, F32, isOutput=True)
    resid = nc.dram_tensor("residI", TD, F32, kind="Internal")
    residb = nc.dram_tensor("residbI", TD, F32, kind="Internal")
    xi = nc.dram_tensor("xiI", TD, F32, kind="Internal")

    import contextlib
    ctx = contextlib.ExitStack()
    sems = make_sems(nc, ctx)
    P = Prog(nc)
    com = Common(nc, P)
    sb_base = nc.sbuf_base
    psb = nc.psum_base

    def phase(emit):
        nonlocal sb_base
        nc.sbuf_base = sb_base
        nc.psum_base = psb
        emit()
        P.barrier()

    phase(lambda: emit_attn(P, nc, com, h["x"], resid, h["wdq"], h["wuq"],
                            h["wdkv"], h["wuk"], h["wuv"], h["wo0"], pfx="Pa"))
    phase(lambda: emit_moe(P, nc, com, resid, residb, h["gate"], h["mw1"],
                           h["mw3"], h["mw2"], pfx="Pm"))
    phase(lambda: emit_ln_stream(P, nc, com, residb, xi, "Pq"))
    for L in range(2):
        phase(lambda L=L: emit_timemix(
            P, nc, com, xi, h[f"L{L}pvec"], h[f"L{L}tm1"], h[f"L{L}tm2"],
            h[f"L{L}td1"], h[f"L{L}td2"], h[f"L{L}wr"], h[f"L{L}wk"],
            h[f"L{L}wv"], h[f"L{L}wg"], h[f"L{L}wo"], h[f"L{L}gnw"],
            h[f"L{L}gnb"], h["mku"], pfx=f"Pt{L}"))
        phase(lambda L=L: emit_chanmix(
            P, nc, com, xi, h[f"L{L}cpvec"], h[f"L{L}cwk"], h[f"L{L}cwv"],
            h[f"L{L}cwr"], pfx=f"Pc{L}"))
    phase(lambda: emit_rwkv_post(P, nc, com, xi, residb, resid, h["lnout"], pfx="Pp"))
    phase(lambda: emit_l1ffn(P, nc, com, resid, out, h["fw1t"], h["fb1"],
                             h["fw2t"], h["fb2"], pfx="Pf"))
    P.finalize(sems)
    with nc.Block() as block:
        P.emit_all(block)
    ctx.close()
    return nc


def _fused_in_maps(inp, shards):
    f32 = lambda a: np.ascontiguousarray(np.asarray(a, np.float32))
    aw = {n: f32(inp["L0_attn_" + n]) for n in
          ("W_dq", "W_uq", "W_dkv", "W_uk", "W_uv", "W_o")}
    w1e = f32(inp["L0_ffn_w1_e"]); w3e = f32(inp["L0_ffn_w3_e"])
    w2e = f32(inp["L0_ffn_w2_e"])
    w1s = f32(inp["L0_ffn_w1_s"]); w3s = f32(inp["L0_ffn_w3_s"])
    w2s = f32(inp["L0_ffn_w2_s"])
    base = dict(
        wdq=aw["W_dq"], wuq=aw["W_uq"], wdkv=aw["W_dkv"], wuk=aw["W_uk"],
        wuv=aw["W_uv"], wo0=aw["W_o"],
        gate=np.ascontiguousarray(f32(inp["L0_ffn_w_gate"]).T),
        mw1=np.ascontiguousarray(np.concatenate(
            [w1e.transpose(0, 2, 1), w1s.T.reshape(1024, 2, 512).transpose(1, 0, 2)], 0)),
        mw3=np.ascontiguousarray(np.concatenate(
            [w3e.transpose(0, 2, 1), w3s.T.reshape(1024, 2, 512).transpose(1, 0, 2)], 0)),
        mw2=np.ascontiguousarray(np.concatenate(
            [w2e.transpose(0, 2, 1), w2s.T.reshape(2, 512, 1024)], 0)),
        lnout=f32(inp["L1_attn_ln_out_w"]),
        fw1t=np.ascontiguousarray(f32(inp["L1_ffn_W1"]).T),
        fb1=np.ascontiguousarray(f32(inp["L1_ffn_b1"]).reshape(32, 128).T),
        fw2t=np.ascontiguousarray(f32(inp["L1_ffn_W2"]).T),
        fb2=f32(inp["L1_ffn_b2"]),
        mku=_masku_host(),
    )
    for L in range(2):
        g = lambda n: f32(inp["L1_attn_" + n][L])
        base[f"L{L}pvec"] = np.concatenate(
            [_pd_layout(g("ln1_w")), _pd_layout(g("x_maa")), _pd_layout(g("w_maa")),
             _pd_layout(g("k_maa")), _pd_layout(g("v_maa")), _pd_layout(g("r_maa")),
             _pd_layout(g("g_maa")), _pd_layout(g("time_decay").reshape(-1)),
             _pd_layout(g("time_first").reshape(-1)),
             np.zeros((128, 8), np.float32)], 1)
        base[f"L{L}tm1"] = g("tm_w1")
        base[f"L{L}tm2"] = np.ascontiguousarray(g("tm_w2").transpose(1, 0, 2))
        base[f"L{L}td1"] = g("td_w1"); base[f"L{L}td2"] = g("td_w2")
        for n in ("Wr", "Wk", "Wv", "Wg", "Wo"):
            base[f"L{L}{'w' + n[1].lower()}"] = g(n)
        base[f"L{L}gnw"] = g("gn_w"); base[f"L{L}gnb"] = g("gn_b")
        base[f"L{L}cpvec"] = np.concatenate(
            [_pd_layout(g("ln2_w")), _pd_layout(g("ck_maa")), _pd_layout(g("cr_maa"))], 1)
        base[f"L{L}cwk"] = g("cWk"); base[f"L{L}cwv"] = g("cWv")
        base[f"L{L}cwr"] = g("cWr")
    return [dict(base, x=shards[c]) for c in range(N_CORES)]


def kernel_fused(**inp):
    f32 = lambda a: np.ascontiguousarray(np.asarray(a, np.float32))
    x = f32(inp["L0_attn_x"]).reshape(B, S, D)
    shards = [np.ascontiguousarray(x[2 * c:2 * c + 2].reshape(T_LOC, D))
              for c in range(N_CORES)]
    nc = _launch_cache.get("fused")
    if nc is None:
        nc = _build_fused()
        _launch_cache["fused"] = nc
    maps = _fused_in_maps(inp, shards)
    res = run_bass_kernel_spmd(nc, maps, core_ids=list(range(N_CORES)))
    out = np.concatenate([np.asarray(res.results[c]["out"]) for c in range(N_CORES)], 0)
    return np.ascontiguousarray(out.reshape(B, S, D).astype(np.float32))


# version 07
# speedup vs baseline: 1.2678x; 1.0731x over previous
import math
import numpy as np
import concourse.bass as bass
import concourse.mybir as mybir
from concourse.bass_utils import run_bass_kernel_spmd

F32 = mybir.dt.float32
BF16 = mybir.dt.bfloat16
I32 = mybir.dt.int32
AF = mybir.ActivationFunctionType
ALU = mybir.AluOpType
AX = mybir.AxisListType

N_CORES = 8
B, S, D = 16, 2048, 1024
B_LOC = B // N_CORES
T_LOC = B_LOC * S
LN_EPS = 1e-5


ENGINES = ("tensor", "vector", "scalar", "gpsimd", "sync")
RELAX_SAME_ENGINE = False
SCHED = True
DMA_QUEUES = ("sync", "gpsimd", "scalar")


class Reg:
    __slots__ = ("space", "name", "p0", "p1", "b0", "b1")

    def __init__(self, space, name, p0, p1, b0, b1):
        self.space, self.name = space, name
        self.p0, self.p1, self.b0, self.b1 = p0, p1, b0, b1

    def overlaps(self, o):
        return (self.name == o.name and self.p0 < o.p1 and o.p0 < self.p1
                and self.b0 < o.b1 and o.b0 < self.b1)

    def covers(self, o):
        return (self.name == o.name and self.p0 <= o.p0 and self.p1 >= o.p1
                and self.b0 <= o.b0 and self.b1 >= o.b1)

    def __repr__(self):
        return f"R({self.name},p{self.p0}:{self.p1},b{self.b0}:{self.b1})"


class TV:
    __slots__ = ("h", "psl", "csl", "dt_size", "space", "vdt", "base_el")

    def __init__(self, h, psl=None, csl=None, space="SB", vdt=None, base_el=0):
        self.h = h
        self.psl = psl if psl is not None else (0, h.shape[0])
        self.csl = csl if csl is not None else (0, int(np.prod(h.shape[1:])))
        self.vdt = vdt
        self.dt_size = mybir.dt.size(vdt if vdt is not None else h.dtype)
        self.space = space
        self.base_el = base_el

    def __getitem__(self, key):
        ps, cs = key
        assert isinstance(ps, slice) and isinstance(cs, slice)
        p0 = self.psl[0] + (ps.start or 0)
        p1 = self.psl[0] + (ps.stop if ps.stop is not None else (self.psl[1] - self.psl[0]))
        c0 = self.csl[0] + (cs.start or 0)
        c1 = self.csl[0] + (cs.stop if cs.stop is not None else (self.csl[1] - self.csl[0]))
        return TV(self.h, (p0, p1), (c0, c1), self.space, self.vdt, self.base_el)

    def ap(self):
        a = self.h.ap()
        if len(self.h.shape) > 2:
            a = a.rearrange(
                "p " + " ".join(f"d{i}" for i in range(len(self.h.shape) - 1)) +
                " -> p (" + " ".join(f"d{i}" for i in range(len(self.h.shape) - 1)) + ")")
        if self.vdt is not None:
            a = a.bitcast(self.vdt)
        b = self.base_el
        return a[self.psl[0]:self.psl[1], b + self.csl[0]:b + self.csl[1]]

    @property
    def reg(self):
        return Reg(self.space, self.h.name, self.psl[0], self.psl[1],
                   (self.base_el + self.csl[0]) * self.dt_size,
                   (self.base_el + self.csl[1]) * self.dt_size)

    @property
    def shape(self):
        return (self.psl[1] - self.psl[0], self.csl[1] - self.csl[0])


def arena_tv(arena_h, off_bytes, shape, dtype):
    esz = mybir.dt.size(dtype)
    assert off_bytes % esz == 0
    p, cols = shape
    return TV(arena_h, (0, p), (0, cols), "SB", vdt=dtype, base_el=off_bytes // esz)


def dram_reg(h, row0=None, row1=None):
    shape = h.shape
    rowbytes = int(np.prod(shape[1:])) * mybir.dt.size(h.dtype) if len(shape) > 1 else mybir.dt.size(h.dtype)
    r0 = 0 if row0 is None else row0
    r1 = shape[0] if row1 is None else row1
    return Reg("DRAM", h.name, 0, 1, r0 * rowbytes, r1 * rowbytes)


class Op:
    __slots__ = ("idx", "engine", "emit", "reads", "writes", "is_dma", "slot",
                 "use", "inc", "inc_no", "waits", "tag", "deps")

    def __init__(self, idx, engine, emit, reads, writes, is_dma, tag):
        self.idx, self.engine, self.emit = idx, engine, emit
        self.reads, self.writes = reads, writes
        self.is_dma = is_dma
        self.slot = None
        self.use = 0
        self.inc = False
        self.waits = []
        self.tag = tag


class Prog:
    def __init__(self, nc, dma_pool=12):
        self.nc = nc
        self.ops = []
        self.per_engine = {e: [] for e in ENGINES}
        self.records = {}
        self.dma_pool = dma_pool
        self.dma_uses = {q: 0 for q in DMA_QUEUES}
        self._finalized = False

    def _dep_events(self, op):
        deps = set()
        for r in op.reads:
            for (reg, oi, is_w) in self.records.get(r.name, ()):
                if is_w and reg.overlaps(r):
                    deps.add(oi)
        for w in op.writes:
            for (reg, oi, is_w) in self.records.get(w.name, ()):
                if reg.overlaps(w):
                    deps.add(oi)
        return deps

    def _record_access(self, op):
        for w in op.writes:
            lst = self.records.setdefault(w.name, [])
            lst[:] = [rec for rec in lst if not w.covers(rec[0])]
            lst.append((w, op.idx, True))
        for r in op.reads:
            lst = self.records.setdefault(r.name, [])
            lst.append((r, op.idx, False))
        for rs in (op.writes, op.reads):
            for x in rs:
                lst = self.records.get(x.name)
                if lst is not None and len(lst) > 512:
                    k = len(lst) - 384
                    old = lst[:k]
                    u = Reg(old[0][0].space, x.name,
                            min(r.p0 for (r, _, _) in old),
                            max(r.p1 for (r, _, _) in old),
                            min(r.b0 for (r, _, _) in old),
                            max(r.b1 for (r, _, _) in old))
                    mi = max(oi for (_, oi, _) in old)
                    lst[:k] = [(u, mi, True)]

    @staticmethod
    def _to_reg(x):
        r = x.reg if isinstance(x, TV) else x
        if r.space == "PS":
            return Reg("PS", r.name, 0, 128, 0, 1 << 30)
        return r

    def add(self, engine, emit, reads=(), writes=(), is_dma=False, tag=""):
        assert not self._finalized
        regs_r = [self._to_reg(x) for x in reads]
        regs_w = [self._to_reg(x) for x in writes]
        regs_w = regs_w + [r for r in regs_r if r.space == "PS"]
        op = Op(len(self.ops), engine, emit, regs_r, regs_w, is_dma, tag)
        op.deps = self._dep_events(op)
        lb = getattr(self, "_last_barrier", None)
        if lb is not None and engine in lb:
            op.deps.add(lb[engine])
        self.ops.append(op)
        self.per_engine[engine].append(op)
        self._record_access(op)
        if is_dma:
            q = engine
            op.use = self.dma_uses[q]
            self.dma_uses[q] += 1
        return op

    def dma(self, queue, out, in_, out_regs=None, in_regs=None, tag="dma", **kw):
        out_ap = out.ap() if isinstance(out, TV) else out
        in_ap = in_.ap() if isinstance(in_, TV) else in_
        reads = in_regs if in_regs is not None else ([in_] if isinstance(in_, TV) else [])
        writes = out_regs if out_regs is not None else ([out] if isinstance(out, TV) else [])
        return self.add(queue, lambda e: e.dma_start(out=out_ap, in_=in_ap, **kw),
                        reads=reads, writes=writes, is_dma=True, tag=tag)

    def mm(self, out, lhsT, rhs, start, stop, tag="mm", **kw):
        o, l, r = out.ap(), lhsT.ap(), rhs.ap()
        return self.add("tensor", lambda e: e.matmul(o, l, r, start=start, stop=stop, **kw),
                        reads=[lhsT, rhs], writes=[out], tag=tag)

    def transp(self, out, in_, ident, tag="tr"):
        o, i, idn = out.ap(), in_.ap(), ident.ap()
        return self.add("tensor", lambda e: e.transpose(o, i, idn),
                        reads=[in_, ident], writes=[out], tag=tag)

    def act(self, out, in_, func, bias=0.0, scale=1.0, accum_out=None, tag="act"):
        o, i = out.ap(), in_.ap()
        b = bias.ap() if isinstance(bias, TV) else np.float32(bias)
        s = scale.ap() if isinstance(scale, TV) else np.float32(scale)
        ao = accum_out.ap() if isinstance(accum_out, TV) else None
        reads = [in_] + [x for x in (bias, scale) if isinstance(x, TV)]
        writes = [out] + ([accum_out] if accum_out is not None else [])
        return self.add("scalar",
                        lambda e: e.activation(o, i, func, bias=b, scale=s, accum_out=ao),
                        reads=reads, writes=writes, tag=tag)

    def v(self, meth, out, ins, tag="v", engine="vector", **kw):
        o = out.ap() if isinstance(out, TV) else out
        aps = [x.ap() if isinstance(x, TV) else x for x in ins]
        reads = [x for x in ins if isinstance(x, TV)]
        kw2 = {k: (v.ap() if isinstance(v, TV) else v) for k, v in kw.items()}
        reads += [v for v in kw.values() if isinstance(v, TV)]
        return self.add(engine, lambda e: getattr(e, meth)(o, *aps, **kw2),
                        reads=reads, writes=[out] if isinstance(out, TV) else [], tag=tag)

    def raw(self, engine, emit, reads=(), writes=(), tag="raw"):
        return self.add(engine, emit, reads=reads, writes=writes, tag=tag)

    def barrier(self, tag="bar"):
        last = {}
        for e in ENGINES:
            if self.per_engine[e]:
                last[e] = self.per_engine[e][-1].idx
        dma_last = {}
        for op in self.ops:
            if op.is_dma:
                dma_last[(op.engine, op.use % self.dma_pool)] = op.idx
        dep_idxs = set(last.values()) | set(dma_last.values())
        if not hasattr(self, "_last_barrier"):
            self._last_barrier = {}
        for e in ENGINES:
            op = self.add(e, lambda eng: eng.nop(), reads=(), writes=(),
                          tag=tag + "_" + e)
            op.deps |= {d for d in dep_idxs if d != op.idx}
            self._last_barrier[e] = op.idx

    def _schedule(self):
        import heapq
        n = len(self.ops)
        children = [[] for _ in range(n)]
        ndeps = [0] * n
        for op in self.ops:
            ds = {d for d in op.deps if d != op.idx}
            ndeps[op.idx] = len(ds)
            for d in ds:
                children[d].append(op.idx)
        DUR = {"tensor": 220.0, "vector": 420.0, "scalar": 450.0,
               "gpsimd": 600.0, "sync": 80.0}
        DMA_ISSUE, DMA_LAT = 150.0, 2500.0
        eng_free = {e: 0.0 for e in ENGINES}
        ready = {e: [] for e in ENGINES}
        done_t = [0.0] * n
        for op in self.ops:
            if ndeps[op.idx] == 0:
                heapq.heappush(ready[op.engine], (0.0, op.idx))
        order = []
        remaining = n
        while remaining > 0:
            best = None
            for e in ENGINES:
                if ready[e]:
                    rt, idx = ready[e][0]
                    st = max(rt, eng_free[e])
                    if best is None or st < best[0] or (st == best[0] and idx < best[2]):
                        best = (st, e, idx)
            st, e, idx = best
            heapq.heappop(ready[e])
            op = self.ops[idx]
            order.append(idx)
            if op.is_dma:
                eng_free[e] = st + DMA_ISSUE
                fin = st + DMA_ISSUE + DMA_LAT
            else:
                fin = st + DUR[e]
                eng_free[e] = fin
            done_t[idx] = fin
            remaining -= 1
            for c in children[idx]:
                ndeps[c] -= 1
                if ndeps[c] == 0:
                    rt = 0.0
                    for dd in self.ops[c].deps:
                        if dd != c:
                            rt = max(rt, done_t[dd])
                    heapq.heappush(ready[self.ops[c].engine], (rt, c))
        self.per_engine = {e: [] for e in ENGINES}
        self.dma_uses = {q: 0 for q in DMA_QUEUES}
        for idx in order:
            op = self.ops[idx]
            self.per_engine[op.engine].append(op)
            if op.is_dma:
                op.use = self.dma_uses[op.engine]
                self.dma_uses[op.engine] += 1
        self._sched_order = order

    def finalize(self, sems, schedule=True):
        assert not self._finalized
        if schedule and SCHED:
            self._schedule()
        for op in self.ops:
            for d in op.deps:
                p = self.ops[d]
                if p.is_dma:
                    continue
                if p.engine == "tensor" and op.engine == "tensor":
                    continue
                p.inc = True
        op_iter = ([self.ops[i] for i in self._sched_order]
                   if hasattr(self, "_sched_order") else self.ops)
        inc_no = {e: 0 for e in ENGINES}
        for op in op_iter:
            if op.is_dma:
                q = op.engine
                op.slot = (q, op.use % self.dma_pool, op.use // self.dma_pool)
            if op.inc and not op.is_dma:
                inc_no[op.engine] += 1
                op.inc_no = inc_no[op.engine]
        clocks = {e: {} for e in ENGINES}
        snap = {}

        def key_of(p):
            if p.is_dma:
                return ("dma", p.slot[0], p.slot[1])
            return ("eng", p.engine)

        def val_of(p):
            if p.is_dma:
                return 16 * (p.slot[2] + 1)
            return getattr(p, "inc_no", None)

        for op in op_iter:
            clk = clocks[op.engine]
            if op.is_dma:
                q, s, u = op.slot
                if u > 0:
                    k = ("dma", q, s)
                    if clk.get(k, 0) < 16 * u:
                        op.waits.append((k, 16 * u))
                        clk[k] = 16 * u
            for d in sorted(op.deps):
                p = self.ops[d]
                if p.engine == op.engine and not p.is_dma:
                    if p.engine == "tensor" or RELAX_SAME_ENGINE:
                        continue
                k = key_of(p)
                v = val_of(p)
                if v is None:
                    continue
                if clk.get(k, 0) < v:
                    op.waits.append((k, v))
                    clk[k] = v
                    ps = snap.get(d)
                    if ps:
                        for kk, vv in ps.items():
                            if clk.get(kk, 0) < vv:
                                clk[kk] = vv
            if op.waits or op.is_dma or True:
                snap[op.idx] = dict(clk)
                if op.is_dma:
                    snap[op.idx][("dma", op.slot[0], op.slot[1])] = 16 * (op.slot[2] + 1)
                elif getattr(op, "inc_no", None):
                    snap[op.idx][("eng", op.engine)] = op.inc_no
        self._finalized = True
        self.sems = sems

    def emit_all(self, block):
        sems = self.sems

        def emit_engine(engname):
            def body(eng):
                for op in self.per_engine[engname]:
                    for (k, v) in op.waits:
                        eng.wait_ge(sems[k], v)
                    inst = op.emit(eng)
                    if op.is_dma:
                        q, s, u = op.slot
                        inst.then_inc(sems[("dma", q, s)], 16)
                    elif op.inc:
                        inst.then_inc(sems[("eng", op.engine)], 1)
            return body

        for e in ENGINES:
            getattr(block, e)(emit_engine(e))

    def final_waits(self):
        out = []
        for q in DMA_QUEUES:
            n = self.dma_uses[q]
            for s in range(self.dma_pool):
                uses = (n - s + self.dma_pool - 1) // self.dma_pool
                if uses > 0:
                    out.append((("dma", q, s), 16 * uses))
        return out


def make_sems(nc, ctx, dma_pool=12):
    sems = {}
    for e in ENGINES:
        sems[("eng", e)] = ctx.enter_context(nc.semaphore(f"se_{e}"))
    for q in DMA_QUEUES:
        for i in range(dma_pool):
            sems[("dma", q, i)] = ctx.enter_context(nc.semaphore(f"sd_{q}{i}"))
    return sems



def sb(nc, name, p, cols, dtype):
    return TV(nc.alloc_sbuf_tensor(name, [p, cols], dtype), space="SB")


def ps(nc, name, p, cols, dtype=F32):
    return TV(nc.alloc_psum_tensor(name, [p, cols], dtype), space="PS")


def rsqrt_eps(P, out, var, eps=LN_EPS, tag="rsqrt"):
    P.act(out, var, AF.Ln, bias=eps, tag=tag + "_ln")
    P.act(out, out, AF.Exp, scale=-0.5, tag=tag + "_exp")


def ln_stats(P, x, mu, rstd, scr, n_inv, eps=LN_EPS, tag="ln"):
    P.v("reduce_sum", mu, [x], axis=AX.X, tag=tag + "_sum")
    P.v("tensor_scalar", mu, [mu], scalar1=n_inv, scalar2=None,
        op0=ALU.mult, tag=tag + "_mu")
    P.v("tensor_scalar", scr, [x], scalar1=mu, scalar2=None,
        op0=ALU.subtract, tag=tag + "_ctr")
    P.v("tensor_tensor", scr, [scr, scr], op=ALU.mult, tag=tag + "_sq")
    P.v("reduce_sum", rstd, [scr], axis=AX.X, tag=tag + "_vsum")
    P.v("tensor_scalar", rstd, [rstd], scalar1=n_inv, scalar2=None,
        op0=ALU.mult, tag=tag + "_var")
    rsqrt_eps(P, rstd, rstd, eps=eps, tag=tag + "_rs")


class Common:

    def __init__(self, nc, P):
        self.ones = sb(nc, "c_ones", 128, 128, F32)
        self.identb = sb(nc, "c_identb", 128, 128, BF16)
        P.v("memset", self.ones, [1.0], tag="ones", engine="gpsimd")
        P.v("memset", self.identb, [1.0], tag="identb", engine="gpsimd")
        idap = self.identb.ap()
        P.add("gpsimd", lambda e: e.affine_select(
            out=idap, in_=idap, pattern=[[-1, 128]], compare_op=ALU.is_equal,
            fill=0.0, base=0, channel_multiplier=1),
            reads=[self.identb], writes=[self.identb], tag="ident_sel")


def transpose_128(P, com, out_ps, xin, tag="tr"):
    P.transp(out_ps, xin, com.identb, tag=tag)




def emit_l1ffn(P, nc, com, resid_in, out_hbm, w1t_h, b1_h, w2t_h, b2_h, pfx="f"):
    EPSQ = 1e-5
    CH = 256
    NCH = T_LOC // CH
    NSUB = CH // 128
    wq1 = TV(nc.alloc_sbuf_tensor(pfx + "wq1", [128, 8, 4096], BF16))
    wq2 = TV(nc.alloc_sbuf_tensor(pfx + "wq2", [128, 32, 1024], BF16))
    arena = nc.alloc_sbuf_tensor(pfx + "arena", [128, 16384], mybir.dt.uint8)
    wld = arena_tv(arena, 0, [128, 8 * 512], F32)
    wld2 = arena_tv(arena, 0, [128, 32 * 128], F32)
    xch = arena_tv(arena, 0, [128, 2 * 1024], F32)
    ych = arena_tv(arena, 8192, [128, 2 * 1024], F32)
    wldh = arena_tv(arena, 0, [128, 8 * 256], F32)
    wld2h = arena_tv(arena, 0, [128, 32 * 64], F32)
    qm2 = arena_tv(arena, 8192, [128, 8 * 256], F32)
    qm2b = arena_tv(arena, 8192, [128, 32 * 64], F32)
    b1t = TV(nc.alloc_sbuf_tensor(pfx + "b1t", [128, 32], F32))
    b2bc = TV(nc.alloc_sbuf_tensor(pfx + "b2bc", [128, 1024], F32))
    stats = TV(nc.alloc_sbuf_tensor(pfx + "stats", [128, 64], F32))
    gsc = TV(nc.alloc_sbuf_tensor(pfx + "gsc", [128, 32], F32))
    xln = TV(nc.alloc_sbuf_tensor(pfx + "xln", [128, 2, 1024], BF16))
    xlnT = TV(nc.alloc_sbuf_tensor(pfx + "xlnT", [128, 2, 8, 256], BF16))
    hT = TV(nc.alloc_sbuf_tensor(pfx + "hT", [128, 32, 256], BF16))
    scr = TV(nc.alloc_sbuf_tensor(pfx + "scr", [128, 1024], F32))

    ps_tr = ps(nc, pfx + "ps_tr", 128, 512, BF16)
    ps_tr2 = ps(nc, pfx + "ps_tr2", 128, 512, BF16)
    ps_h = [ps(nc, pfx + f"ps_h{i}", 128, 512, F32) for i in range(2)]
    ps_y = [ps(nc, pfx + f"ps_y{i}", 128, 512, F32) for i in range(2)]
    ps_g = ps(nc, pfx + "ps_g", 128, 128, F32)

    w1t_ap = w1t_h.ap().rearrange("(a p) f -> p a f", p=128)
    w2t_ap = w2t_h.ap().rearrange("(a p) f -> p a f", p=128)

    wldg = [arena_tv(arena, 0, [128, 8 * 256], F32),
            arena_tv(arena, 8192, [128, 8 * 256], F32)]
    wld2g = [arena_tv(arena, 0, [128, 32 * 64], F32),
             arena_tv(arena, 8192, [128, 32 * 64], F32)]
    for ci in range(16):
        wv_ = wldg[ci % 2]
        P.dma("sync", wv_, w1t_ap[:, :, ci * 256:(ci + 1) * 256],
              in_regs=[dram_reg(w1t_h)], tag=f"w1g{ci}")
        P.v("reduce_sum", gsc[:, ci:ci + 1], [wv_], axis=AX.X,
            apply_absolute_value=True, tag=f"w1abs{ci}")
    for ci in range(16):
        wv_ = wld2g[ci % 2]
        P.dma("sync", wv_, w2t_ap[:, :, ci * 64:(ci + 1) * 64],
              in_regs=[dram_reg(w2t_h)], tag=f"w2g{ci}")
        P.v("reduce_sum", gsc[:, 16 + ci:17 + ci], [wv_], axis=AX.X,
            apply_absolute_value=True, tag=f"w2abs{ci}")
    P.mm(TV(ps_g.h, (0, 1), (0, 32), "PS"), com.ones[:, 0:1], gsc[:, 0:32],
         start=True, stop=True, tag="gsum")
    P.v("tensor_copy", stats[0:1, 0:32], [TV(ps_g.h, (0, 1), (0, 32), "PS")], tag="gevac")
    P.v("reduce_sum", stats[0:1, 32:33], [stats[0:1, 0:16]], axis=AX.X, tag="g1")
    P.v("reduce_sum", stats[0:1, 33:34], [stats[0:1, 16:32]], axis=AX.X, tag="g2")
    P.v("tensor_scalar", stats[0:1, 32:33], [stats[0:1, 32:33]],
        scalar1=1.0 / (4096 * 1024), scalar2=EPSQ, op0=ALU.mult, op1=ALU.add, tag="g1m")
    P.v("tensor_scalar", stats[0:1, 33:34], [stats[0:1, 33:34]],
        scalar1=1.0 / (4096 * 1024), scalar2=EPSQ, op0=ALU.mult, op1=ALU.add, tag="g2m")
    P.v("tensor_scalar", stats[0:1, 34:35], [stats[0:1, 32:33]],
        scalar1=0.5, scalar2=None, op0=ALU.mult, tag="g1t")
    P.v("tensor_scalar", stats[0:1, 35:36], [stats[0:1, 33:34]],
        scalar1=0.5, scalar2=None, op0=ALU.mult, tag="g2t")
    P.v("tensor_scalar", stats[0:1, 36:38], [stats[0:1, 34:36]],
        scalar1=-1.0, scalar2=None, op0=ALU.mult, tag="gneg")
    P.mm(TV(ps_g.h, (0, 128), (0, 4), "PS"), com.ones[0:1, 0:128], stats[0:1, 34:38],
         start=True, stop=True, tag="gbc")
    sbc = stats[:, 40:44]
    P.v("tensor_copy", sbc, [TV(ps_g.h, (0, 128), (0, 4), "PS")], tag="gbce")

    qstg = nc.alloc_sbuf_tensor(pfx + "qstg", [128, 16384], mybir.dt.uint8)
    wqs = [(arena_tv(arena, 0, [128, 8 * 256], F32), arena_tv(arena, 8192, [128, 8 * 256], F32)),
           (arena_tv(qstg, 0, [128, 8 * 256], F32), arena_tv(qstg, 8192, [128, 8 * 256], F32))]
    w2qs = [(arena_tv(arena, 0, [128, 32 * 64], F32), arena_tv(arena, 8192, [128, 32 * 64], F32)),
            (arena_tv(qstg, 0, [128, 32 * 64], F32), arena_tv(qstg, 8192, [128, 32 * 64], F32))]
    for ci in range(16):
        wv_, qv_ = wqs[ci % 2]
        P.dma("sync", wv_, w1t_ap[:, :, ci * 256:(ci + 1) * 256],
              in_regs=[dram_reg(w1t_h)], tag=f"w1q{ci}")
        P.v("tensor_scalar", qv_, [wv_], scalar1=sbc[:, 2:3], scalar2=None,
            op0=ALU.is_le, tag=f"q1a{ci}")
        dap = wq1.h.ap()[:, :, ci * 256:(ci + 1) * 256]
        P.add("vector", lambda e, dap=dap, sap=wv_.ap(), m2ap=qv_.ap(), th=sbc[:, 0:1].ap():
              e.scalar_tensor_tensor(
            out=dap, in0=sap, scalar=th, in1=m2ap, op0=ALU.is_ge, op1=ALU.subtract),
            reads=[wv_, qv_, sbc], writes=[wq1], tag=f"q1c{ci}")
    for ci in range(16):
        wv_, qv_ = w2qs[ci % 2]
        P.dma("sync", wv_, w2t_ap[:, :, ci * 64:(ci + 1) * 64],
              in_regs=[dram_reg(w2t_h)], tag=f"w2q{ci}")
        P.v("tensor_scalar", qv_, [wv_], scalar1=sbc[:, 3:4], scalar2=None,
            op0=ALU.is_le, tag=f"q2a{ci}")
        dap = wq2.h.ap()[:, :, ci * 64:(ci + 1) * 64]
        P.add("vector", lambda e, dap=dap, sap=wv_.ap(), m2ap=qv_.ap(), th=sbc[:, 1:2].ap():
              e.scalar_tensor_tensor(
            out=dap, in0=sap, scalar=th, in1=m2ap, op0=ALU.is_ge, op1=ALU.subtract),
            reads=[wv_, qv_, sbc], writes=[wq2], tag=f"q2c{ci}")

    P.dma("sync", b1t, b1_h.ap(), in_regs=[dram_reg(b1_h)], tag="b1")
    P.dma("sync", b2bc, b2_h.ap().unsqueeze(0).partition_broadcast(128),
          in_regs=[dram_reg(b2_h)], tag="b2")

    rin_ap = resid_in.ap().rearrange("(c p) d -> p c d", p=128)
    rout_ap = out_hbm.ap().rearrange("(c p) d -> p c d", p=128)

    for ch in range(NCH):
        P.dma("sync", xch, rin_ap[:, ch * NSUB:(ch + 1) * NSUB, :],
              in_regs=[dram_reg(resid_in, ch * CH, (ch + 1) * CH)], tag=f"x{ch}")
        for c in range(NSUB):
            x_t = xch[:, c * 1024:(c + 1) * 1024]
            mu = stats[:, 32 + c * 2:33 + c * 2]
            rs_ = stats[:, 33 + c * 2:34 + c * 2]
            ln_stats(P, x_t, mu, rs_, scr, 1.0 / 1024, tag=f"ln{ch}_{c}")
            P.v("tensor_scalar", xln[:, c * 1024:(c + 1) * 1024], [x_t],
                scalar1=mu, scalar2=rs_, op0=ALU.subtract, op1=ALU.mult,
                tag=f"lnn{ch}_{c}")
        for c in range(NSUB):
            for dt_ in range(8):
                trb = ps_tr if (2 * c + dt_ // 4) % 2 == 0 else ps_tr2
                transpose_128(P, com,
                              TV(trb.h, (0, 128), ((dt_ % 4) * 128, (dt_ % 4 + 1) * 128), "PS"),
                              xln[:, c * 1024 + dt_ * 128: c * 1024 + (dt_ + 1) * 128],
                              tag=f"tr{ch}_{c}_{dt_}")
                if dt_ % 4 == 3:
                    grp = dt_ // 4
                    dap = xlnT.h.ap()[:, ch % 2, grp * 4:(grp + 1) * 4, c * 128:(c + 1) * 128]
                    pap = trb.ap()
                    P.add("vector", lambda e, dap=dap, pap=pap: e.tensor_copy(out=dap, in_=pap),
                          reads=[trb],
                          writes=[TV(xlnT.h, (0, 128), ((ch % 2) * 2048 + grp * 4 * 256,
                                                        (ch % 2) * 2048 + (grp * 4 + 4) * 256))],
                          tag=f"tre{ch}_{c}_{grp}")
        for ft in range(32):
            bank = TV(ps_h[ft % 2].h, (0, 128), (0, CH), "PS")
            for dt_ in range(8):
                P.mm(bank, TV(wq1.h, (0, 128), (dt_ * 4096 + ft * 128, dt_ * 4096 + (ft + 1) * 128)),
                     TV(xlnT.h, (0, 128), ((ch % 2) * 2048 + dt_ * 256, (ch % 2) * 2048 + (dt_ + 1) * 256)),
                     start=(dt_ == 0), stop=(dt_ == 7), tag=f"h{ch}_{ft}")
            P.act(hT[:, ft * 256:(ft + 1) * 256], bank, AF.Relu,
                  bias=b1t[:, ft:ft + 1], tag=f"he{ch}_{ft}")
        for c in range(NSUB):
            for half in range(2):
                bank = ps_y[half]
                for ft in range(32):
                    P.mm(bank, TV(hT.h, (0, 128), (ft * 256 + c * 128, ft * 256 + (c + 1) * 128)),
                         TV(wq2.h, (0, 128), (ft * 1024 + half * 512, ft * 1024 + (half + 1) * 512)),
                         start=(ft == 0), stop=(ft == 31), tag=f"y{ch}_{c}_{half}")
                P.v("scalar_tensor_tensor",
                    ych[:, c * 1024 + half * 512:c * 1024 + (half + 1) * 512],
                    [bank], scalar=0.0, in1=b2bc[:, half * 512:(half + 1) * 512],
                    op0=ALU.add, op1=ALU.add, tag=f"ye{ch}_{c}_{half}")
        for c in range(NSUB):
            y_t = ych[:, c * 1024:(c + 1) * 1024]
            mu = stats[:, 40 + c * 2:41 + c * 2]
            rs_ = stats[:, 41 + c * 2:42 + c * 2]
            ln_stats(P, y_t, mu, rs_, scr, 1.0 / 1024, tag=f"lo{ch}_{c}")
            P.v("tensor_scalar", y_t, [y_t], scalar1=mu, scalar2=rs_,
                op0=ALU.subtract, op1=ALU.mult, tag=f"lon{ch}_{c}")
            P.v("tensor_tensor", xch[:, c * 1024:(c + 1) * 1024],
                [y_t, xch[:, c * 1024:(c + 1) * 1024]], op=ALU.add, tag=f"oadd{ch}_{c}")
        P.dma("gpsimd", rout_ap[:, ch * NSUB:(ch + 1) * NSUB, :], xch,
              out_regs=[dram_reg(out_hbm, ch * CH, (ch + 1) * CH)],
              in_regs=[xch], tag=f"o{ch}")



def emit_moe(P, nc, com, resid_in, out_hbm, gate_h, w1_h, w3_h, w2_h, pfx="m"):
    TH = 2048
    NCH = TH // 512
    xmT = TV(nc.alloc_sbuf_tensor(pfx + "xmT", [128, 8, TH], BF16))
    arena = nc.alloc_sbuf_tensor(pfx + "arena", [128, 16384], mybir.dt.uint8)
    xch = arena_tv(arena, 0, [128, 4 * 1024], F32)
    xlnt = TV(nc.alloc_sbuf_tensor(pfx + "xlnt", [128, 4, 1024], BF16))
    wbuf = nc.alloc_sbuf_tensor(pfx + "wbuf", [128, 49152], mybir.dt.uint8)
    def wtv(par, which):
        off = par * 24576 + which * 8192
        return arena_tv(wbuf, off, [128, 4096], BF16)
    s1 = TV(nc.alloc_sbuf_tensor(pfx + "s1", [128, 2, 4, 512], BF16))
    s3 = TV(nc.alloc_sbuf_tensor(pfx + "s3", [128, 4, 512], BF16))
    acc = TV(nc.alloc_sbuf_tensor(pfx + "acc", [128, 16, 1024], F32))
    comb = TV(nc.alloc_sbuf_tensor(pfx + "comb", [128, 16, 16], F32))
    scs = TV(nc.alloc_sbuf_tensor(pfx + "scs", [128, 16, 16], F32))
    gsc = TV(nc.alloc_sbuf_tensor(pfx + "gscr", [128, 32], F32))
    gatesb = TV(nc.alloc_sbuf_tensor(pfx + "gate", [128, 8, 16], BF16))
    stats = TV(nc.alloc_sbuf_tensor(pfx + "stats", [128, 32], F32))
    scr = TV(nc.alloc_sbuf_tensor(pfx + "scr", [128, 1024], F32))
    ob = TV(nc.alloc_sbuf_tensor(pfx + "ob", [128, 4, 1024], F32))
    maski = TV(nc.alloc_sbuf_tensor(pfx + "maski", [128, 16], I32))
    negt = TV(nc.alloc_sbuf_tensor(pfx + "negt", [128, 16], F32))
    P.v("memset", negt, [-1e9], tag=pfx + "neg", engine="gpsimd")
    pb = [ps(nc, pfx + f"pb{i}", 128, 512, F32) for i in range(8)]

    rin_ap = resid_in.ap().rearrange("(c p) d -> p c d", p=128)
    rout_ap = out_hbm.ap().rearrange("(c p) d -> p c d", p=128)

    P.dma("gpsimd", gatesb, gate_h.ap().rearrange("(a p) e -> p a e", p=128),
          in_regs=[dram_reg(gate_h)], tag="gate")

    for half in range(2):
        tb = half * TH
        cb = half * TH // 128
        for ch in range(NCH):
            P.dma("sync", xch, rin_ap[:, cb + ch * 4:cb + (ch + 1) * 4, :],
                  in_regs=[dram_reg(resid_in, tb + ch * 512, tb + (ch + 1) * 512)],
                  tag=f"h{half}x{ch}")
            for c in range(4):
                x_t = xch[:, c * 1024:(c + 1) * 1024]
                mu = stats[:, c * 2:c * 2 + 1]
                rs_ = stats[:, c * 2 + 1:c * 2 + 2]
                ln_stats(P, x_t, mu, rs_, scr, 1.0 / 1024, tag=f"h{half}ln{ch}_{c}")
                P.v("tensor_scalar", xlnt[:, c * 1024:(c + 1) * 1024], [x_t],
                    scalar1=mu, scalar2=rs_, op0=ALU.subtract, op1=ALU.mult,
                    tag=f"h{half}lnn{ch}_{c}")
            for c in range(4):
                for dt_ in range(8):
                    pv = TV(pb[7 if (2 * c + dt_ // 4) % 2 == 0 else 6].h,
                            (0, 128), (0, 1024), "PS", vdt=BF16)
                    transpose_128(P, com, pv[:, (dt_ % 4) * 128:(dt_ % 4 + 1) * 128],
                                  xlnt[:, c * 1024 + dt_ * 128:c * 1024 + (dt_ + 1) * 128],
                                  tag=f"h{half}tr{ch}_{c}_{dt_}")
                    if dt_ % 4 == 3:
                        grp = dt_ // 4
                        dap = xmT.h.ap()[:, grp * 4:(grp + 1) * 4,
                                         ch * 512 + c * 128:ch * 512 + (c + 1) * 128]
                        pap = pv[:, 0:512].ap()
                        P.add("vector", lambda e, dap=dap, pap=pap: e.tensor_copy(out=dap, in_=pap),
                              reads=[pv], writes=[TV(xmT.h, (0, 128),
                                                     (grp * 4 * TH, (grp * 4 + 4) * TH))],
                              tag=f"h{half}tre{ch}_{c}_{grp}")
        for tt_ in range(16):
            pg = TV(pb[6].h, (0, 128), (0, 16), "PS")
            for dt_ in range(8):
                P.mm(pg, TV(xmT.h, (0, 128), (dt_ * TH + tt_ * 128, dt_ * TH + (tt_ + 1) * 128)),
                     gatesb[:, dt_ * 16:(dt_ + 1) * 16],
                     start=(dt_ == 0), stop=(dt_ == 7), tag=f"h{half}g{tt_}")
            sc_t = scs[:, tt_ * 16:(tt_ + 1) * 16]
            P.act(sc_t, pg, AF.Tanh, scale=0.5, tag=f"h{half}gs{tt_}")
            P.v("tensor_scalar", sc_t, [sc_t], scalar1=0.5, scalar2=0.5,
                op0=ALU.mult, op1=ALU.add, tag=f"h{half}gss{tt_}")
            emit_gate_combine(P, nc, com, scs, comb, gsc, maski, negt, tt_, tag=f"h{half}gc{tt_}")
        for e in range(18):
            par = e % 2
            w1b, w3b, w2b = wtv(par, 0), wtv(par, 1), wtv(par, 2)
            for (wb, wh, wtag) in ((w1b, w1_h, "w1"), (w3b, w3_h, "w3"), (w2b, w2_h, "w2")):
                src = wh.ap()[e].rearrange("(a p) f -> p a f", p=128)
                P.dma("gpsimd", wb, src, in_regs=[dram_reg(wh, e, e + 1)],
                      tag=f"h{half}{wtag}e{e}")
            def emit_y(ch):
                par2 = ch % 2
                for tsub in range(4):
                    ttile = ch * 4 + tsub
                    for h2 in range(2):
                        bank = pb[h2]
                        for ft in range(4):
                            P.mm(bank, TV(s1.h, (0, 128),
                                          (par2 * 2048 + ft * 512 + tsub * 128,
                                           par2 * 2048 + ft * 512 + (tsub + 1) * 128)),
                                 w2b[:, ft * 1024 + h2 * 512:ft * 1024 + (h2 + 1) * 512],
                                 start=(ft == 0), stop=(ft == 3), tag=f"h{half}e{e}c{ch}y")
                        dst = acc[:, ttile * 1024 + h2 * 512:ttile * 1024 + (h2 + 1) * 512]
                        if e == 0:
                            P.v("tensor_scalar", dst, [bank],
                                scalar1=comb[:, ttile * 16 + e:ttile * 16 + e + 1],
                                scalar2=None, op0=ALU.mult,
                                tag=f"h{half}e{e}c{ch}a")
                        elif e < 16:
                            P.v("scalar_tensor_tensor", dst, [bank],
                                scalar=comb[:, ttile * 16 + e:ttile * 16 + e + 1],
                                in1=dst, op0=ALU.mult, op1=ALU.add,
                                tag=f"h{half}e{e}c{ch}a")
                        else:
                            P.v("scalar_tensor_tensor", dst, [bank],
                                scalar=1.0, in1=dst, op0=ALU.mult, op1=ALU.add,
                                tag=f"h{half}e{e}c{ch}a")
            for ch in range(NCH):
                par2 = ch % 2
                for ft in range(4):
                    for dt_ in range(8):
                        P.mm(pb[2 + ft], w1b[:, dt_ * 512 + ft * 128:dt_ * 512 + (ft + 1) * 128],
                             TV(xmT.h, (0, 128), (dt_ * TH + ch * 512, dt_ * TH + (ch + 1) * 512)),
                             start=(dt_ == 0), stop=(dt_ == 7), tag=f"h{half}e{e}c{ch}h1")
                    P.act(TV(s1.h, (0, 128), (par2 * 2048 + ft * 512, par2 * 2048 + (ft + 1) * 512)),
                          pb[2 + ft], AF.Silu, tag=f"h{half}e{e}c{ch}s1_{ft}")
                for ft in range(4):
                    for dt_ in range(8):
                        P.mm(pb[(6 + ft) % 8 if ft < 2 else 6 + ft - 2] if False else pb[6 if ft % 2 == 0 else 7],
                             w3b[:, dt_ * 512 + ft * 128:dt_ * 512 + (ft + 1) * 128],
                             TV(xmT.h, (0, 128), (dt_ * TH + ch * 512, dt_ * TH + (ch + 1) * 512)),
                             start=(dt_ == 0), stop=(dt_ == 7), tag=f"h{half}e{e}c{ch}h3")
                    P.v("tensor_copy", s3[:, ft * 512:(ft + 1) * 512],
                        [pb[6 if ft % 2 == 0 else 7]],
                        tag=f"h{half}e{e}c{ch}s3_{ft}")
                for ft in range(4):
                    P.add("vector", lambda e_, d=TV(s1.h, (0, 128),
                                                    (par2 * 2048 + ft * 512,
                                                     par2 * 2048 + (ft + 1) * 512)).ap(),
                          b2=s3[:, ft * 512:(ft + 1) * 512].ap():
                          e_.tensor_tensor(out=d, in0=d, in1=b2, op=ALU.mult),
                          reads=[TV(s1.h, (0, 128), (par2 * 2048 + ft * 512,
                                                     par2 * 2048 + (ft + 1) * 512)),
                                 s3],
                          writes=[TV(s1.h, (0, 128), (par2 * 2048 + ft * 512,
                                                      par2 * 2048 + (ft + 1) * 512))],
                          op=None, tag=f"h{half}e{e}c{ch}gu{ft}") if False else                     P.v("tensor_tensor",
                        TV(s1.h, (0, 128), (par2 * 2048 + ft * 512, par2 * 2048 + (ft + 1) * 512)),
                        [TV(s1.h, (0, 128), (par2 * 2048 + ft * 512, par2 * 2048 + (ft + 1) * 512)),
                         s3[:, ft * 512:(ft + 1) * 512]],
                        op=ALU.mult, tag=f"h{half}e{e}c{ch}gu{ft}")
                if ch > 0:
                    emit_y(ch - 1)
            emit_y(NCH - 1)
        for ch in range(NCH):
            P.dma("sync", xch, rin_ap[:, cb + ch * 4:cb + (ch + 1) * 4, :],
                  in_regs=[dram_reg(resid_in, tb + ch * 512, tb + (ch + 1) * 512)],
                  tag=f"h{half}xr{ch}")
            for c in range(4):
                ttile = ch * 4 + c
                a_t = acc[:, ttile * 1024:(ttile + 1) * 1024]
                mu = stats[:, 16 + c * 2:17 + c * 2]
                rs_ = stats[:, 17 + c * 2:18 + c * 2]
                ln_stats(P, a_t, mu, rs_, scr, 1.0 / 1024, tag=f"h{half}lo{ch}_{c}")
                P.v("tensor_scalar", a_t, [a_t], scalar1=mu, scalar2=rs_,
                    op0=ALU.subtract, op1=ALU.mult, tag=f"h{half}lon{ch}_{c}")
                P.v("tensor_tensor", ob[:, c * 1024:(c + 1) * 1024],
                    [a_t, xch[:, c * 1024:(c + 1) * 1024]], op=ALU.add,
                    tag=f"h{half}oadd{ch}_{c}")
            P.dma("gpsimd", rout_ap[:, cb + ch * 4:cb + (ch + 1) * 4, :], ob,
                  out_regs=[dram_reg(out_hbm, tb + ch * 512, tb + (ch + 1) * 512)],
                  in_regs=[ob], tag=f"h{half}o{ch}")


def emit_gate_combine(P, nc, com, scs, comb, gsc, maski, negt, tt_, tag="gc"):
    base = tt_ * 16
    scall = TV(scs.h, (0, 128), (base, base + 16))
    msk = TV(comb.h, (0, 128), (base, base + 16))
    g = gsc

    def gcol(i):
        return scs.h.ap().rearrange("p a b -> p (a b)")[:, base:base + 16] \
            .rearrange("p (gg i) -> p gg i", gg=4)[:, :, i:i + 1].squeeze(2)

    rd = [scall, g]
    wr = [g]

    def vv(dst, a, b, op, t):
        P.add("vector", lambda e, d=dst, a=a, b=b, op=op:
              e.tensor_tensor(out=d, in0=a, in1=b, op=op), reads=rd, writes=wr,
              tag=tag + t)

    c0, c1, c2, c3 = gcol(0), gcol(1), gcol(2), gcol(3)
    m2 = g[:, 0:4].ap(); tmp = g[:, 4:8].ap()
    vv(m2, c0, c1, ALU.min, "p01")
    for (a, b, t) in ((c0, c2, "02"), (c0, c3, "03"), (c1, c2, "12"),
                      (c1, c3, "13"), (c2, c3, "23")):
        vv(tmp, a, b, ALU.min, "p" + t)
        vv(m2, m2, tmp, ALU.max, "x" + t)
    m1 = g[:, 4:8].ap()
    vv(m1, c0, c1, ALU.max, "m1a")
    vv(m1, m1, c2, ALU.max, "m1b")
    vv(m1, m1, c3, ALU.max, "m1c")
    gs = g[:, 12:16].ap()
    vv(gs, m1, m2, ALU.add, "gs")
    gg = lambda i: g[:, 12 + i:13 + i].ap()
    t2 = g[:, 16:17].ap(); tm = g[:, 17:18].ap()
    vv(t2, gg(0), gg(1), ALU.min, "q01")
    for (a, b, t) in ((gg(0), gg(2), "02"), (gg(0), gg(3), "03"),
                      (gg(1), gg(2), "12"), (gg(1), gg(3), "13"),
                      (gg(2), gg(3), "23")):
        vv(tm, a, b, ALU.min, "q" + t)
        vv(t2, t2, tm, ALU.max, "y" + t)
    pen = g[:, 18:22].ap()
    P.add("vector", lambda e, d=pen, a=gs, s=t2: e.tensor_scalar(
        out=d, in0=a, scalar1=s, scalar2=None, op0=ALU.is_ge),
        reads=rd, writes=wr, tag=tag + "pen0")
    P.add("vector", lambda e, d=pen: e.tensor_scalar(
        out=d, in0=d, scalar1=1e9, scalar2=-1e9, op0=ALU.mult, op1=ALU.add),
        reads=rd, writes=wr, tag=tag + "pen1")
    for gi in range(4):
        P.add("vector", lambda e, d=msk[:, gi * 4:(gi + 1) * 4].ap(),
              a=scall[:, gi * 4:(gi + 1) * 4].ap(), s=g[:, 18 + gi:19 + gi].ap():
              e.tensor_scalar(out=d, in0=a, scalar1=s, scalar2=None, op0=ALU.add),
              reads=rd, writes=[TV(comb.h, (0, 128), (base, base + 16)), g],
              tag=tag + f"msk{gi}")
    r = g[:, 16:32].ap()
    P.add("vector", lambda e, d=r, a=msk.ap(): e.tensor_copy(out=d, in_=a),
          reads=[msk, g], writes=[g], tag=tag + "rcp")
    mx = g[:, 8:9].ap()
    for it in range(3):
        P.add("vector", lambda e, d=mx, a=r: e.reduce_max(out=d, in_=a, axis=AX.X),
              reads=[g], writes=[g], tag=tag + f"t4m{it}")
        P.add("vector", lambda e, d=maski.ap(), a=r, s=mx: e.tensor_scalar(
            out=d, in0=a, scalar1=s, scalar2=None, op0=ALU.is_ge),
            reads=[g], writes=[maski], tag=tag + f"t4e{it}")
        P.add("vector", lambda e, d=r, m=maski.ap(), nt=negt.ap():
              e.copy_predicated(out=d, mask=m, data=nt),
              reads=[g, maski, negt], writes=[g], tag=tag + f"t4s{it}")
    th4 = g[:, 9:10].ap()
    P.add("vector", lambda e, d=th4, a=r: e.reduce_max(out=d, in_=a, axis=AX.X),
          reads=[g], writes=[g], tag=tag + "th4")
    P.add("vector", lambda e, d=msk.ap(), a=msk.ap(), s=th4: e.tensor_scalar(
        out=d, in0=a, scalar1=s, scalar2=None, op0=ALU.is_ge),
        reads=[msk, g], writes=[msk], tag=tag + "sel")
    P.add("vector", lambda e, d=msk.ap(), a=msk.ap(), b=scall.ap(): e.tensor_tensor(
        out=d, in0=a, in1=b, op=ALU.mult),
        reads=[msk, scall], writes=[msk], tag=tag + "wr")
    dn = g[:, 10:11].ap()
    P.add("vector", lambda e, d=dn, a=msk.ap(): e.reduce_sum(out=d, in_=a, axis=AX.X),
          reads=[msk, g], writes=[g], tag=tag + "dn")
    P.add("vector", lambda e, d=dn: e.tensor_scalar(
        out=d, in0=d, scalar1=1e-6, scalar2=None, op0=ALU.add),
        reads=[g], writes=[g], tag=tag + "dne")
    rc = g[:, 11:12].ap()
    P.add("vector", lambda e, d=rc, a=dn: e.reciprocal(out=d, in_=a),
          reads=[g], writes=[g], tag=tag + "rc")
    P.add("vector", lambda e, d=msk.ap(), a=msk.ap(), s=rc: e.tensor_scalar(
        out=d, in0=a, scalar1=s, scalar2=None, op0=ALU.mult),
        reads=[msk, g], writes=[msk], tag=tag + "cmb")



def emit_attn(P, nc, com, resid_in, out_hbm, wdq_h, wuq_h, wdkv_h, wuk_h,
              wuv_h, wo_h, pfx="a"):
    NB = B_LOC
    wdq = TV(nc.alloc_sbuf_tensor(pfx + "wdq", [128, 8, 256], BF16))
    wdkv = TV(nc.alloc_sbuf_tensor(pfx + "wdkv", [128, 8, 256], BF16))
    wuq = TV(nc.alloc_sbuf_tensor(pfx + "wuq", [128, 2, 1024], BF16))
    wuk = TV(nc.alloc_sbuf_tensor(pfx + "wuk", [128, 2, 1024], BF16))
    wuv = TV(nc.alloc_sbuf_tensor(pfx + "wuv", [128, 2, 1024], BF16))
    wo = TV(nc.alloc_sbuf_tensor(pfx + "wo", [128, 8, 1024], BF16))
    cqT = TV(nc.alloc_sbuf_tensor(pfx + "cqT", [128, 2, T_LOC], BF16))
    ckvT = TV(nc.alloc_sbuf_tensor(pfx + "ckvT", [128, 2, T_LOC], BF16))
    vt = TV(nc.alloc_sbuf_tensor(pfx + "vt", [128, 16, 16, 65], BF16))
    qt = TV(nc.alloc_sbuf_tensor(pfx + "qt", [64, 2048], BF16))
    kt = TV(nc.alloc_sbuf_tensor(pfx + "kt", [64, 2048], BF16))
    et = TV(nc.alloc_sbuf_tensor(pfx + "et", [128, 16, 512], BF16))
    oT = TV(nc.alloc_sbuf_tensor(pfx + "oT", [128, 8, 2048], BF16))
    masks = TV(nc.alloc_sbuf_tensor(pfx + "masks", [128, 4, 512], BF16))
    obuf = TV(nc.alloc_sbuf_tensor(pfx + "obuf", [64, 512], BF16))
    recb = TV(nc.alloc_sbuf_tensor(pfx + "recb", [128, 512], F32))
    arena = nc.alloc_sbuf_tensor(pfx + "arena", [128, 32768], mybir.dt.uint8)
    xch = arena_tv(arena, 0, [128, 4 * 1024], F32)
    ych = arena_tv(arena, 16384, [128, 2 * 1024], F32)
    xlnt = arena_tv(arena, 24576, [128, 2048], BF16)
    scr = arena_tv(arena, 28672, [128, 1024], F32)
    xlnT = TV(nc.alloc_sbuf_tensor(pfx + "xlnT", [128, 8, 512], BF16))
    stats = TV(nc.alloc_sbuf_tensor(pfx + "stats", [128, 32], F32))
    pb = [ps(nc, pfx + f"pb{i}", 128, 512, F32) for i in range(8)]

    rin_ap = resid_in.ap().rearrange("(c p) d -> p c d", p=128)
    rout_ap = out_hbm.ap().rearrange("(c p) d -> p c d", p=128)

    for (wb, wh, n) in ((wdq, wdq_h, "wdq"), (wdkv, wdkv_h, "wdkv")):
        P.dma("gpsimd", wb, wh.ap().rearrange("(a p) f -> p a f", p=128),
              in_regs=[dram_reg(wh)], tag=n)
    for (wb, wh, n) in ((wuq, wuq_h, "wuq"), (wuk, wuk_h, "wuk"), (wuv, wuv_h, "wuv")):
        P.dma("gpsimd", wb, wh.ap().rearrange("(a p) f -> p a f", p=128),
              in_regs=[dram_reg(wh)], tag=n)
    P.dma("gpsimd", wo, wo_h.ap().rearrange("(a p) f -> p a f", p=128),
          in_regs=[dram_reg(wo_h)], tag="wo")
    P.v("memset", masks, [1.0], tag="mkáµ¢", engine="gpsimd")
    for j in range(4):
        dap = masks.h.ap()[:, j, :]
        P.add("gpsimd", lambda e, dap=dap, j=j: e.affine_select(
            out=dap, in_=dap, pattern=[[1, 512]], compare_op=ALU.is_ge,
            fill=0.0, base=-128 * j, channel_multiplier=-1),
            reads=[masks], writes=[masks], tag=f"mask{j}")

    def stage_a(ch):
        P.dma("sync", xch, rin_ap[:, ch * 4:(ch + 1) * 4, :],
              in_regs=[dram_reg(resid_in, ch * 512, (ch + 1) * 512)], tag=f"x{ch}")
        for c in range(4):
            x_t = xch[:, c * 1024:(c + 1) * 1024]
            mu = stats[:, c * 2:c * 2 + 1]
            rs_ = stats[:, c * 2 + 1:c * 2 + 2]
            ln_stats(P, x_t, mu, rs_, scr, 1.0 / 1024, tag=f"ln{ch}_{c}")
            P.v("tensor_scalar", xlnt[:, (c % 2) * 1024:(c % 2) * 1024 + 1024], [x_t],
                scalar1=mu, scalar2=rs_, op0=ALU.subtract, op1=ALU.mult,
                tag=f"lnn{ch}_{c}")
            for dt_ in range(8):
                pv = TV(pb[7 if (2 * c + dt_ // 4) % 2 == 0 else 5].h,
                        (0, 128), (0, 1024), "PS", vdt=BF16)
                transpose_128(P, com, pv[:, (dt_ % 4) * 128:(dt_ % 4 + 1) * 128],
                              xlnt[:, (c % 2) * 1024 + dt_ * 128:(c % 2) * 1024 + (dt_ + 1) * 128],
                              tag=f"tr{ch}_{c}_{dt_}")
                if dt_ % 4 == 3:
                    grp = dt_ // 4
                    dap = xlnT.h.ap()[:, grp * 4:(grp + 1) * 4, c * 128:(c + 1) * 128]
                    pap = pv[:, 0:512].ap()
                    P.add("vector", lambda e, dap=dap, pap=pap: e.tensor_copy(out=dap, in_=pap),
                          reads=[pv], writes=[TV(xlnT.h, (0, 128),
                                                 (grp * 4 * 512, (grp * 4 + 4) * 512))],
                          tag=f"tre{ch}_{c}_{grp}")
        for (dst, wsb, bank) in ((cqT, wdq, 0), (ckvT, wdkv, 2)):
            for ct in range(2):
                pp = pb[bank + ct]
                for dt_ in range(8):
                    P.mm(pp, wsb[:, dt_ * 256 + ct * 128:dt_ * 256 + (ct + 1) * 128],
                         xlnT[:, dt_ * 512:(dt_ + 1) * 512],
                         start=(dt_ == 0), stop=(dt_ == 7), tag=f"c{ch}_{ct}")
                dap = dst.h.ap()[:, ct, ch * 512:(ch + 1) * 512]
                P.add("scalar", lambda e, dap=dap, pap=pp.ap(): e.copy(out=dap, in_=pap),
                      reads=[pp], writes=[TV(dst.h, (0, 128),
                                             (ct * T_LOC + ch * 512, ct * T_LOC + (ch + 1) * 512))],
                      tag=f"ce{ch}_{ct}")

    for b in range(NB):
        for ch in range(4 * b, 4 * b + 4):
            stage_a(ch)
        tb = b * 2048
        onesap = vt.h.ap()[:, :, :, 64:65]
        P.add("vector", lambda e, d=onesap: e.memset(d, 1.0),
              reads=[], writes=[vt], tag=f"b{b}vones")
        for tt_ in range(16):
            for half in range(2):
                pp = pb[6]
                for ct in range(2):
                    P.mm(pp, TV(ckvT.h, (0, 128),
                                (ct * T_LOC + tb + tt_ * 128, ct * T_LOC + tb + (tt_ + 1) * 128)),
                         wuv[:, ct * 1024 + half * 512:ct * 1024 + (half + 1) * 512],
                         start=(ct == 0), stop=(ct == 1), tag=f"b{b}v{tt_}_{half}")
                dap = vt.h.ap()[:, tt_, half * 8:(half + 1) * 8, 0:64]
                pap = pp.ap().rearrange("p (h v) -> p h v", h=8)
                P.add("vector", lambda e, dap=dap, pap=pap: e.tensor_copy(out=dap, in_=pap),
                      reads=[pp], writes=[TV(vt.h, (0, 128),
                                             (tt_ * 16 * 65, (tt_ + 1) * 16 * 65))],
                      tag=f"b{b}ve{tt_}_{half}")
        for h in range(16):
            for (dst, wsb) in ((qt, wuq), (kt, wuk)):
                for qc in range(4):
                    pp = TV(pb[6].h, (0, 64), (0, 512), "PS")
                    for ct in range(2):
                        P.mm(pp, wsb[:, ct * 1024 + h * 64:ct * 1024 + (h + 1) * 64],
                             TV(cqT.h if dst is qt else ckvT.h, (0, 128),
                                (ct * T_LOC + tb + qc * 512, ct * T_LOC + tb + (qc + 1) * 512)),
                             start=(ct == 0), stop=(ct == 1), tag=f"b{b}h{h}qk{qc}")
                    P.add("scalar", lambda e, dap=dst[:, qc * 512:(qc + 1) * 512].ap(),
                          pap=pp.ap(): e.copy(out=dap, in_=pap),
                          reads=[pp], writes=[dst[:, qc * 512:(qc + 1) * 512]],
                          tag=f"b{b}h{h}qke{qc}")
            for qc in range(4):
                nkt = 4 * (qc + 1)
                for kt_ in range(nkt):
                    pp = pb[kt_ % 4]
                    P.mm(pp, kt[:, kt_ * 128:(kt_ + 1) * 128], qt[:, qc * 512:(qc + 1) * 512],
                         start=True, stop=True, tag=f"b{b}h{h}s{qc}_{kt_}")
                    P.act(et[:, kt_ * 512:(kt_ + 1) * 512], pp, AF.Exp,
                          scale=0.125, bias=-20.0, tag=f"b{b}h{h}e{qc}_{kt_}")
                    j = kt_ - 4 * qc
                    if j >= 0:
                        P.v("tensor_tensor", et[:, kt_ * 512:(kt_ + 1) * 512],
                            [et[:, kt_ * 512:(kt_ + 1) * 512], masks[:, j * 512:(j + 1) * 512]],
                            op=ALU.mult, tag=f"b{b}h{h}m{qc}_{kt_}")
                po = TV(pb[4].h, (0, 65), (0, 512), "PS")
                for kt_ in range(nkt):
                    P.mm(po, TV(vt.h, (0, 128), ((kt_ * 16 + h) * 65, (kt_ * 16 + h) * 65 + 65)),
                         et[:, kt_ * 512:(kt_ + 1) * 512],
                         start=(kt_ == 0), stop=(kt_ == nkt - 1), tag=f"b{b}h{h}pv{qc}")
                P.v("reciprocal", TV(recb.h, (64, 65), (0, 512)),
                    [TV(pb[4].h, (64, 65), (0, 512), "PS")], tag=f"b{b}h{h}rc{qc}")
                prd = TV(pb[5].h, (0, 64), (0, 512), "PS")
                P.mm(prd, TV(com.ones.h, (64, 65), (0, 64)),
                     TV(recb.h, (64, 65), (0, 512)), start=True, stop=True,
                     tag=f"b{b}h{h}rb{qc}")
                P.add("scalar", lambda e, dap=obuf.ap(),
                      pap=TV(pb[4].h, (0, 64), (0, 512), "PS").ap(): e.copy(out=dap, in_=pap),
                      reads=[TV(pb[4].h, (0, 65), (0, 512), "PS")], writes=[obuf],
                      tag=f"b{b}h{h}oe{qc}")
                dst = TV(oT.h, ((h % 2) * 64, (h % 2) * 64 + 64),
                         ((h // 2) * 2048 + qc * 512, (h // 2) * 2048 + (qc + 1) * 512))
                P.v("tensor_tensor", dst, [prd, obuf], op=ALU.mult,
                    tag=f"b{b}h{h}od{qc}")
        for ch in range(4):
            P.dma("sync", xch, rin_ap[:, b * 16 + ch * 4:b * 16 + (ch + 1) * 4, :],
                  in_regs=[dram_reg(resid_in, tb + ch * 512, tb + (ch + 1) * 512)],
                  tag=f"b{b}xr{ch}")
            for ts4 in range(4):
                yv = ych[:, (ts4 % 2) * 1024:(ts4 % 2 + 1) * 1024]
                for half in range(2):
                    pp = pb[2 * (ts4 % 2) + half]
                    for dt_ in range(8):
                        P.mm(pp, TV(oT.h, (0, 128),
                                    (dt_ * 2048 + ch * 512 + ts4 * 128,
                                     dt_ * 2048 + ch * 512 + (ts4 + 1) * 128)),
                             wo[:, dt_ * 1024 + half * 512:dt_ * 1024 + (half + 1) * 512],
                             start=(dt_ == 0), stop=(dt_ == 7), tag=f"b{b}wo{ch}_{ts4}")
                    P.v("tensor_copy", yv[:, half * 512:(half + 1) * 512], [pp],
                        tag=f"b{b}woe{ch}_{ts4}_{half}")
                mu = stats[:, 16 + 2 * (ts4 % 2):17 + 2 * (ts4 % 2)]
                rs_ = stats[:, 17 + 2 * (ts4 % 2) + 2:18 + 2 * (ts4 % 2) + 2]
                ln_stats(P, yv, mu, rs_, scr[:, (ts4 % 2) * 512:(ts4 % 2) * 512 + 512] if False else scr,
                         1.0 / 1024, tag=f"b{b}lo{ch}_{ts4}")
                P.v("tensor_scalar", yv, [yv], scalar1=mu,
                    scalar2=rs_, op0=ALU.subtract, op1=ALU.mult, tag=f"b{b}lon{ch}_{ts4}")
                P.v("tensor_tensor", xch[:, ts4 * 1024:(ts4 + 1) * 1024],
                    [yv, xch[:, ts4 * 1024:(ts4 + 1) * 1024]],
                    op=ALU.add, tag=f"b{b}oadd{ch}_{ts4}")
            P.dma("gpsimd", rout_ap[:, b * 16 + ch * 4:b * 16 + (ch + 1) * 4, :], xch,
                  out_regs=[dram_reg(out_hbm, tb + ch * 512, tb + (ch + 1) * 512)],
                  in_regs=[xch], tag=f"b{b}o{ch}")



def emit_ln_stream(P, nc, com, src_h, dst_h, pfx, add_h=None, wvec_h=None):
    xch = TV(nc.alloc_sbuf_tensor(pfx + "xch", [128, 4, 1024], F32))
    ach = TV(nc.alloc_sbuf_tensor(pfx + "ach", [128, 4, 1024], F32)) if add_h is not None else None
    scr = TV(nc.alloc_sbuf_tensor(pfx + "scr", [128, 1024], F32))
    stats = TV(nc.alloc_sbuf_tensor(pfx + "stats", [128, 8], F32))
    wb = None
    if wvec_h is not None:
        wb = TV(nc.alloc_sbuf_tensor(pfx + "wb", [128, 1024], F32))
        P.dma("sync", wb, wvec_h.ap().unsqueeze(0).partition_broadcast(128),
              in_regs=[dram_reg(wvec_h)], tag=pfx + "w")
    src_ap = src_h.ap().rearrange("(c p) d -> p c d", p=128)
    dst_ap = dst_h.ap().rearrange("(c p) d -> p c d", p=128)
    add_ap = add_h.ap().rearrange("(c p) d -> p c d", p=128) if add_h is not None else None
    for ch in range(T_LOC // 512):
        P.dma("sync", xch, src_ap[:, ch * 4:(ch + 1) * 4, :],
              in_regs=[dram_reg(src_h, ch * 512, (ch + 1) * 512)], tag=f"{pfx}x{ch}")
        if add_h is not None:
            P.dma("sync", ach, add_ap[:, ch * 4:(ch + 1) * 4, :],
                  in_regs=[dram_reg(add_h, ch * 512, (ch + 1) * 512)], tag=f"{pfx}a{ch}")
        for c in range(4):
            x_t = xch[:, c * 1024:(c + 1) * 1024]
            mu = stats[:, 0:1]
            rs_ = stats[:, 1:2]
            ln_stats(P, x_t, mu, rs_, scr, 1.0 / 1024, tag=f"{pfx}ln{ch}_{c}")
            P.v("tensor_scalar", x_t, [x_t], scalar1=mu, scalar2=rs_,
                op0=ALU.subtract, op1=ALU.mult, tag=f"{pfx}n{ch}_{c}")
            if wb is not None:
                P.v("tensor_tensor", x_t, [x_t, wb], op=ALU.mult, tag=f"{pfx}w{ch}_{c}")
            if add_h is not None:
                P.v("tensor_tensor", x_t, [x_t, ach[:, c * 1024:(c + 1) * 1024]],
                    op=ALU.add, tag=f"{pfx}ad{ch}_{c}")
        P.dma("gpsimd", dst_ap[:, ch * 4:(ch + 1) * 4, :], xch,
              out_regs=[dram_reg(dst_h, ch * 512, (ch + 1) * 512)],
              in_regs=[xch], tag=f"{pfx}o{ch}")


def emit_rwkv_post(P, nc, com, xi_h, residb_h, out_h, lnoutw_h, pfx="rp"):
    xch = TV(nc.alloc_sbuf_tensor(pfx + "xch", [128, 4, 1024], F32))
    ach = TV(nc.alloc_sbuf_tensor(pfx + "ach", [128, 4, 1024], F32))
    scr = TV(nc.alloc_sbuf_tensor(pfx + "scr", [128, 1024], F32))
    stats = TV(nc.alloc_sbuf_tensor(pfx + "stats", [128, 8], F32))
    wb = TV(nc.alloc_sbuf_tensor(pfx + "wb", [128, 1024], F32))
    P.dma("sync", wb, lnoutw_h.ap().unsqueeze(0).partition_broadcast(128),
          in_regs=[dram_reg(lnoutw_h)], tag=pfx + "w")
    src_ap = xi_h.ap().rearrange("(c p) d -> p c d", p=128)
    add_ap = residb_h.ap().rearrange("(c p) d -> p c d", p=128)
    dst_ap = out_h.ap().rearrange("(c p) d -> p c d", p=128)
    for ch in range(T_LOC // 512):
        P.dma("sync", xch, src_ap[:, ch * 4:(ch + 1) * 4, :],
              in_regs=[dram_reg(xi_h, ch * 512, (ch + 1) * 512)], tag=f"{pfx}x{ch}")
        P.dma("sync", ach, add_ap[:, ch * 4:(ch + 1) * 4, :],
              in_regs=[dram_reg(residb_h, ch * 512, (ch + 1) * 512)], tag=f"{pfx}a{ch}")
        for c in range(4):
            x_t = xch[:, c * 1024:(c + 1) * 1024]
            mu = stats[:, 0:1]
            rs_ = stats[:, 1:2]
            ln_stats(P, x_t, mu, rs_, scr, 1.0 / 1024, tag=f"{pfx}l1{ch}_{c}")
            P.v("tensor_scalar", x_t, [x_t], scalar1=mu, scalar2=rs_,
                op0=ALU.subtract, op1=ALU.mult, tag=f"{pfx}n1{ch}_{c}")
            P.v("tensor_tensor", x_t, [x_t, wb], op=ALU.mult, tag=f"{pfx}w{ch}_{c}")
            ln_stats(P, x_t, mu, rs_, scr, 1.0 / 1024, tag=f"{pfx}l2{ch}_{c}")
            P.v("tensor_scalar", x_t, [x_t], scalar1=mu, scalar2=rs_,
                op0=ALU.subtract, op1=ALU.mult, tag=f"{pfx}n2{ch}_{c}")
            P.v("tensor_tensor", x_t, [x_t, ach[:, c * 1024:(c + 1) * 1024]],
                op=ALU.add, tag=f"{pfx}ad{ch}_{c}")
        P.dma("gpsimd", dst_ap[:, ch * 4:(ch + 1) * 4, :], xch,
              out_regs=[dram_reg(out_h, ch * 512, (ch + 1) * 512)],
              in_regs=[xch], tag=f"{pfx}o{ch}")


TM_STAGE = 99

def emit_timemix(P, nc, com, xi_h, pvec_h, tm1_h, tm2_h, td1_h, td2_h,
                 wr_h, wk_h, wv_h, wg_h, wo_h, gnw_h, gnb_h, masku_h, pfx="t",
                 xo_h=None):
    CH = 256
    TBANK2 = 3
    NSC = CH // 64
    NTS = CH // 128
    wsb = {}
    for (n, h_) in (("wr", wr_h), ("wk", wk_h), ("wv", wv_h), ("wg", wg_h), ("wo", wo_h)):
        wsb[n] = TV(nc.alloc_sbuf_tensor(pfx + n, [128, 8, 1024], BF16))
        P.dma("gpsimd", wsb[n], h_.ap().rearrange("(a p) f -> p a f", p=128),
              in_regs=[dram_reg(h_)], tag=pfx + n)
    tm1 = TV(nc.alloc_sbuf_tensor(pfx + "tm1", [128, 8, 160], BF16))
    P.dma("gpsimd", tm1, tm1_h.ap().rearrange("(a p) f -> p a f", p=128),
          in_regs=[dram_reg(tm1_h)], tag=pfx + "tm1")
    tm2 = TV(nc.alloc_sbuf_tensor(pfx + "tm2", [32, 5, 1024], BF16))
    P.dma("gpsimd", tm2, tm2_h.ap(), in_regs=[dram_reg(tm2_h)], tag=pfx + "tm2")
    td1 = TV(nc.alloc_sbuf_tensor(pfx + "td1", [128, 8, 64], BF16))
    P.dma("gpsimd", td1, td1_h.ap().rearrange("(a p) f -> p a f", p=128),
          in_regs=[dram_reg(td1_h)], tag=pfx + "td1")
    td2 = TV(nc.alloc_sbuf_tensor(pfx + "td2", [64, 1024], BF16))
    P.dma("gpsimd", td2, td2_h.ap(), in_regs=[dram_reg(td2_h)], tag=pfx + "td2")
    pv = TV(nc.alloc_sbuf_tensor(pfx + "pv", [128, 80], F32))
    P.dma("sync", pv, pvec_h.ap(), in_regs=[dram_reg(pvec_h)], tag=pfx + "pv")
    gnw = TV(nc.alloc_sbuf_tensor(pfx + "gnw", [128, 1024], F32))
    gnb = TV(nc.alloc_sbuf_tensor(pfx + "gnb", [128, 1024], F32))
    P.dma("sync", gnw, gnw_h.ap().unsqueeze(0).partition_broadcast(128),
          in_regs=[dram_reg(gnw_h)], tag=pfx + "gnw")
    P.dma("sync", gnb, gnb_h.ap().unsqueeze(0).partition_broadcast(128),
          in_regs=[dram_reg(gnb_h)], tag=pfx + "gnb")
    masku = TV(nc.alloc_sbuf_tensor(pfx + "masku", [128, 512], BF16))
    P.dma("gpsimd", masku, masku_h.ap(), in_regs=[dram_reg(masku_h)], tag=pfx + "mu")
    sel01 = TV(nc.alloc_sbuf_tensor(pfx + "sel", [128, 2], BF16))
    P.v("memset", sel01, [0.0], tag=pfx + "s0", engine="gpsimd")
    P.v("memset", sel01[0:64, 0:1], [1.0], tag=pfx + "s1", engine="gpsimd")
    P.v("memset", TV(sel01.h, (64, 128), (1, 2)), [1.0], tag=pfx + "s2", engine="gpsimd")
    ext = TV(nc.alloc_sbuf_tensor(pfx + "ext", [128, 8, CH + 1], BF16))
    sxT = TV(nc.alloc_sbuf_tensor(pfx + "sxT", [128, 8, CH], BF16))
    mixf = TV(nc.alloc_sbuf_tensor(pfx + "mixf", [32, 5, CH], BF16))
    xf = [TV(nc.alloc_sbuf_tensor(pfx + f"xf{i}", [128, 8, CH], BF16)) for i in range(2)]
    t1b4 = TV(nc.alloc_sbuf_tensor(pfx + "t1b", [128, 4, CH], BF16))
    wT = TV(nc.alloc_sbuf_tensor(pfx + "wT", [128, 8, CH], F32))
    dT = TV(nc.alloc_sbuf_tensor(pfx + "dT", [128, 8, CH], F32))
    dprevd = TV(nc.alloc_sbuf_tensor(pfx + "dprev", [128, 2, CH], F32))
    zz = TV(nc.alloc_sbuf_tensor(pfx + "zz", [128, 64], F32))
    rt = TV(nc.alloc_sbuf_tensor(pfx + "rt", [128, 8, CH], BF16))
    ktl = TV(nc.alloc_sbuf_tensor(pfx + "ktl", [128, 8, CH], BF16))
    khat = TV(nc.alloc_sbuf_tensor(pfx + "khat", [128, 8, 256], BF16))
    sts = TV(nc.alloc_sbuf_tensor(pfx + "sts", [128, 4, 512], BF16))
    vtk = TV(nc.alloc_sbuf_tensor(pfx + "vtk", [128, NTS, 1024], BF16))
    gtk = TV(nc.alloc_sbuf_tensor(pfx + "gtk", [128, NTS, 1024], BF16))
    wkv = TV(nc.alloc_sbuf_tensor(pfx + "wkv", [128, NTS, 1024], F32))
    ddg = TV(nc.alloc_sbuf_tensor(pfx + "ddg", [128, NTS, 16], F32))

    state = TV(nc.alloc_sbuf_tensor(pfx + "state", [128, 8, 64], F32))
    stateb = TV(nc.alloc_sbuf_tensor(pfx + "stateb", [128, 8, 64], BF16))
    xch = TV(nc.alloc_sbuf_tensor(pfx + "xch", [128, NTS, 1024], F32))
    xlt = TV(nc.alloc_sbuf_tensor(pfx + "xlt", [128, 1024], BF16))
    t1g = TV(nc.alloc_sbuf_tensor(pfx + "t1g", [128, 512], BF16))
    hd = TV(nc.alloc_sbuf_tensor(pfx + "hd", [64, 256], BF16))
    t1fd = TV(nc.alloc_sbuf_tensor(pfx + "t1f", [128, 2, 256], F32))
    kscd = TV(nc.alloc_sbuf_tensor(pfx + "ksc", [128, 2, 128], BF16))
    dend = TV(nc.alloc_sbuf_tensor(pfx + "dend", [128, 32], F32))
    scr = TV(nc.alloc_sbuf_tensor(pfx + "scr", [128, 1024], F32))
    stats = TV(nc.alloc_sbuf_tensor(pfx + "stats", [128, 64], F32))
    gno = TV(nc.alloc_sbuf_tensor(pfx + "gno", [128, NTS, 1024], BF16))
    goT = TV(nc.alloc_sbuf_tensor(pfx + "goT", [128, 8, CH], BF16))
    ruk = goT
    pb = [ps(nc, pfx + f"pb{i}", 128, 512, F32) for i in range(8)]

    xi_ap = xi_h.ap().rearrange("(c p) d -> p c d", p=128)
    xo_h = xi_h if xo_h is None else xo_h
    xo_ap = xo_h.ap().rearrange("(c p) d -> p c d", p=128)
    maacol = {"x": 8, "w": 16, "k": 24, "v": 32, "r": 40, "g": 48}

    for b in range(B_LOC):
        P.v("memset", state, [0.0], tag=f"{pfx}b{b}st0", engine="gpsimd")
        P.v("memset", stateb, [0.0], tag=f"{pfx}b{b}st0b", engine="gpsimd")
        for ch in range(2048 // CH):
            tb = b * 2048 + ch * CH
            cb = tb // 128
            P.dma("sync", xch, xi_ap[:, cb:cb + NTS, :],
                  in_regs=[dram_reg(xi_h, tb, tb + CH)], tag=f"{pfx}b{b}c{ch}x")
            if ch == 0:
                P.v("memset", TV(ext.h, (0, 128), (0, 8 * (CH + 1))), [0.0],
                    tag=f"{pfx}b{b}c{ch}z", engine="gpsimd")
            else:
                eap = ext.h.ap()
                P.add("vector", lambda e, d=eap[:, :, 0:1], s=eap[:, :, CH:CH + 1]:
                      e.tensor_copy(out=d, in_=s),
                      reads=[ext], writes=[TV(ext.h, (0, 128), (0, 8 * (CH + 1)))],
                      tag=f"{pfx}b{b}c{ch}bd")
            for c in range(NTS):
                x_t = xch[:, c * 1024:(c + 1) * 1024]
                mu = stats[:, 0:1]; rs_ = stats[:, 1:2]
                ln_stats(P, x_t, mu, rs_, scr, 1.0 / 1024, tag=f"{pfx}b{b}c{ch}l{c}")
                P.v("tensor_scalar", xlt, [x_t], scalar1=mu, scalar2=rs_,
                    op0=ALU.subtract, op1=ALU.mult, tag=f"{pfx}b{b}c{ch}n{c}")
                for dt_ in range(8):
                    grp = dt_ // 4
                    pvw = TV(pb[7 if (2 * c + grp) % 2 == 0 else TBANK2].h,
                             (0, 128), (0, 1024), "PS", vdt=BF16)
                    transpose_128(P, com, pvw[:, (dt_ % 4) * 128:(dt_ % 4 + 1) * 128],
                                  TV(xlt.h, (0, 128), (dt_ * 128, (dt_ + 1) * 128)),
                                  tag=f"{pfx}b{b}c{ch}t{c}_{dt_}")
                    if dt_ % 4 == 3:
                        dap = ext.h.ap()[:, grp * 4:(grp + 1) * 4,
                                         1 + c * 128:1 + (c + 1) * 128]
                        P.add("vector", lambda e, dap=dap, pap=pvw[:, 0:512].ap():
                              e.tensor_copy(out=dap, in_=pap),
                              reads=[pvw], writes=[TV(ext.h, (0, 128), (0, 8 * (CH + 1)))],
                              tag=f"{pfx}b{b}c{ch}te{c}_{grp}")
            for dt_ in range(8):
                P.add("vector", lambda e,
                      d=ext.h.ap()[:, dt_, 1:CH + 1], s=pv[:, dt_:dt_ + 1].ap():
                      e.tensor_scalar(out=d, in0=d, scalar1=s, scalar2=None, op0=ALU.mult),
                      reads=[ext, pv], writes=[TV(ext.h, (0, 128), (0, 8 * (CH + 1)))],
                      tag=f"{pfx}b{b}c{ch}lw{dt_}")
            P.add("vector", lambda e, d=sxT.ap(),
                  a=ext.h.ap()[:, :, 0:CH], b2=ext.h.ap()[:, :, 1:CH + 1]:
                  e.tensor_tensor(out=d, in0=a, in1=b2, op=ALU.subtract),
                  reads=[ext], writes=[sxT], tag=f"{pfx}b{b}c{ch}sx")
            if TM_STAGE < 2:
                P.dma("gpsimd", xo_ap[:, cb:cb + NTS, :], xch,
                      out_regs=[dram_reg(xo_h, tb, tb + CH)], in_regs=[xch],
                      tag=f"{pfx}b{b}c{ch}oS1")
                continue
            xxx = xf[0]
            for dt_ in range(8):
                P.add("vector", lambda e, d=xxx.h.ap()[:, dt_, :],
                      a=sxT.h.ap()[:, dt_, :], s=pv[:, maacol["x"] + dt_:maacol["x"] + dt_ + 1].ap(),
                      x2=ext.h.ap()[:, dt_, 1:CH + 1]:
                      e.scalar_tensor_tensor(out=d, in0=a, scalar=s, in1=x2,
                                             op0=ALU.mult, op1=ALU.add),
                      reads=[sxT, pv, ext], writes=[xxx], tag=f"{pfx}b{b}c{ch}xx{dt_}")
            for f in range(5):
                pm = TV(pb[f % 2].h, (0, 32), (0, CH), "PS")
                for dt_ in range(8):
                    P.mm(pm, tm1[:, dt_ * 160 + f * 32:dt_ * 160 + (f + 1) * 32],
                         TV(xxx.h, (0, 128), (dt_ * CH, (dt_ + 1) * CH)),
                         start=(dt_ == 0), stop=(dt_ == 7), tag=f"{pfx}b{b}c{ch}m{f}")
                P.act(TV(mixf.h, (0, 32), (f * CH, (f + 1) * CH)), pm, AF.Tanh,
                      tag=f"{pfx}b{b}c{ch}mt{f}")
            if TM_STAGE < 3:
                P.dma("gpsimd", xo_ap[:, cb:cb + NTS, :], xch,
                      out_regs=[dram_reg(xo_h, tb, tb + CH)], in_regs=[xch],
                      tag=f"{pfx}b{b}c{ch}oS2")
                continue
            for fi, f in enumerate("wkvrg"):
                buf = xf[fi % 2]
                for dt_ in range(8):
                    pm = pb[4 + dt_ % 2]
                    P.mm(TV(pm.h, (0, 128), (0, CH), "PS"),
                         TV(tm2.h, (0, 32), (fi * 1024 + dt_ * 128, fi * 1024 + (dt_ + 1) * 128)),
                         TV(mixf.h, (0, 32), (fi * CH, (fi + 1) * CH)),
                         start=True, stop=True, tag=f"{pfx}b{b}c{ch}{f}m{dt_}")
                    t1s = t1b4[:, (dt_ % 4) * CH:(dt_ % 4 + 1) * CH]
                    P.add("vector", lambda e, d=t1s.ap(),
                          pap=TV(pm.h, (0, 128), (0, CH), "PS").ap(),
                          s=pv[:, maacol[f] + dt_:maacol[f] + dt_ + 1].ap(),
                          sx=sxT.h.ap()[:, dt_, :]:
                          e.scalar_tensor_tensor(out=d, in0=pap, scalar=s, in1=sx,
                                                 op0=ALU.add, op1=ALU.mult),
                          reads=[TV(pm.h, (0, 128), (0, CH), "PS"), pv, sxT],
                          writes=[t1s], tag=f"{pfx}b{b}c{ch}{f}t2_{dt_}")
                    P.add("vector", lambda e, d=buf.h.ap()[:, dt_, :], a=t1s.ap(),
                          x2=ext.h.ap()[:, dt_, 1:CH + 1]:
                          e.tensor_tensor(out=d, in0=a, in1=x2, op=ALU.add),
                          reads=[t1s, ext], writes=[buf], tag=f"{pfx}b{b}c{ch}{f}t3_{dt_}")
                if f == "w":
                    emit_decay(P, nc, com, buf, td1, td2, pv, wT, dT, pb, hd,
                               t1fd, zz, CH, f"{pfx}b{b}c{ch}")
                elif f in ("k", "r"):
                    dst = ktl if f == "k" else rt
                    for od in range(8):
                        pm = TV(pb[(2 + od) % 4].h, (0, 128), (0, CH), "PS")
                        for dt_ in range(8):
                            P.mm(pm, wsb["w" + f][:, dt_ * 1024 + od * 128:dt_ * 1024 + (od + 1) * 128],
                                 TV(buf.h, (0, 128), (dt_ * CH, (dt_ + 1) * CH)),
                                 start=(dt_ == 0), stop=(dt_ == 7), tag=f"{pfx}b{b}c{ch}{f}W{od}")
                        if f == "k":
                            P.add("scalar", lambda e, d=dst.h.ap()[:, od, :], pap=pm.ap():
                                  e.copy(out=d, in_=pap), reads=[pm], writes=[dst],
                                  tag=f"{pfx}b{b}c{ch}{f}We{od}")
                        else:
                            P.act(TV(dst.h, (0, 128), (od * CH, (od + 1) * CH)), pm,
                                  AF.Tanh, scale=0.5, tag=f"{pfx}b{b}c{ch}rWt{od}")
                            P.add("vector", lambda e, d=dst.h.ap()[:, od, :]:
                                  e.tensor_scalar(out=d, in0=d, scalar1=0.5, scalar2=0.5,
                                                  op0=ALU.mult, op1=ALU.add),
                                  reads=[dst], writes=[dst], tag=f"{pfx}b{b}c{ch}rWs{od}")
                else:
                    dst = vtk if f == "v" else gtk
                    for ts_ in range(NTS):
                        for hf in range(2):
                            pm = TV(pb[(2 + 2 * ts_ + hf) % 4].h, (0, 128), (0, 512), "PS")
                            for dt_ in range(8):
                                P.mm(pm, TV(buf.h, (0, 128),
                                            (dt_ * CH + ts_ * 128, dt_ * CH + (ts_ + 1) * 128)),
                                     wsb["w" + f][:, dt_ * 1024 + hf * 512:dt_ * 1024 + (hf + 1) * 512],
                                     start=(dt_ == 0), stop=(dt_ == 7),
                                     tag=f"{pfx}b{b}c{ch}{f}W{ts_}_{hf}")
                            dap = dst.h.ap()[:, ts_, hf * 512:(hf + 1) * 512]
                            if f == "v":
                                P.add("scalar", lambda e, d=dap, pap=pm.ap():
                                      e.copy(out=d, in_=pap), reads=[pm], writes=[dst],
                                      tag=f"{pfx}b{b}c{ch}vWe{ts_}_{hf}")
                            else:
                                P.act(t1g, pm, AF.Tanh, scale=0.5,
                                      tag=f"{pfx}b{b}c{ch}gWt{ts_}_{hf}")
                                P.add("vector", lambda e, d=t1g.ap():
                                      e.tensor_scalar(out=d, in0=d, scalar1=0.5, scalar2=0.5,
                                                      op0=ALU.mult, op1=ALU.add),
                                      reads=[t1g], writes=[t1g], tag=f"{pfx}b{b}c{ch}gWs{ts_}_{hf}")
                                P.add("vector", lambda e, d=dap, a=t1g.ap(), pap=pm.ap():
                                      e.tensor_tensor(out=d, in0=pap, in1=a, op=ALU.mult),
                                      reads=[t1g, pm], writes=[dst],
                                      tag=f"{pfx}b{b}c{ch}gWm{ts_}_{hf}")
            if TM_STAGE < 4:
                P.dma("gpsimd", xo_ap[:, cb:cb + NTS, :], xch,
                      out_regs=[dram_reg(xo_h, tb, tb + CH)], in_regs=[xch],
                      tag=f"{pfx}b{b}c{ch}oS3")
                continue
            emit_wkv_chunk(P, nc, com, rt, ktl, vtk, wT, dT, dprevd, khat, sts,
                           masku, sel01, ruk, ddg, state, stateb, wkv, pv, pb,
                           kscd, CH, NSC, NTS, f"{pfx}b{b}c{ch}", t1fw=t1fd,
                           dend=dend)
            if TM_STAGE < 5:
                P.dma("gpsimd", xo_ap[:, cb:cb + NTS, :], xch,
                      out_regs=[dram_reg(xo_h, tb, tb + CH)], in_regs=[xch],
                      tag=f"{pfx}b{b}c{ch}oS4")
                continue
            for ts_ in range(NTS):
                seg3 = wkv.h.ap()[:, ts_, :].rearrange("p (a b) -> p a b", b=64)
                mu = stats[:, 16:32]
                rs_ = stats[:, 32:48]
                mu3 = lambda t: t.ap().rearrange("p (a b) -> p a b", b=1).broadcast_to([128, 16, 64])
                P.add("vector", lambda e, d=mu.ap().rearrange("p (a b) -> p a b", b=1),
                      a=seg3: e.reduce_sum(out=d, in_=a, axis=AX.X),
                      reads=[wkv], writes=[stats], tag=f"{pfx}b{b}c{ch}gnm{ts_}")
                P.v("tensor_scalar", mu, [mu], scalar1=1.0 / 64, scalar2=None,
                    op0=ALU.mult, tag=f"{pfx}b{b}c{ch}gnm2{ts_}")
                P.add("vector", lambda e, d=scr.ap().rearrange("p (a b) -> p a b", b=64),
                      a=seg3, m=mu3(mu):
                      e.tensor_tensor(out=d, in0=a, in1=m, op=ALU.subtract),
                      reads=[wkv, stats], writes=[scr], tag=f"{pfx}b{b}c{ch}gnc{ts_}")
                P.add("vector", lambda e, d=wkv.h.ap()[:, ts_, :], a=scr.ap():
                      e.tensor_tensor(out=d, in0=a, in1=a, op=ALU.mult),
                      reads=[scr], writes=[wkv], tag=f"{pfx}b{b}c{ch}gnq{ts_}")
                P.add("vector", lambda e, d=rs_.ap().rearrange("p (a b) -> p a b", b=1),
                      a=wkv.h.ap()[:, ts_, :].rearrange("p (a b) -> p a b", b=64):
                      e.reduce_sum(out=d, in_=a, axis=AX.X),
                      reads=[wkv], writes=[stats], tag=f"{pfx}b{b}c{ch}gnv{ts_}")
                P.v("tensor_scalar", rs_, [rs_], scalar1=1.0 / 64, scalar2=None,
                    op0=ALU.mult, tag=f"{pfx}b{b}c{ch}gnv2{ts_}")
                rsqrt_eps(P, rs_, rs_, tag=f"{pfx}b{b}c{ch}gnr{ts_}")
                P.add("vector", lambda e, d=scr.ap().rearrange("p (a b) -> p a b", b=64),
                      a=scr.ap().rearrange("p (a b) -> p a b", b=64), m=mu3(rs_):
                      e.tensor_tensor(out=d, in0=a, in1=m, op=ALU.mult),
                      reads=[scr, stats], writes=[scr], tag=f"{pfx}b{b}c{ch}gns{ts_}")
                P.v("tensor_tensor", scr, [scr, gnw], op=ALU.mult, tag=f"{pfx}b{b}c{ch}gw{ts_}")
                P.v("tensor_tensor", scr, [scr, gnb], op=ALU.add, tag=f"{pfx}b{b}c{ch}gb{ts_}")
                P.add("vector", lambda e, d=gno.h.ap()[:, ts_, :], a=scr.ap(),
                      g2=gtk.h.ap()[:, ts_, :]:
                      e.tensor_tensor(out=d, in0=a, in1=g2, op=ALU.mult),
                      reads=[scr, gtk], writes=[gno], tag=f"{pfx}b{b}c{ch}gg{ts_}")
                for dt_ in range(8):
                    grp = dt_ // 4
                    pvw = TV(pb[7 if (2 * ts_ + grp) % 2 == 0 else 3].h,
                             (0, 128), (0, 1024), "PS", vdt=BF16)
                    transpose_128(P, com, pvw[:, (dt_ % 4) * 128:(dt_ % 4 + 1) * 128],
                                  TV(gno.h, (0, 128), (ts_ * 1024 + dt_ * 128,
                                                       ts_ * 1024 + (dt_ + 1) * 128)),
                                  tag=f"{pfx}b{b}c{ch}gt{ts_}_{dt_}")
                    if dt_ % 4 == 3:
                        dap = goT.h.ap()[:, grp * 4:(grp + 1) * 4, ts_ * 128:(ts_ + 1) * 128]
                        P.add("vector", lambda e, dap=dap, pap=pvw[:, 0:512].ap():
                              e.tensor_copy(out=dap, in_=pap),
                              reads=[pvw], writes=[goT], tag=f"{pfx}b{b}c{ch}gte{ts_}_{grp}")
            for ts_ in range(NTS):
                for hf in range(2):
                    pm = TV(pb[hf].h, (0, 128), (0, 512), "PS")
                    for dt_ in range(8):
                        P.mm(pm, TV(goT.h, (0, 128),
                                    (dt_ * CH + ts_ * 128, dt_ * CH + (ts_ + 1) * 128)),
                             wsb["wo"][:, dt_ * 1024 + hf * 512:dt_ * 1024 + (hf + 1) * 512],
                             start=(dt_ == 0), stop=(dt_ == 7), tag=f"{pfx}b{b}c{ch}o{ts_}_{hf}")
                    P.add("vector", lambda e, d=xch.h.ap()[:, ts_, hf * 512:(hf + 1) * 512],
                          pap=pm.ap():
                          e.tensor_tensor(out=d, in0=pap, in1=d, op=ALU.add),
                          reads=[pm, xch], writes=[xch], tag=f"{pfx}b{b}c{ch}oa{ts_}_{hf}")
            P.dma("gpsimd", xo_ap[:, cb:cb + NTS, :], xch,
                  out_regs=[dram_reg(xo_h, tb, tb + CH)], in_regs=[xch],
                  tag=f"{pfx}b{b}c{ch}o")



def emit_decay(P, nc, com, wxT, td1, td2, pv, wT, dT, pb, hd, t1f, zz, CH, tg):
    pm = TV(pb[6].h, (0, 64), (0, CH), "PS")
    for dt_ in range(8):
        P.mm(pm, td1[:, dt_ * 64:(dt_ + 1) * 64],
             TV(wxT.h, (0, 128), (dt_ * CH, (dt_ + 1) * CH)),
             start=(dt_ == 0), stop=(dt_ == 7), tag=tg + "d1")
    P.act(hd, pm, AF.Tanh, tag=tg + "d1t")
    for dt_ in range(8):
        pm2 = TV(pb[4 + dt_ % 2].h, (0, 128), (0, CH), "PS")
        P.mm(pm2, TV(td2.h, (0, 64), (dt_ * 128, (dt_ + 1) * 128)), hd,
             start=True, stop=True, tag=tg + f"d2_{dt_}")
        t1s = t1f[:, (dt_ % 2) * CH:(dt_ % 2 + 1) * CH]
        P.act(t1s, pm2, AF.Exp, bias=pv[:, 56 + dt_:57 + dt_], tag=tg + f"de1_{dt_}")
        for sc in range(CH // 64):
            P.add("vector", lambda e, d=dT.h.ap()[:, dt_, sc * 64:(sc + 1) * 64],
                  a=com.ones.ap()[:, 0:64], z=t1s.ap()[:, sc * 64:(sc + 1) * 64]:
                  e.tensor_tensor_scan(out=d, data0=a, data1=z, initial=0.0,
                                       op0=ALU.mult, op1=ALU.subtract),
                  reads=[t1s, com.ones], writes=[dT], tag=tg + f"sc{dt_}_{sc}")


WKV_STAGE = 99

def emit_wkv_chunk(P, nc, com, rt, ktl, vtk, wT, dT, dprev, khat, sts, masku,
                   sel01, ruk, ddg, state, stateb, wkv, pv, pb, ksc,
                   CH, NSC, NTS, tg, t1fw=None, dend=None):
    for dt_ in range(8):
        P.add("vector", lambda e, d=ruk.h.ap()[:, dt_, :], a=rt.h.ap()[:, dt_, :],
              b2=ktl.h.ap()[:, dt_, :]:
              e.tensor_tensor(out=d, in0=a, in1=b2, op=ALU.mult),
              reads=[rt, ktl], writes=[ruk], tag=tg + f"ru{dt_}")
        P.add("vector", lambda e, d=ruk.h.ap()[:, dt_, :],
              s=pv[:, 64 + dt_:65 + dt_].ap():
              e.tensor_scalar(out=d, in0=d, scalar1=s, scalar2=None, op0=ALU.mult),
              reads=[ruk, pv], writes=[ruk], tag=tg + f"rku{dt_}")
    pd = pb[4]
    for ts_ in range(NTS):
        for dt_ in range(8):
            P.mm(TV(pd.h, (0, 128), (ts_ * 16 + dt_ * 2, ts_ * 16 + dt_ * 2 + 2), "PS"),
                 TV(ruk.h, (0, 128), (dt_ * CH + ts_ * 128, dt_ * CH + (ts_ + 1) * 128)),
                 sel01, start=True, stop=True, tag=tg + f"sd{ts_}_{dt_}")
    P.v("tensor_copy", TV(ddg.h, (0, 128), (0, NTS * 16)),
        [TV(pd.h, (0, 128), (0, NTS * 16), "PS")], tag=tg + "sde")
    if WKV_STAGE < 2:
        P.v("memset", wkv, [0.0], tag=tg + "wz", engine="gpsimd")
        return
    for dt_ in range(8):
        dpv = dprev[:, (dt_ % 2) * CH:(dt_ % 2 + 1) * CH]
        t1v = t1fw[:, (dt_ % 2) * CH:(dt_ % 2 + 1) * CH]
        P.add("vector", lambda e, d=dpv.ap()[:, 1:CH], s=dT.h.ap()[:, dt_, 0:CH - 1]:
              e.tensor_copy(out=d, in_=s), reads=[dT], writes=[dpv], tag=tg + f"dp{dt_}")
        P.add("vector", lambda e,
              d=dpv.ap().rearrange("p (s i) -> p s i", s=NSC)[:, :, 0:1]:
              e.memset(d, 0.0), reads=[], writes=[dpv], tag=tg + f"dp1{dt_}")
        P.add("scalar", lambda e, d=t1v.ap(), a=dpv.ap():
              e.activation(out=d, in_=a, func=AF.Exp),
              reads=[dpv], writes=[t1v], tag=tg + f"dpe{dt_}")
        P.add("vector", lambda e, d=rt.h.ap()[:, dt_, :], b2=t1v.ap():
              e.tensor_tensor(out=d, in0=d, in1=b2, op=ALU.mult),
              reads=[rt, t1v], writes=[rt], tag=tg + f"rs{dt_}")
        P.add("scalar", lambda e, d=wT.h.ap()[:, dt_, :], a=dT.h.ap()[:, dt_, :]:
              e.activation(out=d, in_=a, func=AF.Exp, scale=np.float32(-1.0)),
              reads=[dT], writes=[wT], tag=tg + f"rd{dt_}")
        P.add("vector", lambda e, d=ktl.h.ap()[:, dt_, :], b2=wT.h.ap()[:, dt_, :]:
              e.tensor_tensor(out=d, in0=d, in1=b2, op=ALU.mult),
              reads=[ktl, wT], writes=[ktl], tag=tg + f"ks{dt_}")
        P.add("scalar", lambda e,
              d=dend.ap()[:, dt_ * 4:dt_ * 4 + NSC].unsqueeze(1),
              a=dT.h.ap().rearrange("p a (s i) -> p a s i", i=64)[:, dt_:dt_ + 1, :, 63:64].squeeze(3):
              e.activation(out=d, in_=a, func=AF.Exp),
              reads=[dT], writes=[dend], tag=tg + f"dd{dt_}")
    if WKV_STAGE < 3:
        P.v("memset", wkv, [0.0], tag=tg + "wz", engine="gpsimd")
        return
    def s_bank(h):
        return (2 if h % 2 else 0) + (1 if h >= 8 else 0)
    def s_slot(h):
        return (h // 2) % 4
    for h in range(16):
        bank = pb[s_bank(h)]
        for sc in range(NSC):
            P.mm(TV(bank.h, ((sc % 2) * 64, (sc % 2) * 64 + 64),
                    (s_slot(h) * 128 + (sc // 2) * 64,
                     s_slot(h) * 128 + (sc // 2) * 64 + 64), "PS"),
                 TV(ktl.h, ((h % 2) * 64, (h % 2) * 64 + 64),
                    ((h // 2) * CH + sc * 64, (h // 2) * CH + (sc + 1) * 64)),
                 TV(rt.h, ((h % 2) * 64, (h % 2) * 64 + 64),
                    ((h // 2) * CH + sc * 64, (h // 2) * CH + (sc + 1) * 64)),
                 start=True, stop=True, tag=tg + f"S{h}_{sc}")
    for g in range(4):
        P.add("vector", lambda e, d=sts.h.ap()[:, g, :], pap=pb[g].ap(), m=masku.ap():
              e.tensor_tensor(out=d, in0=pap, in1=m, op=ALU.mult),
              reads=[pb[g], masku], writes=[sts], tag=tg + f"Se{g}")
    if WKV_STAGE < 4:
        P.v("memset", wkv, [0.0], tag=tg + "wz", engine="gpsimd")
        return
    for dt_ in range(8):
        for pr in range(NTS):
            kv = ksc[:, (dt_ % 2) * 128:(dt_ % 2 + 1) * 128]
            for half in range(2):
                sc = pr * 2 + half
                P.add("vector", lambda e, d=kv.ap()[:, half * 64:(half + 1) * 64],
                      a=ktl.h.ap()[:, dt_, sc * 64:(sc + 1) * 64],
                      s=dend.ap()[:, dt_ * 4 + sc:dt_ * 4 + sc + 1]:
                      e.tensor_scalar(out=d, in0=a, scalar1=s, scalar2=None, op0=ALU.mult),
                      reads=[ktl, dend], writes=[kv], tag=tg + f"kh{dt_}_{sc}")
            pvw = TV(pb[7 if dt_ % 2 == 0 else 3].h, (0, 128), (0, 256), "PS", vdt=BF16)
            transpose_128(P, com, pvw[:, 0:128], kv, tag=tg + f"kt{dt_}_{pr}")
            P.add("vector", lambda e, d=khat.h.ap()[:, dt_, pr * 128:(pr + 1) * 128],
                  pap=pvw[:, 0:128].ap():
                  e.tensor_copy(out=d, in_=pap), reads=[pvw], writes=[khat],
                  tag=tg + f"kte{dt_}_{pr}")
    if WKV_STAGE < 5:
        P.v("memset", wkv, [0.0], tag=tg + "wz", engine="gpsimd")
        return
    for pr in range(NTS):
        for sc in (pr * 2, pr * 2 + 1):
            rb = (sc % 2) * 64
            for h in range(16):
                pstt = pb[0] if h % 2 == 0 else pb[2]
                P.mm(TV(pstt.h, (rb, rb + 64), ((h // 2) * 64, (h // 2) * 64 + 64), "PS"),
                     TV(rt.h, ((h % 2) * 64, (h % 2) * 64 + 64),
                        ((h // 2) * CH + sc * 64, (h // 2) * CH + (sc + 1) * 64)),
                     TV(stateb.h, ((h % 2) * 64, (h % 2) * 64 + 64),
                        ((h // 2) * 64, (h // 2) * 64 + 64)),
                     start=True, stop=True, tag=tg + f"o2_{h}_{sc}")
            for h in range(16):
                po = pb[5] if h < 8 else pb[6]
                P.mm(TV(po.h, (rb, rb + 64), ((h % 8) * 64, (h % 8) * 64 + 64), "PS"),
                     TV(sts.h, (rb, rb + 64),
                        (s_bank(h) * 512 + s_slot(h) * 128 + (sc // 2) * 64,
                         s_bank(h) * 512 + s_slot(h) * 128 + (sc // 2) * 64 + 64)),
                     TV(vtk.h, (rb, rb + 64), ((sc // 2) * 1024 + h * 64,
                                               (sc // 2) * 1024 + (h + 1) * 64)),
                     start=True, stop=True, tag=tg + f"o1_{h}_{sc}")
            for h in range(16):
                P.mm(TV(pd.h, ((h % 2) * 64, (h % 2) * 64 + 64),
                        ((h // 2) * 64, (h // 2) * 64 + 64), "PS"),
                     TV(khat.h, (rb, rb + 64),
                        ((h // 2) * 256 + pr * 128 + (h % 2) * 64,
                         (h // 2) * 256 + pr * 128 + (h % 2) * 64 + 64)),
                     TV(vtk.h, (rb, rb + 64), ((sc // 2) * 1024 + h * 64,
                                               (sc // 2) * 1024 + (h + 1) * 64)),
                     start=True, stop=True, tag=tg + f"su{h}_{sc}")
            for dt_ in range(8):
                P.add("vector", lambda e, d=state.h.ap()[:, dt_, :],
                      s=dend.ap()[:, dt_ * 4 + sc:dt_ * 4 + sc + 1],
                      pap=pd.ap()[:, dt_ * 64:(dt_ + 1) * 64]:
                      e.scalar_tensor_tensor(out=d, in0=d, scalar=s, in1=pap,
                                             op0=ALU.mult, op1=ALU.add),
                      reads=[state, dend, pd], writes=[state], tag=tg + f"stu{dt_}_{sc}")
            P.v("tensor_copy", stateb, [state], tag=tg + f"stb{sc}")
        stgE = TV(wT.h, (0, 128), (0, 512))
        stgO = TV(wT.h, (0, 128), (512, 1024))
        P.v("tensor_copy", stgE, [pb[0]], tag=tg + f"pse{pr}")
        P.v("tensor_copy", stgO, [pb[2]], tag=tg + f"pso{pr}")
        for h in range(16):
            po = pb[5] if h < 8 else pb[6]
            stg = stgE if h % 2 == 0 else stgO
            P.add("vector", lambda e,
                  d=wkv.h.ap()[:, pr, h * 64:(h + 1) * 64],
                  a=vtk.h.ap()[:, pr, h * 64:(h + 1) * 64],
                  s=ddg.h.ap()[:, pr, h:h + 1],
                  pap=po.ap()[:, (h % 8) * 64:(h % 8) * 64 + 64]:
                  e.scalar_tensor_tensor(out=d, in0=a, scalar=s, in1=pap,
                                         op0=ALU.mult, op1=ALU.add),
                  reads=[vtk, ddg, po], writes=[wkv], tag=tg + f"we{h}_{pr}")
            P.add("vector", lambda e,
                  d=wkv.h.ap()[:, pr, h * 64:(h + 1) * 64],
                  a=wT.h.ap().rearrange("p a c -> p (a c)")[:, (0 if h % 2 == 0 else 512) + (h // 2) * 64:
                                                            (0 if h % 2 == 0 else 512) + (h // 2) * 64 + 64]:
                  e.tensor_tensor(out=d, in0=d, in1=a, op=ALU.add),
                  reads=[wkv, TV(wT.h, (0, 128), (0, 1024))], writes=[wkv],
                  tag=tg + f"ws{h}_{pr}")


def emit_chanmix(P, nc, com, xi_h, pvec_h, cwk_h, cwv_h, cwr_h, pfx="c",
                 xo_h=None):
    CH = 256
    TBANK2 = 6
    NTS = CH // 128
    cwk = TV(nc.alloc_sbuf_tensor(pfx + "cwk", [128, 8, 3072], BF16))
    cwv = TV(nc.alloc_sbuf_tensor(pfx + "cwv", [128, 24, 1024], BF16))
    cwr = TV(nc.alloc_sbuf_tensor(pfx + "cwr", [128, 8, 1024], BF16))
    P.dma("gpsimd", cwk, cwk_h.ap().rearrange("(a p) f -> p a f", p=128),
          in_regs=[dram_reg(cwk_h)], tag=pfx + "wk")
    P.dma("gpsimd", cwv, cwv_h.ap().rearrange("(a p) f -> p a f", p=128),
          in_regs=[dram_reg(cwv_h)], tag=pfx + "wv")
    P.dma("gpsimd", cwr, cwr_h.ap().rearrange("(a p) f -> p a f", p=128),
          in_regs=[dram_reg(cwr_h)], tag=pfx + "wr")
    pv = TV(nc.alloc_sbuf_tensor(pfx + "pv", [128, 24], F32))
    P.dma("sync", pv, pvec_h.ap(), in_regs=[dram_reg(pvec_h)], tag=pfx + "pv")
    ext = TV(nc.alloc_sbuf_tensor(pfx + "ext", [128, 8, CH + 1], BF16))
    sxT = TV(nc.alloc_sbuf_tensor(pfx + "sxT", [128, 8, CH], BF16))
    kx = TV(nc.alloc_sbuf_tensor(pfx + "kx", [128, 8, CH], BF16))
    rx = TV(nc.alloc_sbuf_tensor(pfx + "rx", [128, 8, CH], BF16))
    k2 = TV(nc.alloc_sbuf_tensor(pfx + "k2", [128, 24, CH], BF16))
    rsg = TV(nc.alloc_sbuf_tensor(pfx + "rsg", [128, NTS, 1024], BF16))
    xch = TV(nc.alloc_sbuf_tensor(pfx + "xch", [128, NTS, 1024], F32))
    xlt = TV(nc.alloc_sbuf_tensor(pfx + "xlt", [128, 1024], BF16))
    scr = TV(nc.alloc_sbuf_tensor(pfx + "scr", [128, 1024], F32))
    stats = TV(nc.alloc_sbuf_tensor(pfx + "stats", [128, 8], F32))
    pb = [ps(nc, pfx + f"pb{i}", 128, 512, F32) for i in range(8)]
    xi_ap = xi_h.ap().rearrange("(c p) d -> p c d", p=128)
    xo_h = xi_h if xo_h is None else xo_h
    xo_ap = xo_h.ap().rearrange("(c p) d -> p c d", p=128)

    for b in range(B_LOC):
        for ch in range(2048 // CH):
            tb = b * 2048 + ch * CH
            cb = tb // 128
            P.dma("sync", xch, xi_ap[:, cb:cb + NTS, :],
                  in_regs=[dram_reg(xi_h, tb, tb + CH)], tag=f"{pfx}b{b}c{ch}x")
            if ch == 0:
                P.v("memset", TV(ext.h, (0, 128), (0, 8 * (CH + 1))), [0.0],
                    tag=f"{pfx}b{b}c{ch}z", engine="gpsimd")
            else:
                eap = ext.h.ap()
                P.add("vector", lambda e, d=eap[:, :, 0:1], s=eap[:, :, CH:CH + 1]:
                      e.tensor_copy(out=d, in_=s),
                      reads=[ext], writes=[TV(ext.h, (0, 128), (0, 8 * (CH + 1)))],
                      tag=f"{pfx}b{b}c{ch}bd")
            for c in range(NTS):
                x_t = xch[:, c * 1024:(c + 1) * 1024]
                mu = stats[:, 0:1]; rs_ = stats[:, 1:2]
                ln_stats(P, x_t, mu, rs_, scr, 1.0 / 1024, tag=f"{pfx}b{b}c{ch}l{c}")
                P.v("tensor_scalar", xlt, [x_t], scalar1=mu, scalar2=rs_,
                    op0=ALU.subtract, op1=ALU.mult, tag=f"{pfx}b{b}c{ch}n{c}")
                for dt_ in range(8):
                    grp = dt_ // 4
                    pvw = TV(pb[7 if (2 * c + grp) % 2 == 0 else TBANK2].h,
                             (0, 128), (0, 1024), "PS", vdt=BF16)
                    transpose_128(P, com, pvw[:, (dt_ % 4) * 128:(dt_ % 4 + 1) * 128],
                                  TV(xlt.h, (0, 128), (dt_ * 128, (dt_ + 1) * 128)),
                                  tag=f"{pfx}b{b}c{ch}t{c}_{dt_}")
                    if dt_ % 4 == 3:
                        dap = ext.h.ap()[:, grp * 4:(grp + 1) * 4,
                                         1 + c * 128:1 + (c + 1) * 128]
                        P.add("vector", lambda e, dap=dap, pap=pvw[:, 0:512].ap():
                              e.tensor_copy(out=dap, in_=pap),
                              reads=[pvw], writes=[TV(ext.h, (0, 128), (0, 8 * (CH + 1)))],
                              tag=f"{pfx}b{b}c{ch}te{c}_{grp}")
            for dt_ in range(8):
                P.add("vector", lambda e,
                      d=ext.h.ap()[:, dt_, 1:CH + 1], s=pv[:, dt_:dt_ + 1].ap():
                      e.tensor_scalar(out=d, in0=d, scalar1=s, scalar2=None, op0=ALU.mult),
                      reads=[ext, pv], writes=[TV(ext.h, (0, 128), (0, 8 * (CH + 1)))],
                      tag=f"{pfx}b{b}c{ch}lw{dt_}")
            P.add("vector", lambda e, d=sxT.ap(),
                  a=ext.h.ap()[:, :, 0:CH], b2=ext.h.ap()[:, :, 1:CH + 1]:
                  e.tensor_tensor(out=d, in0=a, in1=b2, op=ALU.subtract),
                  reads=[ext], writes=[sxT], tag=f"{pfx}b{b}c{ch}sx")
            for (buf, colb) in ((kx, 8), (rx, 16)):
                for dt_ in range(8):
                    P.add("vector", lambda e, d=buf.h.ap()[:, dt_, :],
                          a=sxT.h.ap()[:, dt_, :], s=pv[:, colb + dt_:colb + dt_ + 1].ap(),
                          x2=ext.h.ap()[:, dt_, 1:CH + 1]:
                          e.scalar_tensor_tensor(out=d, in0=a, scalar=s, in1=x2,
                                                 op0=ALU.mult, op1=ALU.add),
                          reads=[sxT, pv, ext], writes=[buf],
                          tag=f"{pfx}b{b}c{ch}mx{colb}_{dt_}")
            for od in range(24):
                pm = TV(pb[od % 2].h, (0, 128), (0, CH), "PS")
                for dt_ in range(8):
                    P.mm(pm, cwk[:, dt_ * 3072 + od * 128:dt_ * 3072 + (od + 1) * 128],
                         TV(kx.h, (0, 128), (dt_ * CH, (dt_ + 1) * CH)),
                         start=(dt_ == 0), stop=(dt_ == 7), tag=f"{pfx}b{b}c{ch}k{od}")
                P.act(TV(k2.h, (0, 128), (od * CH, (od + 1) * CH)), pm, AF.Relu,
                      tag=f"{pfx}b{b}c{ch}kr{od}")
                P.add("vector", lambda e, d=k2.h.ap()[:, od, :]:
                      e.tensor_tensor(out=d, in0=d, in1=d, op=ALU.mult),
                      reads=[k2], writes=[k2], tag=f"{pfx}b{b}c{ch}kq{od}")
            for ts_ in range(NTS):
                for hf in range(2):
                    pm = TV(pb[2 + (2 * ts_ + hf) % 2].h, (0, 128), (0, 512), "PS")
                    for dt_ in range(8):
                        P.mm(pm, TV(rx.h, (0, 128),
                                    (dt_ * CH + ts_ * 128, dt_ * CH + (ts_ + 1) * 128)),
                             cwr[:, dt_ * 1024 + hf * 512:dt_ * 1024 + (hf + 1) * 512],
                             start=(dt_ == 0), stop=(dt_ == 7), tag=f"{pfx}b{b}c{ch}r{ts_}_{hf}")
                    P.act(TV(rsg.h, (0, 128), (ts_ * 1024 + hf * 512, ts_ * 1024 + (hf + 1) * 512)),
                          pm, AF.Tanh, scale=0.5, tag=f"{pfx}b{b}c{ch}rt{ts_}_{hf}")
                    P.add("vector", lambda e,
                          d=rsg.h.ap()[:, ts_, hf * 512:(hf + 1) * 512]:
                          e.tensor_scalar(out=d, in0=d, scalar1=0.5, scalar2=0.5,
                                          op0=ALU.mult, op1=ALU.add),
                          reads=[rsg], writes=[rsg], tag=f"{pfx}b{b}c{ch}rs{ts_}_{hf}")
            for ts_ in range(NTS):
                for hf in range(2):
                    pm = TV(pb[4 + (2 * ts_ + hf) % 2].h, (0, 128), (0, 512), "PS")
                    for od in range(24):
                        P.mm(pm, TV(k2.h, (0, 128),
                                    (od * CH + ts_ * 128, od * CH + (ts_ + 1) * 128)),
                             cwv[:, od * 1024 + hf * 512:od * 1024 + (hf + 1) * 512],
                             start=(od == 0), stop=(od == 23), tag=f"{pfx}b{b}c{ch}y{ts_}_{hf}")
                    P.add("vector", lambda e,
                          d=xch.h.ap()[:, ts_, hf * 512:(hf + 1) * 512],
                          r_=rsg.h.ap()[:, ts_, hf * 512:(hf + 1) * 512], pap=pm.ap():
                          e.scalar_tensor_tensor(out=d, in0=pap, scalar=1.0, in1=d,
                                                 op0=ALU.mult, op1=ALU.add)
                          if False else
                          e.tensor_tensor(out=d, in0=pap, in1=r_, op=ALU.mult),
                          reads=[pm, rsg, xch], writes=[xch],
                          tag=f"{pfx}b{b}c{ch}ye{ts_}_{hf}") if False else None
                    P.add("vector", lambda e, d=scr.ap()[:, hf * 512:(hf + 1) * 512],
                          r_=rsg.h.ap()[:, ts_, hf * 512:(hf + 1) * 512], pap=pm.ap():
                          e.tensor_tensor(out=d, in0=pap, in1=r_, op=ALU.mult),
                          reads=[pm, rsg], writes=[scr], tag=f"{pfx}b{b}c{ch}ym{ts_}_{hf}")
                    P.add("vector", lambda e, d=xch.h.ap()[:, ts_, hf * 512:(hf + 1) * 512],
                          a=scr.ap()[:, hf * 512:(hf + 1) * 512]:
                          e.tensor_tensor(out=d, in0=d, in1=a, op=ALU.add),
                          reads=[scr, xch], writes=[xch], tag=f"{pfx}b{b}c{ch}ya{ts_}_{hf}")
            P.dma("gpsimd", xo_ap[:, cb:cb + NTS, :], xch,
                  out_regs=[dram_reg(xo_h, tb, tb + CH)], in_regs=[xch],
                  tag=f"{pfx}b{b}c{ch}o")



def _pd_layout(v):
    return np.ascontiguousarray(np.asarray(v, np.float32).reshape(8, 128).T)


def _masku_host():
    p = np.arange(128)[:, None] % 64
    c = np.arange(512)[None, :] % 64
    return (c > p).astype(np.float32)


def _build_launch(emit_fn, specs, outs):
    nc = bass.Bass("TRN2", target_bir_lowering=False)
    hs = {}
    for (n, shape) in specs:
        hs[n] = nc.declare_dram_parameter(n, list(shape), F32, isOutput=(n in outs))
    import contextlib
    ctx = contextlib.ExitStack()
    sems = make_sems(nc, ctx)
    P = Prog(nc)
    com = Common(nc, P)
    emit_fn(nc, P, com, hs)
    P.finalize(sems)
    with nc.Block() as block:
        P.emit_all(block)
    ctx.close()
    return nc


_launch_cache = {}


def _run(tag, emit_fn, specs, outs, in_maps):
    nc = _launch_cache.get(tag)
    if nc is None:
        nc = _build_launch(emit_fn, specs, outs)
        _launch_cache[tag] = nc
    res = run_bass_kernel_spmd(nc, in_maps, core_ids=list(range(N_CORES)))
    return res.results


def kernel(**inp):
    if FUSED:
        return kernel_fused(**inp)
    f32 = lambda a: np.ascontiguousarray(np.asarray(a, np.float32))
    x = f32(inp["L0_attn_x"]).reshape(B, S, D)
    shards = [np.ascontiguousarray(x[2 * c:2 * c + 2].reshape(T_LOC, D))
              for c in range(N_CORES)]
    TD = (T_LOC, D)

    aw = {n: f32(inp["L0_attn_" + n]) for n in
          ("W_dq", "W_uq", "W_dkv", "W_uk", "W_uv", "W_o")}
    specs = [("resid", TD), ("wdq", (1024, 256)), ("wuq", (256, 1024)),
             ("wdkv", (1024, 256)), ("wuk", (256, 1024)), ("wuv", (256, 1024)),
             ("wo", (1024, 1024)), ("out", TD)]
    maps = [dict(resid=shards[c], wdq=aw["W_dq"], wuq=aw["W_uq"],
                 wdkv=aw["W_dkv"], wuk=aw["W_uk"], wuv=aw["W_uv"],
                 wo=aw["W_o"]) for c in range(N_CORES)]
    r = _run("attn", lambda nc, P, com, h: emit_attn(
        P, nc, com, h["resid"], h["out"], h["wdq"], h["wuq"], h["wdkv"],
        h["wuk"], h["wuv"], h["wo"]), specs, {"out"}, maps)
    resid = [r[c]["out"] for c in range(N_CORES)]

    w1e = f32(inp["L0_ffn_w1_e"]); w3e = f32(inp["L0_ffn_w3_e"])
    w2e = f32(inp["L0_ffn_w2_e"])
    w1s = f32(inp["L0_ffn_w1_s"]); w3s = f32(inp["L0_ffn_w3_s"])
    w2s = f32(inp["L0_ffn_w2_s"])
    w1te = np.ascontiguousarray(np.concatenate(
        [w1e.transpose(0, 2, 1), w1s.T.reshape(1024, 2, 512).transpose(1, 0, 2)], 0))
    w3te = np.ascontiguousarray(np.concatenate(
        [w3e.transpose(0, 2, 1), w3s.T.reshape(1024, 2, 512).transpose(1, 0, 2)], 0))
    w2te = np.ascontiguousarray(np.concatenate(
        [w2e.transpose(0, 2, 1), w2s.T.reshape(2, 512, 1024)], 0))
    gt = np.ascontiguousarray(f32(inp["L0_ffn_w_gate"]).T)
    specs = [("resid", TD), ("gate", (1024, 16)), ("w1", (18, 1024, 512)),
             ("w3", (18, 1024, 512)), ("w2", (18, 512, 1024)), ("out", TD)]
    maps = [dict(resid=resid[c], gate=gt, w1=w1te, w3=w3te, w2=w2te)
            for c in range(N_CORES)]
    r = _run("moe", lambda nc, P, com, h: emit_moe(
        P, nc, com, h["resid"], h["out"], h["gate"], h["w1"], h["w3"], h["w2"]),
        specs, {"out"}, maps)
    resid = [r[c]["out"] for c in range(N_CORES)]

    specs = [("src", TD), ("dst", TD)]
    maps = [dict(src=resid[c]) for c in range(N_CORES)]
    r = _run("pre", lambda nc, P, com, h: emit_ln_stream(
        P, nc, com, h["src"], h["dst"], "rq"), specs, {"dst"}, maps)
    xi = [r[c]["dst"] for c in range(N_CORES)]

    for L in range(2):
        g = lambda n: f32(inp["L1_attn_" + n][L])
        pvec80 = np.concatenate(
            [_pd_layout(g("ln1_w")), _pd_layout(g("x_maa")), _pd_layout(g("w_maa")),
             _pd_layout(g("k_maa")), _pd_layout(g("v_maa")), _pd_layout(g("r_maa")),
             _pd_layout(g("g_maa")), _pd_layout(g("time_decay").reshape(-1)),
             _pd_layout(g("time_first").reshape(-1)),
             np.zeros((128, 8), np.float32)], 1)
        specs = [("xi", TD), ("pvec", (128, 80)), ("tm1", (1024, 160)),
                 ("tm2", (32, 5, 1024)), ("td1", (1024, 64)), ("td2", (64, 1024)),
                 ("wr", (1024, 1024)), ("wk", (1024, 1024)), ("wv", (1024, 1024)),
                 ("wg", (1024, 1024)), ("wo", (1024, 1024)), ("gnw", (1024,)),
                 ("gnb", (1024,)), ("mku", (128, 512)), ("xo", TD)]
        maps = [dict(xi=xi[c], pvec=pvec80,
                     tm1=g("tm_w1"),
                     tm2=np.ascontiguousarray(g("tm_w2").transpose(1, 0, 2)),
                     td1=g("td_w1"), td2=g("td_w2"), wr=g("Wr"), wk=g("Wk"),
                     wv=g("Wv"), wg=g("Wg"), wo=g("Wo"), gnw=g("gn_w"),
                     gnb=g("gn_b"), mku=_masku_host()) for c in range(N_CORES)]
        r = _run(f"tm{L}", lambda nc, P, com, h: emit_timemix(
            P, nc, com, h["xi"], h["pvec"], h["tm1"], h["tm2"], h["td1"],
            h["td2"], h["wr"], h["wk"], h["wv"], h["wg"], h["wo"], h["gnw"],
            h["gnb"], h["mku"], xo_h=h["xo"]), specs, {"xo"}, maps)
        xi = [r[c]["xo"] for c in range(N_CORES)]
        pvec24 = np.concatenate([_pd_layout(g("ln2_w")), _pd_layout(g("ck_maa")),
                                 _pd_layout(g("cr_maa"))], 1)
        specs = [("xi", TD), ("pvec", (128, 24)), ("cwk", (1024, 3072)),
                 ("cwv", (3072, 1024)), ("cwr", (1024, 1024)), ("xo", TD)]
        maps = [dict(xi=xi[c], pvec=pvec24, cwk=g("cWk"), cwv=g("cWv"),
                     cwr=g("cWr")) for c in range(N_CORES)]
        r = _run(f"cm{L}", lambda nc, P, com, h: emit_chanmix(
            P, nc, com, h["xi"], h["pvec"], h["cwk"], h["cwv"], h["cwr"],
            xo_h=h["xo"]), specs, {"xo"}, maps)
        xi = [r[c]["xo"] for c in range(N_CORES)]

    specs = [("xi", TD), ("residb", TD), ("lnout", (1024,)), ("out", TD)]
    maps = [dict(xi=xi[c], residb=resid[c], lnout=f32(inp["L1_attn_ln_out_w"]))
            for c in range(N_CORES)]
    r = _run("post", lambda nc, P, com, h: emit_rwkv_post(
        P, nc, com, h["xi"], h["residb"], h["out"], h["lnout"]),
        specs, {"out"}, maps)
    resid = [r[c]["out"] for c in range(N_CORES)]

    b1r = np.ascontiguousarray(f32(inp["L1_ffn_b1"]).reshape(32, 128).T)
    specs = [("resid", TD), ("w1t", (1024, 4096)), ("b1", (128, 32)),
             ("w2t", (4096, 1024)), ("b2", (1024,)), ("out", TD)]
    maps = [dict(resid=resid[c], w1t=np.ascontiguousarray(f32(inp["L1_ffn_W1"]).T),
                 b1=b1r, w2t=np.ascontiguousarray(f32(inp["L1_ffn_W2"]).T),
                 b2=f32(inp["L1_ffn_b2"])) for c in range(N_CORES)]
    r = _run("ffn", lambda nc, P, com, h: emit_l1ffn(
        P, nc, com, h["resid"], h["out"], h["w1t"], h["b1"], h["w2t"], h["b2"]),
        specs, {"out"}, maps)
    out = np.concatenate([np.asarray(r[c]["out"]) for c in range(N_CORES)], 0)
    return np.ascontiguousarray(out.reshape(B, S, D).astype(np.float32))



FUSED = True


def _build_fused():
    nc = bass.Bass("TRN2", target_bir_lowering=False)
    TD = [T_LOC, D]
    E = lambda n, sh: nc.declare_dram_parameter(n, list(sh), F32, isOutput=False)
    h = {}
    h["x"] = E("x", TD)
    for n, sh in (("wdq", (1024, 256)), ("wuq", (256, 1024)), ("wdkv", (1024, 256)),
                  ("wuk", (256, 1024)), ("wuv", (256, 1024)), ("wo0", (1024, 1024)),
                  ("gate", (1024, 16)), ("mw1", (18, 1024, 512)),
                  ("mw3", (18, 1024, 512)), ("mw2", (18, 512, 1024)),
                  ("lnout", (1024,)), ("fw1t", (1024, 4096)), ("fb1", (128, 32)),
                  ("fw2t", (4096, 1024)), ("fb2", (1024,)), ("mku", (128, 512))):
        h[n] = E(n, sh)
    for L in range(2):
        for n, sh in (("pvec", (128, 80)), ("tm1", (1024, 160)), ("tm2", (32, 5, 1024)),
                      ("td1", (1024, 64)), ("td2", (64, 1024)), ("wr", (1024, 1024)),
                      ("wk", (1024, 1024)), ("wv", (1024, 1024)), ("wg", (1024, 1024)),
                      ("wo", (1024, 1024)), ("gnw", (1024,)), ("gnb", (1024,)),
                      ("cpvec", (128, 24)), ("cwk", (1024, 3072)),
                      ("cwv", (3072, 1024)), ("cwr", (1024, 1024))):
            h[f"L{L}{n}"] = E(f"L{L}{n}", sh)
    out = nc.declare_dram_parameter("out", TD, F32, isOutput=True)
    resid = nc.dram_tensor("residI", TD, F32, kind="Internal")
    residb = nc.dram_tensor("residbI", TD, F32, kind="Internal")
    xi = nc.dram_tensor("xiI", TD, F32, kind="Internal")

    import contextlib
    ctx = contextlib.ExitStack()
    sems = make_sems(nc, ctx)
    P = Prog(nc)
    com = Common(nc, P)
    sb_base = nc.sbuf_base
    psb = nc.psum_base

    def phase(emit):
        nonlocal sb_base
        nc.sbuf_base = sb_base
        nc.psum_base = psb
        emit()
        P.barrier()

    phase(lambda: emit_attn(P, nc, com, h["x"], resid, h["wdq"], h["wuq"],
                            h["wdkv"], h["wuk"], h["wuv"], h["wo0"], pfx="Pa"))
    phase(lambda: emit_moe(P, nc, com, resid, residb, h["gate"], h["mw1"],
                           h["mw3"], h["mw2"], pfx="Pm"))
    phase(lambda: emit_ln_stream(P, nc, com, residb, xi, "Pq"))
    for L in range(2):
        phase(lambda L=L: emit_timemix(
            P, nc, com, xi, h[f"L{L}pvec"], h[f"L{L}tm1"], h[f"L{L}tm2"],
            h[f"L{L}td1"], h[f"L{L}td2"], h[f"L{L}wr"], h[f"L{L}wk"],
            h[f"L{L}wv"], h[f"L{L}wg"], h[f"L{L}wo"], h[f"L{L}gnw"],
            h[f"L{L}gnb"], h["mku"], pfx=f"Pt{L}"))
        phase(lambda L=L: emit_chanmix(
            P, nc, com, xi, h[f"L{L}cpvec"], h[f"L{L}cwk"], h[f"L{L}cwv"],
            h[f"L{L}cwr"], pfx=f"Pc{L}"))
    phase(lambda: emit_rwkv_post(P, nc, com, xi, residb, resid, h["lnout"], pfx="Pp"))
    phase(lambda: emit_l1ffn(P, nc, com, resid, out, h["fw1t"], h["fb1"],
                             h["fw2t"], h["fb2"], pfx="Pf"))
    P.finalize(sems)
    with nc.Block() as block:
        P.emit_all(block)
    ctx.close()
    return nc


def _fused_in_maps(inp, shards):
    f32 = lambda a: np.ascontiguousarray(np.asarray(a, np.float32))
    aw = {n: f32(inp["L0_attn_" + n]) for n in
          ("W_dq", "W_uq", "W_dkv", "W_uk", "W_uv", "W_o")}
    w1e = f32(inp["L0_ffn_w1_e"]); w3e = f32(inp["L0_ffn_w3_e"])
    w2e = f32(inp["L0_ffn_w2_e"])
    w1s = f32(inp["L0_ffn_w1_s"]); w3s = f32(inp["L0_ffn_w3_s"])
    w2s = f32(inp["L0_ffn_w2_s"])
    base = dict(
        wdq=aw["W_dq"], wuq=aw["W_uq"], wdkv=aw["W_dkv"], wuk=aw["W_uk"],
        wuv=aw["W_uv"], wo0=aw["W_o"],
        gate=np.ascontiguousarray(f32(inp["L0_ffn_w_gate"]).T),
        mw1=np.ascontiguousarray(np.concatenate(
            [w1e.transpose(0, 2, 1), w1s.T.reshape(1024, 2, 512).transpose(1, 0, 2)], 0)),
        mw3=np.ascontiguousarray(np.concatenate(
            [w3e.transpose(0, 2, 1), w3s.T.reshape(1024, 2, 512).transpose(1, 0, 2)], 0)),
        mw2=np.ascontiguousarray(np.concatenate(
            [w2e.transpose(0, 2, 1), w2s.T.reshape(2, 512, 1024)], 0)),
        lnout=f32(inp["L1_attn_ln_out_w"]),
        fw1t=np.ascontiguousarray(f32(inp["L1_ffn_W1"]).T),
        fb1=np.ascontiguousarray(f32(inp["L1_ffn_b1"]).reshape(32, 128).T),
        fw2t=np.ascontiguousarray(f32(inp["L1_ffn_W2"]).T),
        fb2=f32(inp["L1_ffn_b2"]),
        mku=_masku_host(),
    )
    for L in range(2):
        g = lambda n: f32(inp["L1_attn_" + n][L])
        base[f"L{L}pvec"] = np.concatenate(
            [_pd_layout(g("ln1_w")), _pd_layout(g("x_maa")), _pd_layout(g("w_maa")),
             _pd_layout(g("k_maa")), _pd_layout(g("v_maa")), _pd_layout(g("r_maa")),
             _pd_layout(g("g_maa")), _pd_layout(g("time_decay").reshape(-1)),
             _pd_layout(g("time_first").reshape(-1)),
             np.zeros((128, 8), np.float32)], 1)
        base[f"L{L}tm1"] = g("tm_w1")
        base[f"L{L}tm2"] = np.ascontiguousarray(g("tm_w2").transpose(1, 0, 2))
        base[f"L{L}td1"] = g("td_w1"); base[f"L{L}td2"] = g("td_w2")
        for n in ("Wr", "Wk", "Wv", "Wg", "Wo"):
            base[f"L{L}{'w' + n[1].lower()}"] = g(n)
        base[f"L{L}gnw"] = g("gn_w"); base[f"L{L}gnb"] = g("gn_b")
        base[f"L{L}cpvec"] = np.concatenate(
            [_pd_layout(g("ln2_w")), _pd_layout(g("ck_maa")), _pd_layout(g("cr_maa"))], 1)
        base[f"L{L}cwk"] = g("cWk"); base[f"L{L}cwv"] = g("cWv")
        base[f"L{L}cwr"] = g("cWr")
    return [dict(base, x=shards[c]) for c in range(N_CORES)]


def kernel_fused(**inp):
    f32 = lambda a: np.ascontiguousarray(np.asarray(a, np.float32))
    x = f32(inp["L0_attn_x"]).reshape(B, S, D)
    shards = [np.ascontiguousarray(x[2 * c:2 * c + 2].reshape(T_LOC, D))
              for c in range(N_CORES)]
    nc = _launch_cache.get("fused")
    if nc is None:
        nc = _build_fused()
        _launch_cache["fused"] = nc
    maps = _fused_in_maps(inp, shards)
    res = run_bass_kernel_spmd(nc, maps, core_ids=list(range(N_CORES)))
    out = np.concatenate([np.asarray(res.results[c]["out"]) for c in range(N_CORES)], 0)
    return np.ascontiguousarray(out.reshape(B, S, D).astype(np.float32))
